# version 10
# baseline (speedup 1.0000x reference)
import numpy as np
from contextlib import ExitStack
import concourse.bass as bass
import concourse.mybir as mybir
from concourse.bass_utils import run_bass_kernel_spmd

F32 = mybir.dt.float32
BF16 = mybir.dt.bfloat16
AF = mybir.ActivationFunctionType
ALU = mybir.AluOpType
AX = mybir.AxisListType

NCORES = 8
HD = 128
NH = 32
KVH = 8
GR = NH // KVH
CD = 512
RD = 512
NOC = 32
NEG = -60000.0
EPS = 1e-5


class Cfg:
    def __init__(self, S=8192, D=4096, F=11008, hres=36,
                 phases=("A1", "A2", "B", "C", "F1", "F2"), dbg=()):
        self.S, self.D, self.F = S, D, F
        self.KD = D // 128
        self.NTT = S // 512
        self.TOWN = S // NCORES
        self.NJ = self.TOWN // 128
        self.NF = F // 128
        self.HRES = min(hres, self.NF)
        self.NDG = (self.KD + 1) // 2
        self.phases = set(phases)
        self.dbg = set(dbg)
        self.halves = [(h0, min(self.TOWN, h0 + 512)) for h0 in range(0, self.TOWN, 512)]
        assert S % 1024 == 0 and D % 256 == 0 and F % 128 == 0
        assert self.NJ == S // 1024


PERM = np.array(list(range(0, 32)) + list(range(64, 96)) +
                list(range(32, 64)) + list(range(96, 128)))


def tile_of(c, j):
    p = c if j % 2 == 0 else 7 - c
    return 8 * j + p



ENGINES = ("sync", "gpsimd", "scalar", "vector", "tensor")
QSEM = {"sync": "s", "gpsimd": "g", "scalar": "c"}


class Stream:
    def __init__(self, P, queue, nbufs, tag):
        self.P, self.q, self.n = P, queue, nbufs
        self.sems = []
        for i in range(nbufs):
            name = f"{tag}{i}"
            assert name not in P.cnt
            P.cnt[name] = 0
            P.dma_sems.add(name)
            self.sems.append(name)
        self.idx = 0

    def load(self, transfers, waits=(), idx=None):
        P = self.P
        if idx is None:
            idx = self.idx
        self.idx = idx + 1
        sem = self.sems[idx % self.n]
        P.waits(self.q, waits)
        for (out_ap, in_ap, kw) in transfers:
            P.cnt[sem] += 16
            def f(e, SS, out_ap=out_ap, in_ap=in_ap, sem=sem, kw=kw):
                e.dma_start(out=out_ap, in_=in_ap, **kw).then_inc(SS[sem], 16)
            P.raw(self.q, f)
        return (sem, P.cnt[sem])


class Prog:
    def __init__(self):
        self.ops = {e: [] for e in ENGINES}
        self.cnt = {k: 0 for k in ("v", "a", "p")}
        self.dma_sems = set()

    def stream(self, queue, nbufs, tag):
        return Stream(self, queue, nbufs, tag)

    def raw(self, eng, fn):
        self.ops[eng].append(fn)

    def waits(self, eng, toks):
        best = {}
        for t in toks:
            if t is None:
                continue
            s, v = t
            assert v <= self.cnt[s], (s, v, self.cnt[s])
            best[s] = max(best.get(s, 0), v)
        for s, v in sorted(best.items()):
            if v <= 0:
                continue
            def f(e, SS, s=s, v=v):
                e.wait_ge(SS[s], v)
            self.raw(eng, f)

    def fence(self):
        return [(s, c) for s, c in self.cnt.items() if c > 0]

    def _chain(self, eng, sem, fn, waits):
        self.waits(eng, waits)
        prev = self.cnt[sem]
        self.cnt[sem] += 1
        val = self.cnt[sem]
        def f(e, SS, fn=fn, sem=sem, prev=prev):
            if prev:
                e.wait_ge(SS[sem], prev)
            fn(e).then_inc(SS[sem], 1)
        self.raw(eng, f)
        return (sem, val)

    def ve(self, fn, waits=()):
        return self._chain("vector", "v", fn, waits)

    def sc(self, fn, waits=()):
        return self._chain("scalar", "a", fn, waits)

    def pe(self, fn, waits=(), inc=False):
        self.waits("tensor", waits)
        if inc:
            self.cnt["p"] += 1
            val = self.cnt["p"]
            def f(e, SS, fn=fn):
                fn(e).then_inc(SS["p"], 1)
            self.raw("tensor", f)
            return ("p", val)
        def f(e, SS, fn=fn):
            fn(e)
        self.raw("tensor", f)
        return None


class Guard:
    def __init__(self):
        self.toks = []

    def use(self, tok):
        if tok is not None:
            self.toks.append(tok)

    def take(self):
        t, self.toks = self.toks, []
        return t



def build(cfg: Cfg):
    nc = bass.Bass("TRN2", target_bir_lowering=False, debug=False)
    S, D, F, KD, NF = cfg.S, cfg.D, cfg.F, cfg.KD, cfg.NF
    NTT, NJ, TOWN = cfg.NTT, cfg.NJ, cfg.TOWN
    HALVES = cfg.halves
    P = Prog()

    dt = nc.dram_tensor
    xT = dt("xT", [KD, 128, S], F32, kind="ExternalInput").ap()
    x_ownT = dt("x_ownT", [KD, 128, TOWN], F32, kind="ExternalInput").ap()
    wkvT = dt("wkvT", [KD, 128, CD + RD], F32, kind="ExternalInput").ap()
    upkvT = dt("upkvT", [4, 128, 1536], F32, kind="ExternalInput").ap()
    wqT = dt("wqT", [NH, KD, 128, 128], F32, kind="ExternalInput").ap()
    woT = dt("woT", [KD, NOC, 128, 128], F32, kind="ExternalInput").ap()
    wgT = dt("wgT", [NF, KD, 128, 128], F32, kind="ExternalInput").ap()
    wuT = dt("wuT", [NF, KD, 128, 128], F32, kind="ExternalInput").ap()
    wdT = dt("wdT", [cfg.NDG, NF, 128, 256], F32, kind="ExternalInput").ap()
    consts = dt("consts", [128, 417], F32, kind="ExternalInput").ap()
    cosk = dt("cosk", [128, S], F32, kind="ExternalInput").ap()
    sinmk = dt("sinmk", [128, S], F32, kind="ExternalInput").ap()
    cosq = dt("cosq", [128, TOWN], F32, kind="ExternalInput").ap()
    sinmq = dt("sinmq", [128, TOWN], F32, kind="ExternalInput").ap()
    maskin = dt("maskin", [128, 2048], F32, kind="ExternalInput").ap()

    y_own = dt("y_own", [KD, 128, TOWN], F32, kind="ExternalOutput").ap()

    K_scr = dt("K_scr", [KVH, 128, S], BF16, kind="Internal").ap()
    V_scr = dt("V_scr", [S // 128, 128, KVH, 128], BF16, kind="Internal").ap()
    O_scr = dt("O_scr", [NOC, NJ, 128, 128], BF16, kind="Internal").ap()
    NSPILL = NF - cfg.HRES
    h_scr = dt("h_scr", [max(NSPILL, 1), 128, TOWN], BF16, kind="Internal").ap()

    def dbg(name, shape, dtype=F32):
        if name in cfg.dbg:
            return dt("dbg_" + name, shape, dtype, kind="ExternalOutput").ap()
        return None

    dbg_q = dbg("q", [NH, 128, TOWN])
    dbg_k = dbg("k", [KVH, 128, S])
    dbg_v = dbg("v", [S // 128, 128, KVH, 128])
    dbg_o = dbg("o", [NOC, NJ, 128, 128])
    dbg_r1 = dbg("r1", [KD, 128, TOWN])
    dbg_ln1 = dbg("ln1", [1, 4 * TOWN])
    dbg_ps = dbg("ps", [1, 2 * TOWN])
    dbg_bc = dbg("bc", [2, 128, TOWN])
    dbg_x1 = dbg("x1", [KD, 128, TOWN])
    dbg_h = dbg("h", [NF, 128, TOWN])
    dbg_hend = dbg("hend", [NF, 128, TOWN])

    def SB(stk, name, shape, dtype):
        return stk.enter_context(nc.sbuf_tensor(name, shape, dtype))

    def PS(stk, name, shape, dtype=F32):
        esz = 4 if dtype == F32 else 2
        per = int(np.prod(shape[1:])) * esz
        pad = (-per) % 2048
        shape = list(shape)
        shape[-1] = shape[-1] + pad // (esz * int(np.prod(shape[1:-1])))
        return stk.enter_context(nc.psum_tensor(name, shape, dtype))

    st_top = ExitStack()

    const_sb = SB(st_top, "const_sb", [128, 417], F32)
    ident_bf = SB(st_top, "ident_bf", [128, 128], BF16)
    r32_bf = SB(st_top, "r32_bf", [128, 128], BF16)
    ones_bf = SB(st_top, "ones_bf", [128, 32], BF16)
    onesr = SB(st_top, "onesr", [32, 128], F32)
    mask_sb = SB(st_top, "mask_sb", [128, 2048], BF16)
    cosq_sb = SB(st_top, "cosq_sb", [128, TOWN], F32)
    sinmq_sb = SB(st_top, "sinmq_sb", [128, TOWN], F32)

    stCS = P.stream("sync", 1, "dCS")
    stCG = P.stream("gpsimd", 1, "dCG")
    tok_cs = stCS.load([(const_sb[:], consts, {}), (onesr[:], consts[0:32, 256:384], {}),
                        (cosq_sb[:], cosq, {}), (sinmq_sb[:], sinmq, {})])
    tok_cg = stCG.load([(ident_bf[:], consts[:, 0:128], {}),
                        (r32_bf[:], consts[:, 128:256], {}),
                        (ones_bf[:], consts[:, 384:416], {}),
                        (mask_sb[:], maskin, {})])
    tok_const = tok_onesr = tok_cq = tok_sq = tok_cs
    tok_ident = tok_r32 = tok_ones = tok_mask = tok_cg

    def load_slab(stream, idx, dst, src3d, kd, width, extra_waits=()):
        ngrp = max(1, (kd * 128) // 1024)
        ngrp = min(ngrp, kd)
        per = (kd + ngrp - 1) // ngrp
        tr = []
        for gi in range(ngrp):
            k0, k1 = gi * per, min(kd, (gi + 1) * per)
            if k0 >= k1:
                break
            tr.append((dst.ap().rearrange("p (k m) -> p k m", k=kd)[:, k0:k1, :],
                       src3d[k0:k1].rearrange("k p m -> p k m"), {}))
        return stream.load(tr, waits=extra_waits, idx=idx)

    tok_q_head = [None] * NH
    tok_A1_stores = []
    fence = []

    if "A1" in cfg.phases:
        stA = ExitStack()
        CW = CD + RD
        wkv_sb = SB(stA, "wkv_sb", [128, KD * CW], BF16)
        upkv_sb = SB(stA, "upkv_sb", [128, 4 * 1536], BF16)
        x_bf = [SB(stA, f"x_bf{b}", [128, KD * 512], BF16) for b in range(2)]
        ckv_bf = [SB(stA, f"ckv_bf{b}", [128, 4 * 512], BF16) for b in range(2)]
        ckt = [SB(stA, f"ckt{b}", [128, 1024], F32) for b in range(2)]
        kr_bf = [SB(stA, f"kr_bf{b}", [128, 512], BF16) for b in range(2)]
        kt1 = SB(stA, "kt1", [128, 512], F32)
        kt2 = SB(stA, "kt2", [128, 512], F32)
        kA = SB(stA, "kA", [128, 512], BF16)
        kc_bf = SB(stA, "kc_bf", [128, 512], BF16)
        vt_bf = SB(stA, "vt_bf", [128, 512], BF16)

        ps_ckv = [PS(stA, f"ps_ckv{b}", [128, 512]) for b in range(2)]
        ps_kr = [PS(stA, f"ps_kr{b}", [128, 512]) for b in range(2)]
        ps_rot = PS(stA, "ps_rot", [128, 512])
        ps_kc = PS(stA, "ps_kc", [128, 512])
        ps_v = [PS(stA, f"ps_v{b}", [128, 512]) for b in range(2)]

        g_xbf = [Guard(), Guard()]
        g_ckt = [Guard(), Guard()]
        g_psckv = [Guard(), Guard()]
        g_pskr = [Guard(), Guard()]
        g_psrot = Guard()
        g_pskc = Guard()
        g_psv = [Guard(), Guard()]
        g_ckvbf = [Guard(), Guard()]
        g_krbf = [Guard(), Guard()]
        g_kA = Guard()
        g_kcbf = Guard()
        g_vtbf = Guard()

        stWV = P.stream("gpsimd", 1, "dWV")
        stX = P.stream("gpsimd", 2, "dX")
        stCK = P.stream("sync", 2, "dCK")
        stKA = P.stream("sync", 1, "sKA")
        stKC = P.stream("sync", 1, "sKC")
        stVst = P.stream("sync", 1, "sVs")

        load_slab(stWV, 0, wkv_sb, wkvT, KD, CW)
        tok_wkv = tok_upkv = load_slab(stWV, 0, upkv_sb, upkvT, 4, 1536)

        tok_x = [None] * NTT
        tok_ck = [None] * NTT
        tok_ckv_done = [None] * NTT

        def load_tile(tt):
            b = tt % 2
            ngrp = max(1, KD // 8)
            per = (KD + ngrp - 1) // ngrp
            tr = []
            for gi in range(ngrp):
                k0, k1 = gi * per, min(KD, (gi + 1) * per)
                tr.append((x_bf[b].ap().rearrange("p (k t) -> p k t", k=KD)[:, k0:k1, :],
                           xT[k0:k1, :, tt * 512:(tt + 1) * 512].rearrange("k p t -> p k t"),
                           {}))
            tok_x[tt] = stX.load(tr, waits=g_xbf[b].take(), idx=tt)
            tok_ck[tt] = stCK.load(
                [(ckt[b][:, 0:512], cosk[:, tt * 512:(tt + 1) * 512], {}),
                 (ckt[b][:, 512:1024], sinmk[:, tt * 512:(tt + 1) * 512], {})],
                waits=g_ckt[b].take(), idx=tt)

        load_tile(0)
        if NTT > 1:
            load_tile(1)

        rope_state = {}

        def rope_tail(tt, ri):
            b = tt % 2
            bk = ri % 2
            tok_mm = rope_state.pop((tt, ri))
            t_copy = P.ve(lambda e, o=kr_bf[bk][:], i2=ps_kr[bk][:]: e.tensor_copy(o, i2),
                          waits=[tok_mm] + g_krbf[bk].take())
            g_pskr[bk].use(t_copy)
            t_rot = P.pe(lambda e, r=kr_bf[bk][:]: e.matmul(ps_rot[:], r32_bf[:], r,
                                                            start=True, stop=True),
                         waits=[t_copy, tok_r32] + g_psrot.take(), inc=True)
            t_1 = P.ve(lambda e, c1=ckt[b][:, 0:512], r=kr_bf[bk][:]: e.tensor_tensor(
                           out=kt1[:], in0=r, in1=c1, op=ALU.mult),
                       waits=[tok_ck[tt]])
            t_2 = P.ve(lambda e, c2=ckt[b][:, 512:1024]: e.tensor_tensor(
                           out=kt2[:], in0=ps_rot[:], in1=c2, op=ALU.mult),
                       waits=[t_rot])
            g_psrot.use(t_2)
            g_krbf[bk].use(t_2)
            t_A = P.ve(lambda e: e.tensor_tensor(out=kA[:], in0=kt1[:], in1=kt2[:],
                                                 op=ALU.add),
                       waits=g_kA.take())
            g_ckt[b].use(t_1)
            g_ckt[b].use(t_2)
            g_kA.use(stKA.load(
                [(K_scr[2 * ri + hh, 0:64, slice(tt * 512, (tt + 1) * 512)],
                  kA[hh * 64:(hh + 1) * 64, :], {}) for hh in range(2)], waits=[t_A]))

        def emit_proj(tt):
            b = tt % 2
            xa = x_bf[b].ap().rearrange("p (k t) -> p k t", k=KD)
            t_ev = None
            for mi in range(4):
                pp = ps_ckv[mi % 2]
                tok_mm = None
                for ki in range(KD):
                    w = ([tok_x[tt], tok_wkv] + g_psckv[mi % 2].take()) if ki == 0 else []
                    tok_mm = P.pe(
                        lambda e, o=pp[:], l=wkv_sb[:, ki * CW + mi * 128: ki * CW + mi * 128 + 128],
                               r=xa[:, ki, :], s=(ki == 0), t_=(ki == KD - 1):
                            e.matmul(o, l, r, start=s, stop=t_),
                        waits=w, inc=(ki == KD - 1))
                t_ev = P.ve(lambda e, o=ckv_bf[b][:, mi * 512:(mi + 1) * 512], i2=pp[:]:
                               e.tensor_copy(o, i2),
                            waits=[tok_mm] + (g_ckvbf[b].take() if mi == 0 else []))
                g_psckv[mi % 2].use(t_ev)
            tok_ckv_done[tt] = t_ev
            for ri in range(RD // 128):
                bk = ri % 2
                tok_mm = None
                for ki in range(KD):
                    w = ([tok_x[tt]] + g_pskr[bk].take()) if ki == 0 else []
                    tok_mm = P.pe(
                        lambda e, o=ps_kr[bk][:],
                               l=wkv_sb[:, ki * CW + CD + ri * 128: ki * CW + CD + ri * 128 + 128],
                               r=xa[:, ki, :], s=(ki == 0), t_=(ki == KD - 1):
                            e.matmul(o, l, r, start=s, stop=t_),
                        waits=w, inc=(ki == KD - 1))
                rope_state[(tt, ri)] = tok_mm
                if ri > 0:
                    rope_tail(tt, ri - 1)
            g_xbf[b].use(("p", P.cnt["p"]))
            rope_tail(tt, RD // 128 - 1)

        def emit_upkv(tt):
            b = tt % 2
            cb = ckv_bf[b]
            tcol = slice(tt * 512, (tt + 1) * 512)

            def kc_group(ri):
                tok_mm = None
                for ki in range(4):
                    w = ([tok_ckv_done[tt], tok_upkv] + g_pskc.take()) if ki == 0 else []
                    tok_mm = P.pe(
                        lambda e, o=ps_kc[:],
                               l=upkv_sb[:, ki * 1536 + ri * 128: ki * 1536 + ri * 128 + 128],
                               r=cb[:, ki * 512:(ki + 1) * 512], s=(ki == 0), t_=(ki == 3):
                            e.matmul(o, l, r, start=s, stop=t_),
                        waits=w, inc=(ki == 3))
                t_kc = P.ve(lambda e: e.tensor_copy(kc_bf[:], ps_kc[:]),
                            waits=[tok_mm] + g_kcbf.take())
                g_pskc.use(t_kc)
                g_kcbf.use(stKC.load(
                    [(K_scr[2 * ri + hh, 64:128, tcol], kc_bf[hh * 64:(hh + 1) * 64, :], {})
                     for hh in range(2)], waits=[t_kc]))

            def v_group(tc, vh):
                gl = tt * 4 + tc
                pv = ps_v[vh]
                tok_mm = None
                for ki in range(4):
                    w = ([tok_ckv_done[tt]] + g_psv[vh].take()) if ki == 0 else []
                    tok_mm = P.pe(
                        lambda e, o=pv[:],
                               l=cb[:, ki * 512 + tc * 128: ki * 512 + tc * 128 + 128],
                               r=upkv_sb[:, ki * 1536 + 512 + vh * 512: ki * 1536 + 512 + (vh + 1) * 512],
                               s=(ki == 0), t_=(ki == 3):
                            e.matmul(o, l, r, start=s, stop=t_),
                        waits=w, inc=(ki == 3))
                t_v = P.ve(lambda e, i2=pv[:]: e.tensor_copy(vt_bf[:], i2),
                           waits=[tok_mm] + g_vtbf.take())
                g_psv[vh].use(t_v)
                g_vtbf.use(stVst.load(
                    [(V_scr[gl, :, 4 * vh:4 * vh + 4, :],
                      vt_bf.ap().rearrange("p (h d) -> p h d", h=4), {})],
                    waits=[t_v]))

            for tc in range(4):
                kc_group(tc)
                v_group(tc, 0)
                v_group(tc, 1)
            g_ckvbf[b].use(("p", P.cnt["p"]))

        emit_proj(0)
        for tt in range(NTT):
            if tt + 1 < NTT:
                emit_proj(tt + 1)
            if tt + 2 < NTT:
                load_tile(tt + 2)
            emit_upkv(tt)

        tok_A1_stores = [(nm, P.cnt[nm]) for nm in ("sKA0", "sKC0", "sVs0")]
        if dbg_k is not None:
            P.stream("gpsimd", 1, "dbgK").load([(dbg_k, K_scr, {})], waits=tok_A1_stores)
        if dbg_v is not None:
            P.stream("gpsimd", 1, "dbgV").load([(dbg_v, V_scr, {})], waits=tok_A1_stores)
        fence = P.fence()
        stA.close()

    st_q = ExitStack()
    q_all = SB(st_q, "q_all", [128, NH * TOWN], BF16) \
        if ("A2" in cfg.phases or "B" in cfg.phases) else None

    if "A2" in cfg.phases:
        stA2 = ExitStack()
        x_own_bf = SB(stA2, "x_own_bf", [128, KD * TOWN], BF16)
        wq_bf = [SB(stA2, f"wq_bf{b}", [128, KD * 128], BF16) for b in range(2)]
        qraw = [SB(stA2, f"qraw{b}", [128, TOWN], BF16) for b in range(2)]
        qt1 = SB(stA2, "qt1", [128, TOWN], F32)
        qt2 = SB(stA2, "qt2", [128, TOWN], F32)
        ps_q = [PS(stA2, f"ps_q{b}", [128, TOWN]) for b in range(2)]
        ps_qrot = PS(stA2, "ps_qrot", [128, TOWN])

        g_wq = [Guard(), Guard()]
        g_psq = [Guard(), Guard()]
        g_psqrot = Guard()
        g_qraw = [Guard(), Guard()]

        stXO = P.stream("gpsimd", 1, "dXO")
        stWQ = P.stream("gpsimd", 2, "dWQ")
        tok_xo = stXO.load([(x_own_bf[:, ki * TOWN:(ki + 1) * TOWN], x_ownT[ki], {})
                            for ki in range(KD)], waits=fence)

        tok_wq = [None] * NH

        def load_wq(hi):
            tok_wq[hi] = load_slab(stWQ, hi, wq_bf[hi % 2], wqT[hi], KD, 128,
                                   extra_waits=g_wq[hi % 2].take())

        load_wq(0)
        if NH > 1:
            load_wq(1)

        q_state = {}

        def q_mms(hi):
            bw = hi % 2
            pq = ps_q[hi % 2]
            tok_mm = None
            for ki in range(KD):
                for (h0, h1) in HALVES:
                    w = ([tok_xo, tok_wq[hi]] + g_psq[hi % 2].take()) \
                        if (ki == 0 and h0 == 0) else []
                    tok_mm = P.pe(
                        lambda e, o=pq[:, h0:h1],
                               l=wq_bf[bw][:, ki * 128:(ki + 1) * 128],
                               r=x_own_bf[:, ki * TOWN + h0: ki * TOWN + h1],
                               s=(ki == 0), t_=(ki == KD - 1):
                            e.matmul(o, l, r, start=s, stop=t_),
                        waits=w, inc=(ki == KD - 1 and h1 == TOWN))
            q_state[hi] = tok_mm
            g_wq[hi % 2].use(("p", P.cnt["p"]))
            if hi + 2 < NH:
                load_wq(hi + 2)

        def q_rope(hi):
            bq = hi % 2
            pq = ps_q[bq]
            tok_mm = q_state.pop(hi)
            t_copy = P.ve(lambda e, o=qraw[bq][:], i2=pq[:, 0:TOWN]: e.tensor_copy(o, i2),
                          waits=[tok_mm] + g_qraw[bq].take())
            g_psq[bq].use(t_copy)
            t_rot = None
            for (h0, h1) in HALVES:
                t_rot = P.pe(lambda e, o=ps_qrot[:, h0:h1], r=qraw[bq][:, h0:h1]:
                                e.matmul(o, r32_bf[:], r, start=True, stop=True),
                             waits=([t_copy, tok_r32] + g_psqrot.take()) if h0 == 0 else (),
                             inc=(h1 == TOWN))
            t_1 = P.ve(lambda e, r=qraw[bq][:]: e.tensor_tensor(out=qt1[:], in0=r,
                                                                in1=cosq_sb[:], op=ALU.mult),
                       waits=[tok_cq])
            t_2 = P.ve(lambda e: e.tensor_tensor(out=qt2[:], in0=ps_qrot[:, 0:TOWN],
                                                 in1=sinmq_sb[:], op=ALU.mult),
                       waits=[t_rot, tok_sq])
            g_psqrot.use(t_2)
            g_qraw[bq].use(t_2)
            t_q = P.ve(lambda e, o=q_all[:, hi * TOWN:(hi + 1) * TOWN]:
                          e.tensor_tensor(out=o, in0=qt1[:], in1=qt2[:], op=ALU.add))
            tok_q_head[hi] = t_q

        q_mms(0)
        for hi in range(NH):
            if hi + 1 < NH:
                q_mms(hi + 1)
            q_rope(hi)

        if dbg_q is not None:
            P.stream("gpsimd", 1, "dbgQ").load(
                [(dbg_q[hi], q_all[:, hi * TOWN:(hi + 1) * TOWN], {}) for hi in range(NH)],
                waits=[tok_q_head[NH - 1]])
        fence = P.fence()
        stA2.close()

    tok_O_stores = None
    if "B" in cfg.phases:
        stB = ExitStack()
        k_sb = [SB(stB, f"k_sb{b}", [128, S], BF16) for b in range(2)]
        v_sb = [SB(stB, f"v_sb{b}", [128, S], BF16) for b in range(2)]
        pr_bf = [SB(stB, f"pr_bf{b}", [128, S], BF16) for b in range(2)]
        ksq = SB(stB, "ksq", [128, S], BF16)
        pt_sb = [SB(stB, f"pt_sb{b}", [128, 1024], BF16) for b in range(2)]
        kslot = SB(stB, "kslot", [1, 16], F32)
        kn1 = SB(stB, "kn1", [1, 1], F32)
        kn_bc = [SB(stB, f"kn_bc{b}", [128, 1], F32) for b in range(2)]
        qsq = SB(stB, "qsq", [128, 128], BF16)
        qnrow = SB(stB, "qnrow", [1, 128], F32)
        negb = [SB(stB, f"negb{b}", [128, 1], F32) for b in range(2)]
        qn_s = SB(stB, "qn_s", [128, 1], F32)
        rsum = SB(stB, "rsum", [128, 1], F32)
        rs_all = [SB(stB, f"rs_all{b}", [128, 8], F32) for b in range(2)]
        rinv = SB(stB, "rinv", [128, 1], F32)
        onb = SB(stB, "onb", [128, 128], BF16)
        otr = SB(stB, "otr", [128, 128], BF16)

        ps_s = [PS(stB, f"ps_s{b}", [128, 1024]) for b in range(2)]
        ps_tr = [PS(stB, f"ps_tr{b}", [128, 1024], BF16) for b in range(2)]
        ps_o = [PS(stB, f"ps_o{b}", [128, 512]) for b in range(2)]

        g_kv = [Guard(), Guard()]
        g_pss = [Guard(), Guard()]
        g_pstr = [Guard(), Guard()]
        g_ptsb = [Guard(), Guard()]
        g_pso = [Guard(), Guard()]
        g_prbf = [Guard(), Guard()]
        g_negb = [Guard(), Guard()]
        g_rs = [Guard(), Guard()]
        g_rinv = Guard()
        g_otr = Guard()
        g_onb = Guard()
        g_ksq = Guard()
        g_knbc = [Guard(), Guard()]
        g_qsq = Guard()
        g_qn = Guard()

        stKV = P.stream("sync", 2, "dKV")
        stOst = P.stream("sync", 1, "sO")
        tok_kld = [None] * KVH
        tok_kn = [None] * KVH

        def load_kv(g, first=False):
            w = g_kv[g % 2].take() + list(tok_A1_stores)
            if first:
                w = w + fence
            tok_kld[g] = stKV.load(
                [(k_sb[g % 2][:], K_scr[g], {}),
                 (v_sb[g % 2].ap().rearrange("p (tc d) -> p tc d", d=128),
                  V_scr[:, :, g, :].rearrange("tc p d -> p tc d"), {})],
                waits=w, idx=g)

        def emit_knorm(g):
            bg = g % 2
            t_sq = P.sc(lambda e, i2=k_sb[bg][:]: e.activation(out=ksq[:], in_=i2,
                                                               func=AF.Square),
                        waits=[tok_kld[g]] + g_ksq.take())
            nblk = S // 512
            t_mx = None
            for b in range(nblk):
                w = [t_sq] + ([t_mx] if t_mx else [])
                if b == 0:
                    w += g_pss[0].take()
                t_mm = P.pe(lambda e, o=ps_s[0][0:32, 0:512],
                                   r=ksq[:, b * 512:(b + 1) * 512]:
                               e.matmul(o, ones_bf[:, 0:32], r, start=True, stop=True),
                            waits=w, inc=True)
                t_mx = P.ve(lambda e, a=kslot[0:1, b:b + 1], i2=ps_s[0][0:1, 0:512]:
                               e.tensor_scalar(out=ksq[0:1, 0:512], in0=i2, scalar1=1.0,
                                               scalar2=None, op0=ALU.mult, op1=ALU.max,
                                               accum_out=a),
                            waits=[t_mm])
                g_pss[0].use(t_mx)
            t_red = P.ve(lambda e, n=nblk: e.reduce_max(kn1[:], kslot[0:1, 0:n], axis=AX.X),
                         waits=[t_mx])
            t_srt = P.sc(lambda e: e.activation(out=kn1[:], in_=kn1[:], func=AF.Sqrt,
                                                scale=1.0),
                         waits=[t_red])
            t_rf = P.ve(lambda e: e.tensor_scalar(out=qnrow[:], in0=onesr[0:1, :],
                                                  scalar1=kn1[0:1, 0:1], scalar2=None,
                                                  op0=ALU.mult),
                        waits=[t_srt, tok_onesr] + g_qn.take())
            t_bc = P.pe(lambda e: e.transpose(ps_s[0][0:128, 0:1], qnrow[:],
                                              const_sb[0:1, 384:385]),
                        waits=[t_rf] + g_pss[0].take(), inc=True)
            g_qn.use(t_bc)
            t_cp = P.ve(lambda e, o=kn_bc[bg][:]: e.tensor_scalar(
                            out=o, in0=ps_s[0][:, 0:1], scalar1=-1.02, scalar2=None,
                            op0=ALU.mult),
                        waits=[t_bc] + g_knbc[bg].take())
            g_pss[0].use(t_cp)
            g_ksq.use(t_bc)
            tok_kn[g] = t_cp

        load_kv(0, first=True)
        if KVH > 1:
            load_kv(1)
        emit_knorm(0)

        iters = [(g, g * GR + hh, j) for g in range(KVH) for hh in range(GR)
                 for j in range(NJ)]
        state = {}

        def emit_bias(i):
            g, h, j = iters[i]
            bi = i % 2
            qtile = q_all[:, h * TOWN + j * 128: h * TOWN + j * 128 + 128]
            t_sq = P.sc(lambda e, i2=qtile: e.activation(out=qsq[:], in_=i2, func=AF.Square),
                        waits=([tok_q_head[h]] if tok_q_head[h] else []) + g_qsq.take())
            po = ps_o[i % 2]
            t_mm = P.pe(lambda e: e.matmul(po[0:32, 0:128], ones_bf[:, 0:32], qsq[:],
                                           start=True, stop=True),
                        waits=[t_sq, tok_ones] + g_pso[i % 2].take(), inc=True)
            t_cr = P.ve(lambda e: e.tensor_copy(qnrow[:], po[0:1, 0:128]),
                        waits=[t_mm] + g_qn.take())
            g_qsq.use(t_mm)
            t_tr = P.pe(lambda e: e.transpose(po[0:128, 128:129], qnrow[:],
                                              const_sb[0:1, 384:385]),
                        waits=[t_cr], inc=True)
            g_qn.use(t_tr)
            t_st = P.sc(lambda e: e.activation(out=qn_s[:], in_=po[:, 128:129],
                                               func=AF.Sqrt),
                        waits=[t_tr])
            t_nb = P.ve(lambda e, o=negb[bi][:], kn=kn_bc[(g % 2)][:]:
                           e.tensor_tensor(out=o, in0=qn_s[:], in1=kn, op=ALU.mult),
                        waits=[t_st, tok_kn[g]] + g_negb[bi].take())
            g_pso[i % 2].use(t_nb)
            g_knbc[g % 2].use(t_nb)
            state[("b", i)] = t_nb

        def emit_iter(i):
            g, h, j = iters[i]
            bg = g % 2
            bi = i % 2
            t_nb = state.pop(("b", i))
            qtile = q_all[:, h * TOWN + j * 128: h * TOWN + j * 128 + 128]
            nkb = 2 * (j + 1)
            npr = nkb // 2
            nkc = 4 * nkb
            po = ps_o[i % 2]
            pr = pr_bf[bi]
            tok_exp = [None] * npr
            tok_pev = [None] * npr
            tok_tr = [None] * npr
            avtok = [None]

            def mm_pair(pb):
                bank = pb % 2
                masked = (pb == npr - 1)
                tok_mm = None
                for half in range(2):
                    kb = 2 * pb + half
                    w = g_pss[bank].take() if half == 0 else []
                    if pb == 0 and half == 0:
                        w += [tok_kld[g]]
                        if tok_q_head[h]:
                            w.append(tok_q_head[h])
                    tok_mm = P.pe(
                        lambda e, o=ps_s[bank][:, half * 512:(half + 1) * 512], l=qtile,
                               r=k_sb[bg][:, kb * 512:(kb + 1) * 512],
                               t_=(not masked):
                            e.matmul(o, l, r, start=True, stop=t_),
                        waits=w, inc=(half == 1 and not masked))
                if masked:
                    moff = (j % 2) * 1024
                    for half in range(2):
                        tok_mm = P.pe(
                            lambda e, o=ps_s[bank][:, half * 512:(half + 1) * 512],
                                   r=mask_sb[:, moff + half * 512: moff + (half + 1) * 512]:
                                e.matmul(o, ident_bf[:], r, start=False, stop=True),
                            waits=[tok_mask] if half == 0 else (), inc=(half == 1))
                w = [tok_mm, t_nb]
                if pb == 0:
                    w += g_prbf[bi].take() + g_rs[bi].take()
                tok_exp[pb] = P.sc(
                    lambda e, o=pr[:, pb * 1024:(pb + 1) * 1024],
                           i2=ps_s[pb % 2][:], a=rs_all[bi][:, pb:pb + 1],
                           bb=negb[bi].ap():
                        e.activation(out=o, in_=i2, func=AF.Exp, bias=bb, scale=1.0,
                                     accum_out=a),
                    waits=w)
                g_pss[bank].use(tok_exp[pb])

            def do_tr(pk):
                btr = pk % 2
                t_tr = None
                for i8 in range(8):
                    kc = pk * 8 + i8
                    w = []
                    if i8 == 0:
                        w = [tok_exp[pk], tok_ident] + g_pstr[btr].take()
                    t_tr = P.pe(
                        lambda e, o=ps_tr[btr][:, i8 * 128:(i8 + 1) * 128],
                               i2=pr[:, kc * 128:(kc + 1) * 128]:
                            e.transpose(o, i2, ident_bf[:]),
                        waits=w, inc=(i8 == 7))
                tok_tr[pk] = t_tr
                t_pev = P.ve(lambda e, o=pt_sb[btr][:], i2=ps_tr[btr][:, 0:1024]:
                                e.tensor_copy(o, i2),
                             waits=[t_tr] + g_ptsb[btr].take())
                g_pstr[btr].use(t_pev)
                tok_pev[pk] = t_pev

            def do_av(pk):
                btr = pk % 2
                for i8 in range(8):
                    kc = pk * 8 + i8
                    w = [tok_pev[pk]] if i8 == 0 else []
                    if kc == 0:
                        w = list(w) + g_pso[i % 2].take()
                    avtok[0] = P.pe(
                        lambda e, o=po[:, 0:128],
                               l=pt_sb[btr][:, i8 * 128:(i8 + 1) * 128],
                               r=v_sb[bg][:, kc * 128:(kc + 1) * 128],
                               s=(kc == 0), t_=(kc == nkc - 1):
                            e.matmul(o, l, r, start=s, stop=t_),
                        waits=w, inc=(i8 == 7))
                g_ptsb[btr].use(avtok[0])

            for pb in range(npr + 2):
                if pb < npr:
                    mm_pair(pb)
                if pb >= 1 and pb - 1 < npr:
                    do_tr(pb - 1)
                if pb >= 2 and pb - 2 < npr:
                    do_av(pb - 2)
            g_prbf[bi].use(tok_tr[npr - 1])

            t_rsu = P.ve(lambda e, n=npr: e.reduce_sum(rsum[:], rs_all[bi][:, 0:n],
                                                       axis=AX.X),
                         waits=[tok_exp[-1]])
            g_rs[bi].use(t_rsu)
            t_ri = P.ve(lambda e: e.reciprocal(rinv[:], rsum[:]),
                        waits=g_rinv.take())
            t_on = P.ve(lambda e, i2=po[:, 0:128]: e.tensor_scalar(
                            out=onb[:], in0=i2, scalar1=rinv.ap(), scalar2=None,
                            op0=ALU.mult),
                        waits=[avtok[0], t_ri] + g_onb.take())
            g_rinv.use(t_on)
            g_pso[i % 2].use(t_on)
            botr = npr % 2
            t_otr = P.pe(lambda e, o=ps_tr[botr][:, 0:128]:
                            e.transpose(o, onb[:], ident_bf[:]),
                         waits=[t_on] + g_pstr[botr].take(), inc=True)
            g_onb.use(t_otr)
            t_oc = P.ve(lambda e, i2=ps_tr[botr][:, 0:128]: e.tensor_copy(otr[:], i2),
                        waits=[t_otr] + g_otr.take())
            g_pstr[botr].use(t_oc)
            g_otr.use(stOst.load([(O_scr[h, j], otr[:], {})], waits=[t_oc]))

        emit_bias(0)
        for i in range(len(iters)):
            if i + 1 < len(iters):
                g2 = iters[i + 1][0]
                if g2 != iters[i][0] and tok_kn[g2] is None:
                    emit_knorm(g2)
                emit_bias(i + 1)
            emit_iter(i)
            g, h, j = iters[i]
            if h % GR == GR - 1 and j == NJ - 1:
                g_kv[g % 2].use(("p", P.cnt["p"]))
                if g + 2 < KVH:
                    load_kv(g + 2)

        tok_O_stores = ("sO0", P.cnt["sO0"])
        if dbg_o is not None:
            P.stream("gpsimd", 1, "dbgO").load([(dbg_o, O_scr, {})], waits=[tok_O_stores])
        fence = P.fence()
        stB.close()
    st_q.close()

    st_rx = ExitStack()
    need_rx = ("C" in cfg.phases) or ("F1" in cfg.phases) or ("F2" in cfg.phases)
    rx = SB(st_rx, "rx", [128, KD * TOWN], BF16) if need_rx else None

    def mk_rvec(rstat):
        def rvec(i, h0=0, h1=None):
            h1 = TOWN if h1 is None else h1
            return rstat[0:1, i * TOWN + h0: i * TOWN + h1]
        return rvec

    tok_x1 = [None] * KD

    if "C" in cfg.phases:
        stC = ExitStack()
        o_sb = SB(stC, "o_sb", [128, NOC * TOWN], BF16)
        wo_bf = [SB(stC, f"wo_bf{b}", [128, NOC * 128], BF16) for b in range(2)]
        xf32 = [SB(stC, f"xf32{b}", [128, TOWN], F32) for b in range(2)]
        r1sq = [SB(stC, f"r1sq{b}", [128, TOWN], BF16) for b in range(2)]
        bc_rstd = SB(stC, "bc_rstd", [128, TOWN], F32)
        bc_mr = SB(stC, "bc_mr", [128, TOWN], F32)
        tmpb = SB(stC, "tmpb", [128, TOWN], F32) if TOWN <= 512 else xf32[0]

        rstat1 = SB(stC, "rstat1", [32, 4 * TOWN], F32)
        rvec = mk_rvec(rstat1)
        P.ve(lambda e: e.memset(rstat1[:], 0.0), waits=fence)
        ps_a = [PS(stC, f"ps_a{b}", [128, TOWN]) for b in range(2)]
        ps_sum = PS(stC, "ps_sum", [32, max(TOWN, 512)])
        ps_sq = PS(stC, "ps_sq", [32, max(TOWN, 512)])

        g_wo = [Guard(), Guard()]
        g_psa = [Guard(), Guard()]
        g_xf = [Guard(), Guard()]
        g_sq = [Guard(), Guard()]

        stOL = P.stream("sync", 1, "dOL")
        stWO = P.stream("gpsimd", 2, "dWO")
        stXF = P.stream("sync", 2, "dXF")
        w0 = list(fence)
        if tok_O_stores:
            w0.append(tok_O_stores)
        tok_oall = stOL.load(
            [(o_sb.ap().rearrange("p (c j q) -> p c j q", c=NOC, j=NJ)[:, hi],
              O_scr[hi].rearrange("j d q -> d j q"), {}) for hi in range(NOC)],
            waits=w0)

        tok_wo = [None] * KD

        def load_wo(oi, first=False):
            tok_wo[oi] = load_slab(stWO, oi, wo_bf[oi % 2], woT[oi], NOC, 128,
                                   extra_waits=g_wo[oi % 2].take() + (list(fence) if first else []))

        load_wo(0, first=True)
        if KD > 1:
            load_wo(1)

        tok_r1 = [None] * KD
        tok_stats_done = None
        for oi in range(KD):
            bw = oi % 2
            pa = ps_a[oi % 2]
            t_xf = stXF.load([(xf32[bw][:], x_ownT[oi], {})], waits=g_xf[bw].take(), idx=oi)
            tok_mm = None
            for ki in range(NOC):
                for (h0, h1) in HALVES:
                    w = ([tok_oall, tok_wo[oi]] + g_psa[oi % 2].take()) \
                        if (ki == 0 and h0 == 0) else []
                    tok_mm = P.pe(
                        lambda e, o=pa[:, h0:h1],
                               l=wo_bf[bw][:, ki * 128:(ki + 1) * 128],
                               r=o_sb[:, ki * TOWN + h0: ki * TOWN + h1],
                               s=(ki == 0), t_=(ki == NOC - 1):
                            e.matmul(o, l, r, start=s, stop=t_),
                        waits=w, inc=(ki == NOC - 1 and h1 == TOWN))
            t_r1 = P.ve(lambda e, o=rx[:, oi * TOWN:(oi + 1) * TOWN], i=pa[:, 0:TOWN],
                               x=xf32[bw][:]:
                           e.scalar_tensor_tensor(out=o, in0=i, scalar=1.0, in1=x,
                                                  op0=ALU.mult, op1=ALU.add),
                        waits=[tok_mm, t_xf])
            tok_r1[oi] = t_r1
            g_psa[oi % 2].use(t_r1)
            g_xf[bw].use(t_r1)
            t_sqa = P.sc(lambda e, o=r1sq[oi % 2][:], i=rx[:, oi * TOWN:(oi + 1) * TOWN]:
                            e.activation(out=o, in_=i, func=AF.Square),
                         waits=[t_r1] + g_sq[oi % 2].take())
            for (h0, h1) in HALVES:
                P.pe(lambda e, o=ps_sum[0:1, h0:h1],
                            r=rx[:, oi * TOWN + h0: oi * TOWN + h1],
                            s=(oi == 0), t_=(oi == KD - 1):
                        e.matmul(o, ones_bf[:, 0:1], r, start=s, stop=t_),
                     waits=[t_r1, tok_ones] if (oi == 0 and h0 == 0) else [t_r1])
                tok_stats_done = P.pe(
                    lambda e, o=ps_sq[0:1, h0:h1], r=r1sq[oi % 2][:, h0:h1],
                           s=(oi == 0), t_=(oi == KD - 1):
                        e.matmul(o, ones_bf[:, 0:1], r, start=s, stop=t_),
                    waits=[t_sqa], inc=(h1 == TOWN))
            g_sq[oi % 2].use(tok_stats_done)
            g_wo[oi % 2].use(("p", P.cnt["p"]))
            if oi + 2 < KD:
                load_wo(oi + 2)

        tok_dbgr1 = None
        if dbg_r1 is not None:
            tok_dbgr1 = P.stream("gpsimd", 1, "dbgR1").load(
                [(dbg_r1[oi], rx[:, oi * TOWN:(oi + 1) * TOWN], {}) for oi in range(KD)],
                waits=[tok_r1[KD - 1]])

        if dbg_ln1 is not None:
            pssum_dump = SB(stC, "pssum_dump", [1, 2 * TOWN], F32)
            t_pd = P.ve(lambda e: e.tensor_copy(pssum_dump[0:1, 0:TOWN], ps_sum[0:1, 0:TOWN]),
                        waits=[tok_stats_done])
            t_pd2 = P.ve(lambda e: e.tensor_copy(pssum_dump[0:1, TOWN:2 * TOWN], ps_sq[0:1, 0:TOWN]))
            P.stream("sync", 1, "dbgPS").load([(dbg_ps, pssum_dump[:], {})], waits=[t_pd2])
        t_mean = P.ve(lambda e: e.tensor_scalar(out=rvec(0), in0=ps_sum[0:1, 0:TOWN],
                                                scalar1=1.0 / D, scalar2=None, op0=ALU.mult),
                      waits=[tok_stats_done])
        t_msq = P.ve(lambda e: e.tensor_scalar(out=rvec(1), in0=ps_sq[0:1, 0:TOWN],
                                               scalar1=1.0 / D, scalar2=None, op0=ALU.mult))
        t_m2 = P.ve(lambda e: e.tensor_tensor(out=rvec(2), in0=rvec(0), in1=rvec(0), op=ALU.mult))
        t_var = P.ve(lambda e: e.tensor_tensor(out=rvec(3), in0=rvec(1), in1=rvec(2),
                                               op=ALU.subtract))
        t_ln = P.sc(lambda e: e.activation(out=rvec(1), in_=rvec(3), func=AF.Ln, bias=const_sb[0:1, 416:417]),
                    waits=[t_var])
        t_rstd = P.sc(lambda e: e.activation(out=rvec(2), in_=rvec(1), func=AF.Exp, scale=-0.5))
        t_mr = P.ve(lambda e: e.tensor_tensor(out=rvec(1), in0=rvec(0), in1=rvec(2), op=ALU.mult),
                    waits=[t_rstd])
        t_bc = None
        for (h0, h1) in HALVES:
            P.pe(lambda e, o=ps_a[0][:, h0:h1], r=rvec(2, h0, h1):
                    e.matmul(o, onesr[0:1, :], r, start=True, stop=True),
                 waits=[t_mr, tok_onesr] if h0 == 0 else ())
            t_bc = P.pe(lambda e, o=ps_a[1][:, h0:h1], r=rvec(1, h0, h1):
                           e.matmul(o, onesr[0:1, :], r, start=True, stop=True),
                        waits=(), inc=(h1 == TOWN))
        t_bc1 = P.ve(lambda e: e.tensor_copy(bc_rstd[:], ps_a[0][:, 0:TOWN]), waits=[t_bc])
        t_bc2 = P.ve(lambda e: e.tensor_copy(bc_mr[:], ps_a[1][:, 0:TOWN]))
        if dbg_ln1 is not None:
            P.stream("sync", 1, "dbgL").load(
                [(dbg_ln1, rstat1[0:1, :], {}), (dbg_bc[0], bc_rstd[:], {}),
                 (dbg_bc[1], bc_mr[:], {})], waits=[t_bc2])
        for oi in range(KD):
            t_t = P.ve(lambda e, i=rx[:, oi * TOWN:(oi + 1) * TOWN]:
                          e.tensor_tensor(out=tmpb[:], in0=i, in1=bc_rstd[:], op=ALU.mult),
                       waits=[tok_dbgr1] if (oi == 0 and tok_dbgr1) else ())
            tok_x1[oi] = P.ve(lambda e, o=rx[:, oi * TOWN:(oi + 1) * TOWN]:
                                 e.tensor_tensor(out=o, in0=tmpb[:], in1=bc_mr[:],
                                                 op=ALU.subtract))
        if dbg_x1 is not None:
            tok_dbgx1 = P.stream("gpsimd", 1, "dbgX1").load(
                [(dbg_x1[oi], rx[:, oi * TOWN:(oi + 1) * TOWN], {}) for oi in range(KD)],
                waits=[tok_x1[KD - 1]])
        fence = P.fence()
        stC.close()

    st_h = ExitStack()
    HRES = cfg.HRES
    h_sb = SB(st_h, "h_sb", [128, max(HRES, 1) * TOWN], BF16) \
        if ("F1" in cfg.phases or "F2" in cfg.phases) else None
    tok_h = [None] * NF

    if "F1" in cfg.phases:
        stF1 = ExitStack()
        wg_bf = [SB(stF1, f"wg_bf{b}", [128, KD * 128], BF16) for b in range(2)]
        wu_bf = [SB(stF1, f"wu_bf{b}", [128, KD * 128], BF16) for b in range(2)]
        gsil = SB(stF1, "gsil", [128, TOWN], BF16)
        hsp = [SB(stF1, f"hsp{b}", [128, TOWN], BF16) for b in range(2)]
        ps_g = [PS(stF1, f"ps_g{b}", [128, TOWN]) for b in range(2)]
        ps_u = [PS(stF1, f"ps_u{b}", [128, TOWN]) for b in range(2)]

        g_wg = [Guard(), Guard()]
        g_wu = [Guard(), Guard()]
        g_psg = [Guard(), Guard()]
        g_psu = [Guard(), Guard()]
        g_hsp = [Guard(), Guard()]
        g_gsil = Guard()

        stWGU = P.stream("gpsimd", 2, "dWGU")
        stHsp = P.stream("sync", 2, "sH")
        tok_wgu = [None] * NF

        def load_wgu(f, first=False):
            ext = g_wg[f % 2].take() + g_wu[f % 2].take() + (list(fence) if first else [])
            load_slab(stWGU, f, wg_bf[f % 2], wgT[f], KD, 128, extra_waits=ext)
            tok_wgu[f] = load_slab(stWGU, f, wu_bf[f % 2], wuT[f], KD, 128)

        load_wgu(0, first=True)
        if NF > 1:
            load_wgu(1)

        for f in range(NF):
            bw = f % 2
            pg, pu = ps_g[f % 2], ps_u[f % 2]
            tok_gmm = None
            tok_umm = None
            for ki in range(KD):
                for (h0, h1) in HALVES:
                    w = []
                    if ki == 0 and h0 == 0:
                        w = [tok_wgu[f]] + g_psg[f % 2].take()
                        if tok_x1[KD - 1]:
                            w.append(tok_x1[KD - 1])
                    tok_gmm = P.pe(
                        lambda e, o=pg[:, h0:h1],
                               l=wg_bf[bw][:, ki * 128:(ki + 1) * 128],
                               r=rx[:, ki * TOWN + h0: ki * TOWN + h1],
                               s=(ki == 0), t_=(ki == KD - 1):
                            e.matmul(o, l, r, start=s, stop=t_),
                        waits=w, inc=(ki == KD - 1 and h1 == TOWN))
            for ki in range(KD):
                for (h0, h1) in HALVES:
                    w = g_psu[f % 2].take() if (ki == 0 and h0 == 0) else []
                    tok_umm = P.pe(
                        lambda e, o=pu[:, h0:h1],
                               l=wu_bf[bw][:, ki * 128:(ki + 1) * 128],
                               r=rx[:, ki * TOWN + h0: ki * TOWN + h1],
                               s=(ki == 0), t_=(ki == KD - 1):
                            e.matmul(o, l, r, start=s, stop=t_),
                        waits=w, inc=(ki == KD - 1 and h1 == TOWN))
            t_sil = P.sc(lambda e, i=pg[:, 0:TOWN]: e.activation(out=gsil[:], in_=i, func=AF.Silu),
                         waits=[tok_gmm] + g_gsil.take())
            g_psg[f % 2].use(t_sil)
            if f < HRES:
                t_h = P.ve(lambda e, o=h_sb[:, f * TOWN:(f + 1) * TOWN], i=pu[:, 0:TOWN]:
                              e.tensor_tensor(out=o, in0=gsil[:], in1=i, op=ALU.mult),
                           waits=[tok_umm, t_sil])
                tok_h[f] = t_h
            else:
                bs = f % 2
                t_h = P.ve(lambda e, o=hsp[bs][:], i=pu[:, 0:TOWN]:
                              e.tensor_tensor(out=o, in0=gsil[:], in1=i, op=ALU.mult),
                           waits=[tok_umm, t_sil] + g_hsp[bs].take())
                tok_h[f] = stHsp.load([(h_scr[f - HRES], hsp[bs][:], {})],
                                      waits=[t_h], idx=f)
                g_hsp[bs].use(tok_h[f])
            g_gsil.use(t_h)
            g_psu[f % 2].use(t_h)
            g_wg[f % 2].use(("p", P.cnt["p"]))
            g_wu[f % 2].use(("p", P.cnt["p"]))
            if f + 2 < NF:
                load_wgu(f + 2)

        if dbg_h is not None:
            stDH = P.stream("gpsimd", 1, "dbgH")
            stDH.load([(dbg_h[f], h_sb[:, f * TOWN:(f + 1) * TOWN], {})
                       for f in range(HRES)] +
                      [(dbg_h[f], h_scr[f - HRES], {}) for f in range(HRES, NF)],
                      waits=[t for t in tok_h if t is not None])
        fence = P.fence()
        stF1.close()

    if "F2" in cfg.phases:
        stF2 = ExitStack()
        wd_bf = [SB(stF2, f"wd_bf{b}", [128, 4 * 256], BF16) for b in range(2)]
        hst = [SB(stF2, f"hst{b}", [128, TOWN], BF16) for b in range(4)]
        r2sq = [SB(stF2, f"r2sq{b}", [128, TOWN], BF16) for b in range(2)]
        bc2_rstd = SB(stF2, "bc2_rstd", [128, TOWN], F32)
        bc2_mr = SB(stF2, "bc2_mr", [128, TOWN], F32)
        tmp2 = SB(stF2, "tmp2", [128, TOWN], F32)
        outf = [SB(stF2, f"outf{b}", [128, TOWN], F32) for b in range(2)]

        rstat2 = SB(stF2, "rstat2", [32, 4 * TOWN], F32)
        rvec2 = mk_rvec(rstat2)
        P.ve(lambda e: e.memset(rstat2[:], 0.0), waits=fence)
        ps_y = [PS(stF2, f"ps_y{b}", [128, TOWN]) for b in range(2)]
        ps2_sum = PS(stF2, "ps2_sum", [32, max(TOWN, 512)])
        ps2_sq = PS(stF2, "ps2_sq", [32, max(TOWN, 512)])

        stY = P.stream("sync", 2, "sY")
        g_wd = [Guard(), Guard()]
        g_hst = [Guard() for _ in range(4)]
        g_psy = [Guard(), Guard()]
        g_sq2 = [Guard(), Guard()]
        g_outf = [Guard(), Guard()]

        stWD = P.stream("gpsimd", 2, "dWD")
        stHL = P.stream("sync", 4, "dHL")
        tok_wd = {}
        tok_hload = {}

        def load_wd(dg, fb, first=False):
            f0, f1 = 4 * fb, min(NF, 4 * fb + 4)
            tok_wd[(dg, fb)] = stWD.load(
                [(wd_bf[fb % 2].ap().rearrange("p (f m) -> p f m", f=4)[:, 0:f1 - f0, :],
                  wdT[dg, f0:f1].rearrange("f p m -> p f m"), {})],
                waits=g_wd[fb % 2].take() + (list(fence) if first else []), idx=fb)

        hfirst = [True] * 4
        def load_h(dg, f):
            if f < HRES:
                return
            bs = f % 4
            w = g_hst[bs].take() + [tok_h[f]]
            if hfirst[bs]:
                hfirst[bs] = False
                w = w + list(fence)
            tok_hload[(dg, f)] = stHL.load([(hst[bs][:], h_scr[f - HRES], {})],
                                           waits=w, idx=f)

        PRE = 4
        NFB = (NF + 3) // 4
        tok_r2 = [None] * KD
        tok_stats2 = None
        for dg in range(cfg.NDG):
            d0 = 2 * dg
            nd = min(2, KD - d0)
            load_wd(dg, 0, first=(dg == 0))
            if NFB > 1:
                load_wd(dg, 1)
            for f in range(min(PRE, NF)):
                load_h(dg, f)
            tok_dmm = None
            for f in range(NF):
                bw = (f // 4) % 2
                if f < HRES:
                    hsrc = h_sb.ap()[:, f * TOWN:(f + 1) * TOWN]
                    tok_hs = tok_h[f]
                else:
                    hsrc = hst[f % 4][:]
                    tok_hs = tok_hload[(dg, f)]
                for d in range(nd):
                    for (h0, h1) in HALVES:
                        w = []
                        if d == 0 and h0 == 0:
                            w = [tok_wd[(dg, f // 4)], tok_hs]
                            if f == 0:
                                for dd in range(nd):
                                    w += g_psy[dd].take()
                        tok_dmm = P.pe(
                            lambda e, o=ps_y[d][:, h0:h1],
                                   l=wd_bf[bw][:, (f % 4) * 256 + d * 128:
                                               (f % 4) * 256 + (d + 1) * 128],
                                   r=hsrc[:, h0:h1],
                                   s=(f == 0), t_=(f == NF - 1):
                                e.matmul(o, l, r, start=s, stop=t_),
                            waits=w, inc=(d == nd - 1 and h1 == TOWN))
                if f >= HRES:
                    g_hst[f % 4].use(tok_dmm)
                if f % 4 == 3 or f == NF - 1:
                    g_wd[(f // 4) % 2].use(tok_dmm)
                    if f // 4 + 2 < NFB:
                        load_wd(dg, f // 4 + 2)
                if f + PRE < NF:
                    load_h(dg, f + PRE)
            for d in range(nd):
                oi = d0 + d
                t_r2 = P.ve(lambda e, o=rx[:, oi * TOWN:(oi + 1) * TOWN], i=ps_y[d][:, 0:TOWN]:
                               e.scalar_tensor_tensor(out=o, in0=i, scalar=1.0, in1=o,
                                                      op0=ALU.mult, op1=ALU.add),
                            waits=[tok_dmm, tok_x1[oi] if tok_x1[oi] else None])
                tok_r2[oi] = t_r2
                g_psy[d].use(t_r2)
                t_sq = P.sc(lambda e, o=r2sq[d][:], i=rx[:, oi * TOWN:(oi + 1) * TOWN]:
                               e.activation(out=o, in_=i, func=AF.Square),
                            waits=[t_r2] + g_sq2[d].take())
                for (h0, h1) in HALVES:
                    P.pe(lambda e, o=ps2_sum[0:1, h0:h1],
                                r=rx[:, oi * TOWN + h0: oi * TOWN + h1],
                                s=(oi == 0), t_=(oi == KD - 1):
                            e.matmul(o, ones_bf[:, 0:1], r, start=s, stop=t_),
                         waits=[t_r2, tok_ones] if (oi == 0 and h0 == 0) else [t_r2])
                    tok_stats2 = P.pe(
                        lambda e, o=ps2_sq[0:1, h0:h1], r=r2sq[d][:, h0:h1],
                               s=(oi == 0), t_=(oi == KD - 1):
                            e.matmul(o, ones_bf[:, 0:1], r, start=s, stop=t_),
                        waits=[t_sq], inc=(h1 == TOWN))
                g_sq2[d].use(tok_stats2)

        t_mean = P.ve(lambda e: e.tensor_scalar(out=rvec2(0), in0=ps2_sum[0:1, 0:TOWN],
                                                scalar1=1.0 / D, scalar2=None, op0=ALU.mult),
                      waits=[tok_stats2])
        t_msq = P.ve(lambda e: e.tensor_scalar(out=rvec2(1), in0=ps2_sq[0:1, 0:TOWN],
                                               scalar1=1.0 / D, scalar2=None, op0=ALU.mult))
        t_m2 = P.ve(lambda e: e.tensor_tensor(out=rvec2(2), in0=rvec2(0), in1=rvec2(0), op=ALU.mult))
        t_var = P.ve(lambda e: e.tensor_tensor(out=rvec2(3), in0=rvec2(1), in1=rvec2(2),
                                               op=ALU.subtract))
        t_ln = P.sc(lambda e: e.activation(out=rvec2(1), in_=rvec2(3), func=AF.Ln, bias=const_sb[0:1, 416:417]),
                    waits=[t_var])
        t_rstd = P.sc(lambda e: e.activation(out=rvec2(2), in_=rvec2(1), func=AF.Exp, scale=-0.5))
        t_mr = P.ve(lambda e: e.tensor_tensor(out=rvec2(1), in0=rvec2(0), in1=rvec2(2), op=ALU.mult),
                    waits=[t_rstd])
        t_bc = None
        for (h0, h1) in HALVES:
            P.pe(lambda e, o=ps_y[0][:, h0:h1], r=rvec2(2, h0, h1):
                    e.matmul(o, onesr[0:1, :], r, start=True, stop=True),
                 waits=([t_mr, tok_onesr] + g_psy[0].take() + g_psy[1].take()) if h0 == 0 else ())
            t_bc = P.pe(lambda e, o=ps_y[1][:, h0:h1], r=rvec2(1, h0, h1):
                           e.matmul(o, onesr[0:1, :], r, start=True, stop=True),
                        waits=(), inc=(h1 == TOWN))
        t_bc1 = P.ve(lambda e: e.tensor_copy(bc2_rstd[:], ps_y[0][:, 0:TOWN]), waits=[t_bc])
        t_bc2 = P.ve(lambda e: e.tensor_copy(bc2_mr[:], ps_y[1][:, 0:TOWN]))
        for oi in range(KD):
            bo = oi % 2
            t_t = P.ve(lambda e, i=rx[:, oi * TOWN:(oi + 1) * TOWN]:
                          e.tensor_tensor(out=tmp2[:], in0=i, in1=bc2_rstd[:], op=ALU.mult),
                       waits=[tok_r2[oi]] if tok_r2[oi] else ())
            t_o = P.ve(lambda e, o=outf[bo][:]:
                          e.tensor_tensor(out=o, in0=tmp2[:], in1=bc2_mr[:], op=ALU.subtract),
                       waits=g_outf[bo].take())
            g_outf[bo].use(stY.load([(y_own[oi], outf[bo][:], {})], waits=[t_o], idx=oi))
        stF2.close()

    st_h.close()
    st_rx.close()

    if dbg_hend is not None and "F1" in cfg.phases:
        P.stream("gpsimd", 1, "dbgHE").load(
            [(dbg_hend[f], h_sb[:, f * TOWN:(f + 1) * TOWN], {}) for f in range(cfg.HRES)] +
            [(dbg_hend[f], h_scr[f - cfg.HRES], {}) for f in range(cfg.HRES, NF)],
            waits=P.fence())

    P.waits("sync", [(k, P.cnt[k]) for k in P.dma_sems if P.cnt[k]])
    P.waits("gpsimd", [(k, P.cnt[k]) for k in P.dma_sems if P.cnt[k]])

    with ExitStack() as es:
        sems = {}
        for s in list(P.cnt):
            if P.cnt[s]:
                sems[s] = es.enter_context(nc.semaphore("sem_" + s))
        blk = es.enter_context(nc.Block())
        for eng_name in ENGINES:
            opl = P.ops[eng_name]
            if not opl:
                continue
            dec = getattr(blk, eng_name)
            def body(eng, opl=opl):
                for fn in opl:
                    fn(eng, sems)
            dec(body)
    st_top.close()
    return nc



def rope_tables(S):
    inv = 1.0 / 10000.0 ** (np.arange(0, HD, 2, dtype=np.float64) / HD)
    pos = np.arange(S, dtype=np.float64)
    fr = np.outer(pos, inv)
    emb = np.concatenate([fr, fr], axis=-1)
    return np.cos(emb), np.sin(emb)


def prep_maps(cfg, inputs):
    S, D, F, KD, NF = cfg.S, cfg.D, cfg.F, cfg.KD, cfg.NF
    TOWN, NJ = cfg.TOWN, cfg.NJ
    x = np.ascontiguousarray(np.asarray(inputs["L0_attn_hidden_states"], np.float32).reshape(S, D))
    Wq = np.asarray(inputs["L0_attn_Wq"], np.float32)
    Wkr = np.asarray(inputs["L0_attn_Wkr"], np.float32)
    Wdk = np.asarray(inputs["L0_attn_Wdk"], np.float32)
    Wupk = np.asarray(inputs["L0_attn_Wupk"], np.float32)
    Wupv = np.asarray(inputs["L0_attn_Wupv"], np.float32)
    Wo = np.asarray(inputs["L0_attn_Wo"], np.float32)
    Wg = np.asarray(inputs["L0_ffn_W_gate"], np.float32)
    Wu = np.asarray(inputs["L0_ffn_W_up"], np.float32)
    Wd = np.asarray(inputs["L0_ffn_W_down"], np.float32)

    xT_full = np.ascontiguousarray(x.T)
    xT = xT_full.reshape(KD, 128, S)

    qperm = np.concatenate([128 * h + PERM for h in range(NH)])
    WqT = np.ascontiguousarray(Wq[qperm].reshape(NH, 128, KD, 128).transpose(0, 2, 3, 1))
    WoT = np.ascontiguousarray(Wo.reshape(KD, 128, NOC, 128).transpose(0, 2, 3, 1))
    wkv = np.concatenate([Wdk, Wkr], axis=0)
    wkvT = np.ascontiguousarray(wkv.T).reshape(KD, 128, CD + RD)
    upkv = np.concatenate([Wupk, Wupv], axis=0)
    upkvT = np.ascontiguousarray(upkv.T).reshape(4, 128, 1536)
    WgT = np.ascontiguousarray(Wg.reshape(NF, 128, KD, 128).transpose(0, 2, 3, 1))
    WuT = np.ascontiguousarray(Wu.reshape(NF, 128, KD, 128).transpose(0, 2, 3, 1))
    WdT = np.ascontiguousarray(Wd.reshape(cfg.NDG, 256, NF, 128).transpose(0, 2, 3, 1))

    cos, sin = rope_tables(S)
    dscale = 1.0 / np.sqrt(HD)
    sgn = np.where(PERM < 64, -1.0, 1.0)
    cosq_p = (cos[:, PERM].T * dscale).astype(np.float32)
    sinmq_p = ((sin[:, PERM] * sgn[None, :]).T * dscale).astype(np.float32)
    rdim = np.array([r if r < 32 else r + 32 for r in range(64)])
    krow = np.concatenate([rdim, rdim])
    ksgn = np.where(krow < 64, -1.0, 1.0)
    cosk_p = np.ascontiguousarray(cos[:, krow].T.astype(np.float32))
    sinmk_p = np.ascontiguousarray((sin[:, krow] * ksgn[None, :]).T.astype(np.float32))

    consts = np.zeros((128, 417), np.float32)
    consts[:, 0:128] = np.eye(128)
    R32 = np.zeros((128, 128), np.float32)
    for i in range(128):
        R32[i ^ 32, i] = 1.0
    consts[:, 128:256] = R32
    consts[0, 256:384] = 1.0
    consts[:, 384] = 1.0
    consts[:, 416] = EPS

    shared = dict(xT=xT, wkvT=wkvT, upkvT=upkvT, wqT=WqT, woT=WoT,
                  wgT=WgT, wuT=WuT, wdT=WdT, consts=consts,
                  cosk=cosk_p, sinmk=sinmk_p)

    in_maps = []
    own_cols_all = []
    for c in range(NCORES):
        cols = np.concatenate([np.arange(128) + 128 * tile_of(c, j) for j in range(NJ)])
        own_cols_all.append(cols)
        x_ownT = np.ascontiguousarray(xT_full[:, cols]).reshape(KD, 128, TOWN)
        cq = np.ascontiguousarray(cosq_p[:, cols])
        sq = np.ascontiguousarray(sinmq_p[:, cols])
        mask = np.zeros((128, 2048), np.float32)
        for par, p in ((0, c), (1, 7 - c)):
            r = np.arange(128)[:, None]
            kk = np.arange(1024)[None, :]
            mask[:, par * 1024:(par + 1) * 1024] = np.where(kk <= 128 * p + r, 0.0, NEG)
        m = dict(shared)
        m.update(x_ownT=x_ownT, cosq=cq, sinmq=sq, maskin=mask)
        in_maps.append(m)
    return in_maps, own_cols_all


def assemble(cfg, results, own_cols_all):
    S, D = cfg.S, cfg.D
    y = np.empty((S, D), np.float32)
    for c in range(NCORES):
        yc = results[c]["y_own"].reshape(D, cfg.TOWN)
        y[own_cols_all[c], :] = yc.T
    return y.reshape(1, S, D)


def kernel(**inputs):
    cfg = Cfg()
    nc = build(cfg)
    in_maps, own_cols = prep_maps(cfg, inputs)
    res = run_bass_kernel_spmd(nc, in_maps, core_ids=list(range(NCORES)))
    return assemble(cfg, res.results, own_cols)


# version 11
# speedup vs baseline: 1.0227x; 1.0227x over previous
import numpy as np
from contextlib import ExitStack
import concourse.bass as bass
import concourse.mybir as mybir
from concourse.bass_utils import run_bass_kernel_spmd

F32 = mybir.dt.float32
BF16 = mybir.dt.bfloat16
AF = mybir.ActivationFunctionType
ALU = mybir.AluOpType
AX = mybir.AxisListType

NCORES = 8
HD = 128
NH = 32
KVH = 8
GR = NH // KVH
CD = 512
RD = 512
NOC = 32
NEG = -60000.0
EPS = 1e-5


class Cfg:
    def __init__(self, S=8192, D=4096, F=11008, hres=36,
                 phases=("A1", "A2", "B", "C", "F1", "F2"), dbg=()):
        self.S, self.D, self.F = S, D, F
        self.KD = D // 128
        self.NTT = S // 512
        self.TOWN = S // NCORES
        self.NJ = self.TOWN // 128
        self.NF = F // 128
        self.HRES = min(hres, self.NF)
        self.NDG = (self.KD + 1) // 2
        self.phases = set(phases)
        self.dbg = set(dbg)
        self.halves = [(h0, min(self.TOWN, h0 + 512)) for h0 in range(0, self.TOWN, 512)]
        assert S % 1024 == 0 and D % 256 == 0 and F % 128 == 0
        assert self.NJ == S // 1024


PERM = np.array(list(range(0, 32)) + list(range(64, 96)) +
                list(range(32, 64)) + list(range(96, 128)))


def tile_of(c, j):
    p = c if j % 2 == 0 else 7 - c
    return 8 * j + p



ENGINES = ("sync", "gpsimd", "scalar", "vector", "tensor")
QSEM = {"sync": "s", "gpsimd": "g", "scalar": "c"}


class Stream:
    def __init__(self, P, queue, nbufs, tag):
        self.P, self.q, self.n = P, queue, nbufs
        self.sems = []
        for i in range(nbufs):
            name = f"{tag}{i}"
            assert name not in P.cnt
            P.cnt[name] = 0
            P.dma_sems.add(name)
            self.sems.append(name)
        self.idx = 0

    def load(self, transfers, waits=(), idx=None):
        P = self.P
        if idx is None:
            idx = self.idx
        self.idx = idx + 1
        sem = self.sems[idx % self.n]
        P.waits(self.q, waits)
        for (out_ap, in_ap, kw) in transfers:
            P.cnt[sem] += 16
            def f(e, SS, out_ap=out_ap, in_ap=in_ap, sem=sem, kw=kw):
                e.dma_start(out=out_ap, in_=in_ap, **kw).then_inc(SS[sem], 16)
            P.raw(self.q, f)
        return (sem, P.cnt[sem])


class Prog:
    def __init__(self):
        self.ops = {e: [] for e in ENGINES}
        self.cnt = {k: 0 for k in ("v", "a", "p")}
        self.dma_sems = set()

    def stream(self, queue, nbufs, tag):
        return Stream(self, queue, nbufs, tag)

    def raw(self, eng, fn):
        self.ops[eng].append(fn)

    def waits(self, eng, toks):
        best = {}
        for t in toks:
            if t is None:
                continue
            s, v = t
            assert v <= self.cnt[s], (s, v, self.cnt[s])
            best[s] = max(best.get(s, 0), v)
        for s, v in sorted(best.items()):
            if v <= 0:
                continue
            def f(e, SS, s=s, v=v):
                e.wait_ge(SS[s], v)
            self.raw(eng, f)

    def fence(self):
        return [(s, c) for s, c in self.cnt.items() if c > 0]

    def _chain(self, eng, sem, fn, waits):
        self.waits(eng, waits)
        prev = self.cnt[sem]
        self.cnt[sem] += 1
        val = self.cnt[sem]
        def f(e, SS, fn=fn, sem=sem, prev=prev):
            if prev:
                e.wait_ge(SS[sem], prev)
            fn(e).then_inc(SS[sem], 1)
        self.raw(eng, f)
        return (sem, val)

    def ve(self, fn, waits=()):
        return self._chain("vector", "v", fn, waits)

    def sc(self, fn, waits=()):
        return self._chain("scalar", "a", fn, waits)

    def pe(self, fn, waits=(), inc=False):
        self.waits("tensor", waits)
        if inc:
            self.cnt["p"] += 1
            val = self.cnt["p"]
            def f(e, SS, fn=fn):
                fn(e).then_inc(SS["p"], 1)
            self.raw("tensor", f)
            return ("p", val)
        def f(e, SS, fn=fn):
            fn(e)
        self.raw("tensor", f)
        return None


class Guard:
    def __init__(self):
        self.toks = []

    def use(self, tok):
        if tok is not None:
            self.toks.append(tok)

    def take(self):
        t, self.toks = self.toks, []
        return t



def build(cfg: Cfg):
    nc = bass.Bass("TRN2", target_bir_lowering=False, debug=False)
    S, D, F, KD, NF = cfg.S, cfg.D, cfg.F, cfg.KD, cfg.NF
    NTT, NJ, TOWN = cfg.NTT, cfg.NJ, cfg.TOWN
    HALVES = cfg.halves
    P = Prog()

    dt = nc.dram_tensor
    xT = dt("xT", [KD, 128, S], F32, kind="ExternalInput").ap()
    x_ownT = dt("x_ownT", [KD, 128, TOWN], F32, kind="ExternalInput").ap()
    wkvT = dt("wkvT", [KD, 128, CD + RD], F32, kind="ExternalInput").ap()
    upkvT = dt("upkvT", [4, 128, 1536], F32, kind="ExternalInput").ap()
    wqT = dt("wqT", [NH, KD, 128, 128], F32, kind="ExternalInput").ap()
    woT = dt("woT", [KD, NOC, 128, 128], F32, kind="ExternalInput").ap()
    wgT = dt("wgT", [NF, KD, 128, 128], F32, kind="ExternalInput").ap()
    wuT = dt("wuT", [NF, KD, 128, 128], F32, kind="ExternalInput").ap()
    wdT = dt("wdT", [cfg.NDG, NF, 128, 256], F32, kind="ExternalInput").ap()
    consts = dt("consts", [128, 417], F32, kind="ExternalInput").ap()
    cosk = dt("cosk", [128, S], F32, kind="ExternalInput").ap()
    sinmk = dt("sinmk", [128, S], F32, kind="ExternalInput").ap()
    cosq = dt("cosq", [128, TOWN], F32, kind="ExternalInput").ap()
    sinmq = dt("sinmq", [128, TOWN], F32, kind="ExternalInput").ap()
    maskin = dt("maskin", [128, 2048], F32, kind="ExternalInput").ap()

    y_own = dt("y_own", [KD, 128, TOWN], F32, kind="ExternalOutput").ap()

    K_scr = dt("K_scr", [KVH, 128, S], BF16, kind="Internal").ap()
    V_scr = dt("V_scr", [S // 128, 128, KVH, 128], BF16, kind="Internal").ap()
    O_scr = dt("O_scr", [NOC, NJ, 128, 128], BF16, kind="Internal").ap()
    NSPILL = NF - cfg.HRES
    h_scr = dt("h_scr", [max(NSPILL, 1), 128, TOWN], BF16, kind="Internal").ap()

    def dbg(name, shape, dtype=F32):
        if name in cfg.dbg:
            return dt("dbg_" + name, shape, dtype, kind="ExternalOutput").ap()
        return None

    dbg_q = dbg("q", [NH, 128, TOWN])
    dbg_k = dbg("k", [KVH, 128, S])
    dbg_v = dbg("v", [S // 128, 128, KVH, 128])
    dbg_o = dbg("o", [NOC, NJ, 128, 128])
    dbg_r1 = dbg("r1", [KD, 128, TOWN])
    dbg_ln1 = dbg("ln1", [1, 4 * TOWN])
    dbg_ps = dbg("ps", [1, 2 * TOWN])
    dbg_bc = dbg("bc", [2, 128, TOWN])
    dbg_x1 = dbg("x1", [KD, 128, TOWN])
    dbg_h = dbg("h", [NF, 128, TOWN])
    dbg_hend = dbg("hend", [NF, 128, TOWN])

    def SB(stk, name, shape, dtype):
        return stk.enter_context(nc.sbuf_tensor(name, shape, dtype))

    def PS(stk, name, shape, dtype=F32):
        esz = 4 if dtype == F32 else 2
        per = int(np.prod(shape[1:])) * esz
        pad = (-per) % 2048
        shape = list(shape)
        shape[-1] = shape[-1] + pad // (esz * int(np.prod(shape[1:-1])))
        return stk.enter_context(nc.psum_tensor(name, shape, dtype))

    st_top = ExitStack()

    const_sb = SB(st_top, "const_sb", [128, 417], F32)
    ident_bf = SB(st_top, "ident_bf", [128, 128], BF16)
    r32_bf = SB(st_top, "r32_bf", [128, 128], BF16)
    ones_bf = SB(st_top, "ones_bf", [128, 32], BF16)
    onesr = SB(st_top, "onesr", [32, 128], F32)
    mask_sb = SB(st_top, "mask_sb", [128, 2048], BF16)
    cosq_sb = SB(st_top, "cosq_sb", [128, TOWN], F32)
    sinmq_sb = SB(st_top, "sinmq_sb", [128, TOWN], F32)

    stCS = P.stream("sync", 1, "dCS")
    stCG = P.stream("gpsimd", 1, "dCG")
    tok_cs = stCS.load([(const_sb[:], consts, {}), (onesr[:], consts[0:32, 256:384], {}),
                        (cosq_sb[:], cosq, {}), (sinmq_sb[:], sinmq, {})])
    tok_cg = stCG.load([(ident_bf[:], consts[:, 0:128], {}),
                        (r32_bf[:], consts[:, 128:256], {}),
                        (ones_bf[:], consts[:, 384:416], {}),
                        (mask_sb[:], maskin, {})])
    tok_const = tok_onesr = tok_cq = tok_sq = tok_cs
    tok_ident = tok_r32 = tok_ones = tok_mask = tok_cg

    def load_slab(stream, idx, dst, src3d, kd, width, extra_waits=()):
        ngrp = max(1, (kd * 128) // 1024)
        ngrp = min(ngrp, kd)
        per = (kd + ngrp - 1) // ngrp
        tr = []
        for gi in range(ngrp):
            k0, k1 = gi * per, min(kd, (gi + 1) * per)
            if k0 >= k1:
                break
            tr.append((dst.ap().rearrange("p (k m) -> p k m", k=kd)[:, k0:k1, :],
                       src3d[k0:k1].rearrange("k p m -> p k m"), {}))
        return stream.load(tr, waits=extra_waits, idx=idx)

    tok_q_head = [None] * NH
    tok_A1_stores = []
    fence = []

    if "A1" in cfg.phases:
        stA = ExitStack()
        CW = CD + RD
        wkv_sb = SB(stA, "wkv_sb", [128, KD * CW], BF16)
        upkv_sb = SB(stA, "upkv_sb", [128, 4 * 1536], BF16)
        x_bf = [SB(stA, f"x_bf{b}", [128, KD * 512], BF16) for b in range(2)]
        ckv_bf = [SB(stA, f"ckv_bf{b}", [128, 4 * 512], BF16) for b in range(2)]
        ckt = [SB(stA, f"ckt{b}", [128, 1024], F32) for b in range(2)]
        kr_bf = [SB(stA, f"kr_bf{b}", [128, 512], BF16) for b in range(2)]
        kt1 = SB(stA, "kt1", [128, 512], F32)
        kt2 = SB(stA, "kt2", [128, 512], F32)
        kA = SB(stA, "kA", [128, 512], BF16)
        kc_bf = SB(stA, "kc_bf", [128, 512], BF16)
        vt_bf = SB(stA, "vt_bf", [128, 512], BF16)

        ps_ckv = [PS(stA, f"ps_ckv{b}", [128, 512]) for b in range(2)]
        ps_kr = [PS(stA, f"ps_kr{b}", [128, 512]) for b in range(2)]
        ps_rot = PS(stA, "ps_rot", [128, 512])
        ps_kc = PS(stA, "ps_kc", [128, 512])
        ps_v = [PS(stA, f"ps_v{b}", [128, 512]) for b in range(2)]

        g_xbf = [Guard(), Guard()]
        g_ckt = [Guard(), Guard()]
        g_psckv = [Guard(), Guard()]
        g_pskr = [Guard(), Guard()]
        g_psrot = Guard()
        g_pskc = Guard()
        g_psv = [Guard(), Guard()]
        g_ckvbf = [Guard(), Guard()]
        g_krbf = [Guard(), Guard()]
        g_kA = Guard()
        g_kcbf = Guard()
        g_vtbf = Guard()

        stWV = P.stream("gpsimd", 1, "dWV")
        stX = P.stream("gpsimd", 2, "dX")
        stCK = P.stream("sync", 2, "dCK")
        stKA = P.stream("sync", 1, "sKA")
        stKC = P.stream("sync", 1, "sKC")
        stVst = P.stream("sync", 1, "sVs")

        load_slab(stWV, 0, wkv_sb, wkvT, KD, CW)
        tok_wkv = tok_upkv = load_slab(stWV, 0, upkv_sb, upkvT, 4, 1536)

        tok_x = [None] * NTT
        tok_ck = [None] * NTT
        tok_ckv_done = [None] * NTT

        def load_tile(tt):
            b = tt % 2
            ngrp = max(1, KD // 8)
            per = (KD + ngrp - 1) // ngrp
            tr = []
            for gi in range(ngrp):
                k0, k1 = gi * per, min(KD, (gi + 1) * per)
                tr.append((x_bf[b].ap().rearrange("p (k t) -> p k t", k=KD)[:, k0:k1, :],
                           xT[k0:k1, :, tt * 512:(tt + 1) * 512].rearrange("k p t -> p k t"),
                           {}))
            tok_x[tt] = stX.load(tr, waits=g_xbf[b].take(), idx=tt)
            tok_ck[tt] = stCK.load(
                [(ckt[b][:, 0:512], cosk[:, tt * 512:(tt + 1) * 512], {}),
                 (ckt[b][:, 512:1024], sinmk[:, tt * 512:(tt + 1) * 512], {})],
                waits=g_ckt[b].take(), idx=tt)

        load_tile(0)
        if NTT > 1:
            load_tile(1)

        rope_state = {}

        def rope_tail(tt, ri):
            b = tt % 2
            bk = ri % 2
            tok_mm = rope_state.pop((tt, ri))
            t_copy = P.ve(lambda e, o=kr_bf[bk][:], i2=ps_kr[bk][:]: e.tensor_copy(o, i2),
                          waits=[tok_mm] + g_krbf[bk].take())
            g_pskr[bk].use(t_copy)
            t_rot = P.pe(lambda e, r=kr_bf[bk][:]: e.matmul(ps_rot[:], r32_bf[:], r,
                                                            start=True, stop=True),
                         waits=[t_copy, tok_r32] + g_psrot.take(), inc=True)
            t_1 = P.ve(lambda e, c1=ckt[b][:, 0:512], r=kr_bf[bk][:]: e.tensor_tensor(
                           out=kt1[:], in0=r, in1=c1, op=ALU.mult),
                       waits=[tok_ck[tt]])
            t_2 = P.ve(lambda e, c2=ckt[b][:, 512:1024]: e.tensor_tensor(
                           out=kt2[:], in0=ps_rot[:], in1=c2, op=ALU.mult),
                       waits=[t_rot])
            g_psrot.use(t_2)
            g_krbf[bk].use(t_2)
            t_A = P.ve(lambda e: e.tensor_tensor(out=kA[:], in0=kt1[:], in1=kt2[:],
                                                 op=ALU.add),
                       waits=g_kA.take())
            g_ckt[b].use(t_1)
            g_ckt[b].use(t_2)
            g_kA.use(stKA.load(
                [(K_scr[2 * ri + hh, 0:64, slice(tt * 512, (tt + 1) * 512)],
                  kA[hh * 64:(hh + 1) * 64, :], {}) for hh in range(2)], waits=[t_A]))

        def emit_proj(tt):
            b = tt % 2
            xa = x_bf[b].ap().rearrange("p (k t) -> p k t", k=KD)
            t_ev = None
            for mi in range(4):
                pp = ps_ckv[mi % 2]
                tok_mm = None
                for ki in range(KD):
                    w = ([tok_x[tt], tok_wkv] + g_psckv[mi % 2].take()) if ki == 0 else []
                    tok_mm = P.pe(
                        lambda e, o=pp[:], l=wkv_sb[:, ki * CW + mi * 128: ki * CW + mi * 128 + 128],
                               r=xa[:, ki, :], s=(ki == 0), t_=(ki == KD - 1):
                            e.matmul(o, l, r, start=s, stop=t_),
                        waits=w, inc=(ki == KD - 1))
                t_ev = P.ve(lambda e, o=ckv_bf[b][:, mi * 512:(mi + 1) * 512], i2=pp[:]:
                               e.tensor_copy(o, i2),
                            waits=[tok_mm] + (g_ckvbf[b].take() if mi == 0 else []))
                g_psckv[mi % 2].use(t_ev)
            tok_ckv_done[tt] = t_ev
            for ri in range(RD // 128):
                bk = ri % 2
                tok_mm = None
                for ki in range(KD):
                    w = ([tok_x[tt]] + g_pskr[bk].take()) if ki == 0 else []
                    tok_mm = P.pe(
                        lambda e, o=ps_kr[bk][:],
                               l=wkv_sb[:, ki * CW + CD + ri * 128: ki * CW + CD + ri * 128 + 128],
                               r=xa[:, ki, :], s=(ki == 0), t_=(ki == KD - 1):
                            e.matmul(o, l, r, start=s, stop=t_),
                        waits=w, inc=(ki == KD - 1))
                rope_state[(tt, ri)] = tok_mm
                if ri > 0:
                    rope_tail(tt, ri - 1)
            g_xbf[b].use(("p", P.cnt["p"]))
            rope_tail(tt, RD // 128 - 1)

        def emit_upkv(tt):
            b = tt % 2
            cb = ckv_bf[b]
            tcol = slice(tt * 512, (tt + 1) * 512)

            def kc_group(ri):
                tok_mm = None
                for ki in range(4):
                    w = ([tok_ckv_done[tt], tok_upkv] + g_pskc.take()) if ki == 0 else []
                    tok_mm = P.pe(
                        lambda e, o=ps_kc[:],
                               l=upkv_sb[:, ki * 1536 + ri * 128: ki * 1536 + ri * 128 + 128],
                               r=cb[:, ki * 512:(ki + 1) * 512], s=(ki == 0), t_=(ki == 3):
                            e.matmul(o, l, r, start=s, stop=t_),
                        waits=w, inc=(ki == 3))
                t_kc = P.ve(lambda e: e.tensor_copy(kc_bf[:], ps_kc[:]),
                            waits=[tok_mm] + g_kcbf.take())
                g_pskc.use(t_kc)
                g_kcbf.use(stKC.load(
                    [(K_scr[2 * ri + hh, 64:128, tcol], kc_bf[hh * 64:(hh + 1) * 64, :], {})
                     for hh in range(2)], waits=[t_kc]))

            def v_group(tc, vh):
                gl = tt * 4 + tc
                pv = ps_v[vh]
                tok_mm = None
                for ki in range(4):
                    w = ([tok_ckv_done[tt]] + g_psv[vh].take()) if ki == 0 else []
                    tok_mm = P.pe(
                        lambda e, o=pv[:],
                               l=cb[:, ki * 512 + tc * 128: ki * 512 + tc * 128 + 128],
                               r=upkv_sb[:, ki * 1536 + 512 + vh * 512: ki * 1536 + 512 + (vh + 1) * 512],
                               s=(ki == 0), t_=(ki == 3):
                            e.matmul(o, l, r, start=s, stop=t_),
                        waits=w, inc=(ki == 3))
                t_v = P.ve(lambda e, i2=pv[:]: e.tensor_copy(vt_bf[:], i2),
                           waits=[tok_mm] + g_vtbf.take())
                g_psv[vh].use(t_v)
                g_vtbf.use(stVst.load(
                    [(V_scr[gl, :, 4 * vh:4 * vh + 4, :],
                      vt_bf.ap().rearrange("p (h d) -> p h d", h=4), {})],
                    waits=[t_v]))

            for tc in range(4):
                kc_group(tc)
                v_group(tc, 0)
                v_group(tc, 1)
            g_ckvbf[b].use(("p", P.cnt["p"]))

        emit_proj(0)
        for tt in range(NTT):
            if tt + 1 < NTT:
                emit_proj(tt + 1)
            if tt + 2 < NTT:
                load_tile(tt + 2)
            emit_upkv(tt)

        tok_A1_stores = [(nm, P.cnt[nm]) for nm in ("sKA0", "sKC0", "sVs0")]
        if dbg_k is not None:
            P.stream("gpsimd", 1, "dbgK").load([(dbg_k, K_scr, {})], waits=tok_A1_stores)
        if dbg_v is not None:
            P.stream("gpsimd", 1, "dbgV").load([(dbg_v, V_scr, {})], waits=tok_A1_stores)
        fence = P.fence()
        stA.close()

    st_q = ExitStack()
    q_all = SB(st_q, "q_all", [128, NH * TOWN], BF16) \
        if ("A2" in cfg.phases or "B" in cfg.phases) else None

    if "A2" in cfg.phases:
        stA2 = ExitStack()
        x_own_bf = SB(stA2, "x_own_bf", [128, KD * TOWN], BF16)
        wq_bf = [SB(stA2, f"wq_bf{b}", [128, KD * 128], BF16) for b in range(2)]
        qraw = [SB(stA2, f"qraw{b}", [128, TOWN], BF16) for b in range(2)]
        qt1 = SB(stA2, "qt1", [128, TOWN], F32)
        qt2 = SB(stA2, "qt2", [128, TOWN], F32)
        ps_q = [PS(stA2, f"ps_q{b}", [128, TOWN]) for b in range(2)]
        ps_qrot = PS(stA2, "ps_qrot", [128, TOWN])

        g_wq = [Guard(), Guard()]
        g_psq = [Guard(), Guard()]
        g_psqrot = Guard()
        g_qraw = [Guard(), Guard()]

        stXO = P.stream("gpsimd", 1, "dXO")
        stWQ = P.stream("gpsimd", 2, "dWQ")
        tok_xo = stXO.load([(x_own_bf[:, ki * TOWN:(ki + 1) * TOWN], x_ownT[ki], {})
                            for ki in range(KD)], waits=fence)

        tok_wq = [None] * NH

        def load_wq(hi):
            tok_wq[hi] = load_slab(stWQ, hi, wq_bf[hi % 2], wqT[hi], KD, 128,
                                   extra_waits=g_wq[hi % 2].take())

        load_wq(0)
        if NH > 1:
            load_wq(1)

        q_state = {}

        def q_mms(hi):
            bw = hi % 2
            pq = ps_q[hi % 2]
            tok_mm = None
            for ki in range(KD):
                for (h0, h1) in HALVES:
                    w = ([tok_xo, tok_wq[hi]] + g_psq[hi % 2].take()) \
                        if (ki == 0 and h0 == 0) else []
                    tok_mm = P.pe(
                        lambda e, o=pq[:, h0:h1],
                               l=wq_bf[bw][:, ki * 128:(ki + 1) * 128],
                               r=x_own_bf[:, ki * TOWN + h0: ki * TOWN + h1],
                               s=(ki == 0), t_=(ki == KD - 1):
                            e.matmul(o, l, r, start=s, stop=t_),
                        waits=w, inc=(ki == KD - 1 and h1 == TOWN))
            q_state[hi] = tok_mm
            g_wq[hi % 2].use(("p", P.cnt["p"]))
            if hi + 2 < NH:
                load_wq(hi + 2)

        def q_rope(hi):
            bq = hi % 2
            pq = ps_q[bq]
            tok_mm = q_state.pop(hi)
            t_copy = P.ve(lambda e, o=qraw[bq][:], i2=pq[:, 0:TOWN]: e.tensor_copy(o, i2),
                          waits=[tok_mm] + g_qraw[bq].take())
            g_psq[bq].use(t_copy)
            t_rot = None
            for (h0, h1) in HALVES:
                t_rot = P.pe(lambda e, o=ps_qrot[:, h0:h1], r=qraw[bq][:, h0:h1]:
                                e.matmul(o, r32_bf[:], r, start=True, stop=True),
                             waits=([t_copy, tok_r32] + g_psqrot.take()) if h0 == 0 else (),
                             inc=(h1 == TOWN))
            t_1 = P.ve(lambda e, r=qraw[bq][:]: e.tensor_tensor(out=qt1[:], in0=r,
                                                                in1=cosq_sb[:], op=ALU.mult),
                       waits=[tok_cq])
            t_2 = P.ve(lambda e: e.tensor_tensor(out=qt2[:], in0=ps_qrot[:, 0:TOWN],
                                                 in1=sinmq_sb[:], op=ALU.mult),
                       waits=[t_rot, tok_sq])
            g_psqrot.use(t_2)
            g_qraw[bq].use(t_2)
            t_q = P.ve(lambda e, o=q_all[:, hi * TOWN:(hi + 1) * TOWN]:
                          e.tensor_tensor(out=o, in0=qt1[:], in1=qt2[:], op=ALU.add))
            tok_q_head[hi] = t_q

        q_mms(0)
        for hi in range(NH):
            if hi + 1 < NH:
                q_mms(hi + 1)
            q_rope(hi)

        if dbg_q is not None:
            P.stream("gpsimd", 1, "dbgQ").load(
                [(dbg_q[hi], q_all[:, hi * TOWN:(hi + 1) * TOWN], {}) for hi in range(NH)],
                waits=[tok_q_head[NH - 1]])
        fence = P.fence()
        stA2.close()

    tok_O_stores = None
    if "B" in cfg.phases:
        stB = ExitStack()
        k_sb = [SB(stB, f"k_sb{b}", [128, S], BF16) for b in range(2)]
        v_sb = [SB(stB, f"v_sb{b}", [128, S], BF16) for b in range(2)]
        pr_bf = [SB(stB, f"pr_bf{b}", [128, S], BF16) for b in range(2)]
        ksq = SB(stB, "ksq", [128, S], BF16)
        pt_sb = [SB(stB, f"pt_sb{b}", [128, 1024], BF16) for b in range(2)]
        kslot = SB(stB, "kslot", [1, 16], F32)
        kn1 = SB(stB, "kn1", [1, 1], F32)
        kn_bc = [SB(stB, f"kn_bc{b}", [128, 1], F32) for b in range(2)]
        qsq = SB(stB, "qsq", [128, 128], BF16)
        qnrow = SB(stB, "qnrow", [1, 128], F32)
        negb = [SB(stB, f"negb{b}", [128, 1], F32) for b in range(2)]
        qn_s = SB(stB, "qn_s", [128, 1], F32)
        rsum = SB(stB, "rsum", [128, 1], F32)
        rs_all = [SB(stB, f"rs_all{b}", [128, 8], F32) for b in range(2)]
        rinv = SB(stB, "rinv", [128, 1], F32)
        onb = SB(stB, "onb", [128, 128], BF16)
        otr = SB(stB, "otr", [128, 128], BF16)

        ps_s = [PS(stB, f"ps_s{b}", [128, 1024]) for b in range(2)]
        ps_tr = [PS(stB, f"ps_tr{b}", [128, 1024], BF16) for b in range(2)]
        ps_o = [PS(stB, f"ps_o{b}", [128, 512]) for b in range(2)]

        g_kv = [Guard(), Guard()]
        g_pss = [Guard(), Guard()]
        g_pstr = [Guard(), Guard()]
        g_ptsb = [Guard(), Guard()]
        g_pso = [Guard(), Guard()]
        g_prbf = [Guard(), Guard()]
        g_negb = [Guard(), Guard()]
        g_rs = [Guard(), Guard()]
        g_rinv = Guard()
        g_otr = Guard()
        g_onb = Guard()
        g_ksq = Guard()
        g_knbc = [Guard(), Guard()]
        g_qsq = Guard()
        g_qn = Guard()

        stKV = P.stream("sync", 2, "dKV")
        stOst = P.stream("sync", 1, "sO")
        tok_kld = [None] * KVH
        tok_kn = [None] * KVH

        def load_kv(g, first=False):
            w = g_kv[g % 2].take() + list(tok_A1_stores)
            if first:
                w = w + fence
            tok_kld[g] = stKV.load(
                [(k_sb[g % 2][:], K_scr[g], {}),
                 (v_sb[g % 2].ap().rearrange("p (tc d) -> p tc d", d=128),
                  V_scr[:, :, g, :].rearrange("tc p d -> p tc d"), {})],
                waits=w, idx=g)

        def emit_knorm(g):
            bg = g % 2
            t_sq = P.sc(lambda e, i2=k_sb[bg][:]: e.activation(out=ksq[:], in_=i2,
                                                               func=AF.Square),
                        waits=[tok_kld[g]] + g_ksq.take())
            nblk = S // 512
            t_mx = None
            for b in range(nblk):
                w = [t_sq] + ([t_mx] if t_mx else [])
                if b == 0:
                    w += g_pss[0].take()
                t_mm = P.pe(lambda e, o=ps_s[0][0:32, 0:512],
                                   r=ksq[:, b * 512:(b + 1) * 512]:
                               e.matmul(o, ones_bf[:, 0:32], r, start=True, stop=True),
                            waits=w, inc=True)
                t_mx = P.ve(lambda e, a=kslot[0:1, b:b + 1], i2=ps_s[0][0:1, 0:512]:
                               e.tensor_scalar(out=ksq[0:1, 0:512], in0=i2, scalar1=1.0,
                                               scalar2=None, op0=ALU.mult, op1=ALU.max,
                                               accum_out=a),
                            waits=[t_mm])
                g_pss[0].use(t_mx)
            t_red = P.ve(lambda e, n=nblk: e.reduce_max(kn1[:], kslot[0:1, 0:n], axis=AX.X),
                         waits=[t_mx])
            t_srt = P.sc(lambda e: e.activation(out=kn1[:], in_=kn1[:], func=AF.Sqrt,
                                                scale=1.0),
                         waits=[t_red])
            t_rf = P.ve(lambda e: e.tensor_scalar(out=qnrow[:], in0=onesr[0:1, :],
                                                  scalar1=kn1[0:1, 0:1], scalar2=None,
                                                  op0=ALU.mult),
                        waits=[t_srt, tok_onesr] + g_qn.take())
            t_bc = P.pe(lambda e: e.transpose(ps_s[0][0:128, 0:1], qnrow[:],
                                              const_sb[0:1, 384:385]),
                        waits=[t_rf] + g_pss[0].take(), inc=True)
            g_qn.use(t_bc)
            t_cp = P.ve(lambda e, o=kn_bc[bg][:]: e.tensor_scalar(
                            out=o, in0=ps_s[0][:, 0:1], scalar1=-1.02, scalar2=None,
                            op0=ALU.mult),
                        waits=[t_bc] + g_knbc[bg].take())
            g_pss[0].use(t_cp)
            g_ksq.use(t_bc)
            tok_kn[g] = t_cp

        load_kv(0, first=True)
        if KVH > 1:
            load_kv(1)
        emit_knorm(0)

        iters = [(g, g * GR + hh, j) for g in range(KVH) for hh in range(GR)
                 for j in range(NJ)]
        state = {}

        def emit_bias(i):
            g, h, j = iters[i]
            bi = i % 2
            qtile = q_all[:, h * TOWN + j * 128: h * TOWN + j * 128 + 128]
            t_sq = P.sc(lambda e, i2=qtile: e.activation(out=qsq[:], in_=i2, func=AF.Square),
                        waits=([tok_q_head[h]] if tok_q_head[h] else []) + g_qsq.take())
            po = ps_o[i % 2]
            t_mm = P.pe(lambda e: e.matmul(po[0:32, 0:128], ones_bf[:, 0:32], qsq[:],
                                           start=True, stop=True),
                        waits=[t_sq, tok_ones] + g_pso[i % 2].take(), inc=True)
            t_cr = P.ve(lambda e: e.tensor_copy(qnrow[:], po[0:1, 0:128]),
                        waits=[t_mm] + g_qn.take())
            g_qsq.use(t_mm)
            t_tr = P.pe(lambda e: e.transpose(po[0:128, 128:129], qnrow[:],
                                              const_sb[0:1, 384:385]),
                        waits=[t_cr], inc=True)
            g_qn.use(t_tr)
            t_st = P.sc(lambda e: e.activation(out=qn_s[:], in_=po[:, 128:129],
                                               func=AF.Sqrt),
                        waits=[t_tr])
            t_nb = P.ve(lambda e, o=negb[bi][:], kn=kn_bc[(g % 2)][:]:
                           e.tensor_tensor(out=o, in0=qn_s[:], in1=kn, op=ALU.mult),
                        waits=[t_st, tok_kn[g]] + g_negb[bi].take())
            g_pso[i % 2].use(t_nb)
            g_knbc[g % 2].use(t_nb)
            state[("b", i)] = t_nb

        def emit_iter(i, hooks=()):
            g, h, j = iters[i]
            bg = g % 2
            bi = i % 2
            t_nb = state.pop(("b", i))
            qtile = q_all[:, h * TOWN + j * 128: h * TOWN + j * 128 + 128]
            nkb = 2 * (j + 1)
            npr = nkb // 2
            nkc = 4 * nkb
            po = ps_o[i % 2]
            pr = pr_bf[bi]
            tok_exp = [None] * npr
            tok_pev = [None] * npr
            tok_tr = [None] * npr
            avtok = [None]

            def mm_pair(pb):
                bank = pb % 2
                masked = (pb == npr - 1)
                tok_mm = None
                for half in range(2):
                    kb = 2 * pb + half
                    w = g_pss[bank].take() if half == 0 else []
                    if pb == 0 and half == 0:
                        w += [tok_kld[g]]
                        if tok_q_head[h]:
                            w.append(tok_q_head[h])
                    tok_mm = P.pe(
                        lambda e, o=ps_s[bank][:, half * 512:(half + 1) * 512], l=qtile,
                               r=k_sb[bg][:, kb * 512:(kb + 1) * 512],
                               t_=(not masked):
                            e.matmul(o, l, r, start=True, stop=t_),
                        waits=w, inc=(half == 1 and not masked))
                if masked:
                    moff = (j % 2) * 1024
                    for half in range(2):
                        tok_mm = P.pe(
                            lambda e, o=ps_s[bank][:, half * 512:(half + 1) * 512],
                                   r=mask_sb[:, moff + half * 512: moff + (half + 1) * 512]:
                                e.matmul(o, ident_bf[:], r, start=False, stop=True),
                            waits=[tok_mask] if half == 0 else (), inc=(half == 1))
                w = [tok_mm, t_nb]
                if pb == 0:
                    w += g_prbf[bi].take() + g_rs[bi].take()
                tok_exp[pb] = P.sc(
                    lambda e, o=pr[:, pb * 1024:(pb + 1) * 1024],
                           i2=ps_s[pb % 2][:], a=rs_all[bi][:, pb:pb + 1],
                           bb=negb[bi].ap():
                        e.activation(out=o, in_=i2, func=AF.Exp, bias=bb, scale=1.0,
                                     accum_out=a),
                    waits=w)
                g_pss[bank].use(tok_exp[pb])

            def do_tr(pk):
                btr = pk % 2
                t_tr = None
                for i8 in range(8):
                    kc = pk * 8 + i8
                    w = []
                    if i8 == 0:
                        w = [tok_exp[pk], tok_ident] + g_pstr[btr].take()
                    t_tr = P.pe(
                        lambda e, o=ps_tr[btr][:, i8 * 128:(i8 + 1) * 128],
                               i2=pr[:, kc * 128:(kc + 1) * 128]:
                            e.transpose(o, i2, ident_bf[:]),
                        waits=w, inc=(i8 == 7))
                tok_tr[pk] = t_tr
                t_pev = P.ve(lambda e, o=pt_sb[btr][:], i2=ps_tr[btr][:, 0:1024]:
                                e.tensor_copy(o, i2),
                             waits=[t_tr] + g_ptsb[btr].take())
                g_pstr[btr].use(t_pev)
                tok_pev[pk] = t_pev

            def do_av(pk):
                btr = pk % 2
                for i8 in range(8):
                    kc = pk * 8 + i8
                    w = [tok_pev[pk]] if i8 == 0 else []
                    if kc == 0:
                        w = list(w) + g_pso[i % 2].take()
                    avtok[0] = P.pe(
                        lambda e, o=po[:, 0:128],
                               l=pt_sb[btr][:, i8 * 128:(i8 + 1) * 128],
                               r=v_sb[bg][:, kc * 128:(kc + 1) * 128],
                               s=(kc == 0), t_=(kc == nkc - 1):
                            e.matmul(o, l, r, start=s, stop=t_),
                        waits=w, inc=(i8 == 7))
                g_ptsb[btr].use(avtok[0])

            for pb in range(npr + 2):
                if pb < npr:
                    mm_pair(pb)
                if pb == 1 or (npr == 1 and pb == 1):
                    for hk in hooks:
                        hk()
                    hooks = ()
                if pb >= 1 and pb - 1 < npr:
                    do_tr(pb - 1)
                if pb >= 2 and pb - 2 < npr:
                    do_av(pb - 2)
            for hk in hooks:
                hk()
            g_prbf[bi].use(tok_tr[npr - 1])

            def epilogue():
                t_rsu = P.ve(lambda e, n=npr: e.reduce_sum(rsum[:], rs_all[bi][:, 0:n],
                                                           axis=AX.X),
                             waits=[tok_exp[-1]])
                g_rs[bi].use(t_rsu)
                t_ri = P.ve(lambda e: e.reciprocal(rinv[:], rsum[:]),
                            waits=g_rinv.take())
                t_on = P.ve(lambda e, i2=po[:, 0:128]: e.tensor_scalar(
                                out=onb[:], in0=i2, scalar1=rinv.ap(), scalar2=None,
                                op0=ALU.mult),
                            waits=[avtok[0], t_ri] + g_onb.take())
                g_rinv.use(t_on)
                g_pso[i % 2].use(t_on)
                botr = npr % 2
                t_otr = P.pe(lambda e, o=ps_tr[botr][:, 0:128]:
                                e.transpose(o, onb[:], ident_bf[:]),
                             waits=[t_on] + g_pstr[botr].take(), inc=True)
                g_onb.use(t_otr)
                t_oc = P.ve(lambda e, i2=ps_tr[botr][:, 0:128]: e.tensor_copy(otr[:], i2),
                            waits=[t_otr] + g_otr.take())
                g_pstr[botr].use(t_oc)
                g_otr.use(stOst.load([(O_scr[h, j], otr[:], {})], waits=[t_oc]))
            return epilogue

        emit_bias(0)
        prev_epi = None
        for i in range(len(iters)):
            hooks = []
            if prev_epi:
                hooks.append(prev_epi)
            if i + 1 < len(iters):
                g2 = iters[i + 1][0]
                if g2 != iters[i][0] and tok_kn[g2] is None:
                    hooks.append(lambda g2=g2: emit_knorm(g2))
                hooks.append(lambda i=i: emit_bias(i + 1))
            prev_epi = emit_iter(i, hooks)
            g, h, j = iters[i]
            if h % GR == GR - 1 and j == NJ - 1:
                g_kv[g % 2].use(("p", P.cnt["p"]))
                if g + 2 < KVH:
                    load_kv(g + 2)
        prev_epi()

        tok_O_stores = ("sO0", P.cnt["sO0"])
        if dbg_o is not None:
            P.stream("gpsimd", 1, "dbgO").load([(dbg_o, O_scr, {})], waits=[tok_O_stores])
        fence = P.fence()
        stB.close()
    st_q.close()

    st_rx = ExitStack()
    need_rx = ("C" in cfg.phases) or ("F1" in cfg.phases) or ("F2" in cfg.phases)
    rx = SB(st_rx, "rx", [128, KD * TOWN], BF16) if need_rx else None

    def mk_rvec(rstat):
        def rvec(i, h0=0, h1=None):
            h1 = TOWN if h1 is None else h1
            return rstat[0:1, i * TOWN + h0: i * TOWN + h1]
        return rvec

    tok_x1 = [None] * KD

    if "C" in cfg.phases:
        stC = ExitStack()
        o_sb = SB(stC, "o_sb", [128, NOC * TOWN], BF16)
        wo_bf = [SB(stC, f"wo_bf{b}", [128, NOC * 128], BF16) for b in range(2)]
        xf32 = [SB(stC, f"xf32{b}", [128, TOWN], F32) for b in range(2)]
        r1sq = [SB(stC, f"r1sq{b}", [128, TOWN], BF16) for b in range(2)]
        bc_rstd = SB(stC, "bc_rstd", [128, TOWN], F32)
        bc_mr = SB(stC, "bc_mr", [128, TOWN], F32)
        tmpb = SB(stC, "tmpb", [128, TOWN], F32) if TOWN <= 512 else xf32[0]

        rstat1 = SB(stC, "rstat1", [32, 4 * TOWN], F32)
        rvec = mk_rvec(rstat1)
        P.ve(lambda e: e.memset(rstat1[:], 0.0), waits=fence)
        ps_a = [PS(stC, f"ps_a{b}", [128, TOWN]) for b in range(2)]
        ps_sum = PS(stC, "ps_sum", [32, max(TOWN, 512)])
        ps_sq = PS(stC, "ps_sq", [32, max(TOWN, 512)])

        g_wo = [Guard(), Guard()]
        g_psa = [Guard(), Guard()]
        g_xf = [Guard(), Guard()]
        g_sq = [Guard(), Guard()]

        stOL = P.stream("sync", 1, "dOL")
        stWO = P.stream("gpsimd", 2, "dWO")
        stXF = P.stream("sync", 2, "dXF")
        w0 = list(fence)
        if tok_O_stores:
            w0.append(tok_O_stores)
        tok_oall = stOL.load(
            [(o_sb.ap().rearrange("p (c j q) -> p c j q", c=NOC, j=NJ)[:, hi],
              O_scr[hi].rearrange("j d q -> d j q"), {}) for hi in range(NOC)],
            waits=w0)

        tok_wo = [None] * KD

        def load_wo(oi, first=False):
            tok_wo[oi] = load_slab(stWO, oi, wo_bf[oi % 2], woT[oi], NOC, 128,
                                   extra_waits=g_wo[oi % 2].take() + (list(fence) if first else []))

        load_wo(0, first=True)
        if KD > 1:
            load_wo(1)

        tok_r1 = [None] * KD
        tok_stats_done = None
        for oi in range(KD):
            bw = oi % 2
            pa = ps_a[oi % 2]
            t_xf = stXF.load([(xf32[bw][:], x_ownT[oi], {})], waits=g_xf[bw].take(), idx=oi)
            tok_mm = None
            for ki in range(NOC):
                for (h0, h1) in HALVES:
                    w = ([tok_oall, tok_wo[oi]] + g_psa[oi % 2].take()) \
                        if (ki == 0 and h0 == 0) else []
                    tok_mm = P.pe(
                        lambda e, o=pa[:, h0:h1],
                               l=wo_bf[bw][:, ki * 128:(ki + 1) * 128],
                               r=o_sb[:, ki * TOWN + h0: ki * TOWN + h1],
                               s=(ki == 0), t_=(ki == NOC - 1):
                            e.matmul(o, l, r, start=s, stop=t_),
                        waits=w, inc=(ki == NOC - 1 and h1 == TOWN))
            t_r1 = P.ve(lambda e, o=rx[:, oi * TOWN:(oi + 1) * TOWN], i=pa[:, 0:TOWN],
                               x=xf32[bw][:]:
                           e.scalar_tensor_tensor(out=o, in0=i, scalar=1.0, in1=x,
                                                  op0=ALU.mult, op1=ALU.add),
                        waits=[tok_mm, t_xf])
            tok_r1[oi] = t_r1
            g_psa[oi % 2].use(t_r1)
            g_xf[bw].use(t_r1)
            t_sqa = P.sc(lambda e, o=r1sq[oi % 2][:], i=rx[:, oi * TOWN:(oi + 1) * TOWN]:
                            e.activation(out=o, in_=i, func=AF.Square),
                         waits=[t_r1] + g_sq[oi % 2].take())
            for (h0, h1) in HALVES:
                P.pe(lambda e, o=ps_sum[0:1, h0:h1],
                            r=rx[:, oi * TOWN + h0: oi * TOWN + h1],
                            s=(oi == 0), t_=(oi == KD - 1):
                        e.matmul(o, ones_bf[:, 0:1], r, start=s, stop=t_),
                     waits=[t_r1, tok_ones] if (oi == 0 and h0 == 0) else [t_r1])
                tok_stats_done = P.pe(
                    lambda e, o=ps_sq[0:1, h0:h1], r=r1sq[oi % 2][:, h0:h1],
                           s=(oi == 0), t_=(oi == KD - 1):
                        e.matmul(o, ones_bf[:, 0:1], r, start=s, stop=t_),
                    waits=[t_sqa], inc=(h1 == TOWN))
            g_sq[oi % 2].use(tok_stats_done)
            g_wo[oi % 2].use(("p", P.cnt["p"]))
            if oi + 2 < KD:
                load_wo(oi + 2)

        tok_dbgr1 = None
        if dbg_r1 is not None:
            tok_dbgr1 = P.stream("gpsimd", 1, "dbgR1").load(
                [(dbg_r1[oi], rx[:, oi * TOWN:(oi + 1) * TOWN], {}) for oi in range(KD)],
                waits=[tok_r1[KD - 1]])

        if dbg_ln1 is not None:
            pssum_dump = SB(stC, "pssum_dump", [1, 2 * TOWN], F32)
            t_pd = P.ve(lambda e: e.tensor_copy(pssum_dump[0:1, 0:TOWN], ps_sum[0:1, 0:TOWN]),
                        waits=[tok_stats_done])
            t_pd2 = P.ve(lambda e: e.tensor_copy(pssum_dump[0:1, TOWN:2 * TOWN], ps_sq[0:1, 0:TOWN]))
            P.stream("sync", 1, "dbgPS").load([(dbg_ps, pssum_dump[:], {})], waits=[t_pd2])
        t_mean = P.ve(lambda e: e.tensor_scalar(out=rvec(0), in0=ps_sum[0:1, 0:TOWN],
                                                scalar1=1.0 / D, scalar2=None, op0=ALU.mult),
                      waits=[tok_stats_done])
        t_msq = P.ve(lambda e: e.tensor_scalar(out=rvec(1), in0=ps_sq[0:1, 0:TOWN],
                                               scalar1=1.0 / D, scalar2=None, op0=ALU.mult))
        t_m2 = P.ve(lambda e: e.tensor_tensor(out=rvec(2), in0=rvec(0), in1=rvec(0), op=ALU.mult))
        t_var = P.ve(lambda e: e.tensor_tensor(out=rvec(3), in0=rvec(1), in1=rvec(2),
                                               op=ALU.subtract))
        t_ln = P.sc(lambda e: e.activation(out=rvec(1), in_=rvec(3), func=AF.Ln, bias=const_sb[0:1, 416:417]),
                    waits=[t_var])
        t_rstd = P.sc(lambda e: e.activation(out=rvec(2), in_=rvec(1), func=AF.Exp, scale=-0.5))
        t_mr = P.ve(lambda e: e.tensor_tensor(out=rvec(1), in0=rvec(0), in1=rvec(2), op=ALU.mult),
                    waits=[t_rstd])
        t_bc = None
        for (h0, h1) in HALVES:
            P.pe(lambda e, o=ps_a[0][:, h0:h1], r=rvec(2, h0, h1):
                    e.matmul(o, onesr[0:1, :], r, start=True, stop=True),
                 waits=[t_mr, tok_onesr] if h0 == 0 else ())
            t_bc = P.pe(lambda e, o=ps_a[1][:, h0:h1], r=rvec(1, h0, h1):
                           e.matmul(o, onesr[0:1, :], r, start=True, stop=True),
                        waits=(), inc=(h1 == TOWN))
        t_bc1 = P.ve(lambda e: e.tensor_copy(bc_rstd[:], ps_a[0][:, 0:TOWN]), waits=[t_bc])
        t_bc2 = P.ve(lambda e: e.tensor_copy(bc_mr[:], ps_a[1][:, 0:TOWN]))
        if dbg_ln1 is not None:
            P.stream("sync", 1, "dbgL").load(
                [(dbg_ln1, rstat1[0:1, :], {}), (dbg_bc[0], bc_rstd[:], {}),
                 (dbg_bc[1], bc_mr[:], {})], waits=[t_bc2])
        for oi in range(KD):
            t_t = P.ve(lambda e, i=rx[:, oi * TOWN:(oi + 1) * TOWN]:
                          e.tensor_tensor(out=tmpb[:], in0=i, in1=bc_rstd[:], op=ALU.mult),
                       waits=[tok_dbgr1] if (oi == 0 and tok_dbgr1) else ())
            tok_x1[oi] = P.ve(lambda e, o=rx[:, oi * TOWN:(oi + 1) * TOWN]:
                                 e.tensor_tensor(out=o, in0=tmpb[:], in1=bc_mr[:],
                                                 op=ALU.subtract))
        if dbg_x1 is not None:
            tok_dbgx1 = P.stream("gpsimd", 1, "dbgX1").load(
                [(dbg_x1[oi], rx[:, oi * TOWN:(oi + 1) * TOWN], {}) for oi in range(KD)],
                waits=[tok_x1[KD - 1]])
        fence = P.fence()
        stC.close()

    st_h = ExitStack()
    HRES = cfg.HRES
    h_sb = SB(st_h, "h_sb", [128, max(HRES, 1) * TOWN], BF16) \
        if ("F1" in cfg.phases or "F2" in cfg.phases) else None
    tok_h = [None] * NF

    if "F1" in cfg.phases:
        stF1 = ExitStack()
        wg_bf = [SB(stF1, f"wg_bf{b}", [128, KD * 128], BF16) for b in range(2)]
        wu_bf = [SB(stF1, f"wu_bf{b}", [128, KD * 128], BF16) for b in range(2)]
        gsil = SB(stF1, "gsil", [128, TOWN], BF16)
        hsp = [SB(stF1, f"hsp{b}", [128, TOWN], BF16) for b in range(2)]
        ps_g = [PS(stF1, f"ps_g{b}", [128, TOWN]) for b in range(2)]
        ps_u = [PS(stF1, f"ps_u{b}", [128, TOWN]) for b in range(2)]

        g_wg = [Guard(), Guard()]
        g_wu = [Guard(), Guard()]
        g_psg = [Guard(), Guard()]
        g_psu = [Guard(), Guard()]
        g_hsp = [Guard(), Guard()]
        g_gsil = Guard()

        stWGU = P.stream("gpsimd", 2, "dWGU")
        stHsp = P.stream("sync", 2, "sH")
        tok_wgu = [None] * NF

        def load_wgu(f, first=False):
            ext = g_wg[f % 2].take() + g_wu[f % 2].take() + (list(fence) if first else [])
            load_slab(stWGU, f, wg_bf[f % 2], wgT[f], KD, 128, extra_waits=ext)
            tok_wgu[f] = load_slab(stWGU, f, wu_bf[f % 2], wuT[f], KD, 128)

        load_wgu(0, first=True)
        if NF > 1:
            load_wgu(1)

        for f in range(NF):
            bw = f % 2
            pg, pu = ps_g[f % 2], ps_u[f % 2]
            tok_gmm = None
            tok_umm = None
            for ki in range(KD):
                for (h0, h1) in HALVES:
                    w = []
                    if ki == 0 and h0 == 0:
                        w = [tok_wgu[f]] + g_psg[f % 2].take()
                        if tok_x1[KD - 1]:
                            w.append(tok_x1[KD - 1])
                    tok_gmm = P.pe(
                        lambda e, o=pg[:, h0:h1],
                               l=wg_bf[bw][:, ki * 128:(ki + 1) * 128],
                               r=rx[:, ki * TOWN + h0: ki * TOWN + h1],
                               s=(ki == 0), t_=(ki == KD - 1):
                            e.matmul(o, l, r, start=s, stop=t_),
                        waits=w, inc=(ki == KD - 1 and h1 == TOWN))
            for ki in range(KD):
                for (h0, h1) in HALVES:
                    w = g_psu[f % 2].take() if (ki == 0 and h0 == 0) else []
                    tok_umm = P.pe(
                        lambda e, o=pu[:, h0:h1],
                               l=wu_bf[bw][:, ki * 128:(ki + 1) * 128],
                               r=rx[:, ki * TOWN + h0: ki * TOWN + h1],
                               s=(ki == 0), t_=(ki == KD - 1):
                            e.matmul(o, l, r, start=s, stop=t_),
                        waits=w, inc=(ki == KD - 1 and h1 == TOWN))
            t_sil = P.sc(lambda e, i=pg[:, 0:TOWN]: e.activation(out=gsil[:], in_=i, func=AF.Silu),
                         waits=[tok_gmm] + g_gsil.take())
            g_psg[f % 2].use(t_sil)
            if f < HRES:
                t_h = P.ve(lambda e, o=h_sb[:, f * TOWN:(f + 1) * TOWN], i=pu[:, 0:TOWN]:
                              e.tensor_tensor(out=o, in0=gsil[:], in1=i, op=ALU.mult),
                           waits=[tok_umm, t_sil])
                tok_h[f] = t_h
            else:
                bs = f % 2
                t_h = P.ve(lambda e, o=hsp[bs][:], i=pu[:, 0:TOWN]:
                              e.tensor_tensor(out=o, in0=gsil[:], in1=i, op=ALU.mult),
                           waits=[tok_umm, t_sil] + g_hsp[bs].take())
                tok_h[f] = stHsp.load([(h_scr[f - HRES], hsp[bs][:], {})],
                                      waits=[t_h], idx=f)
                g_hsp[bs].use(tok_h[f])
            g_gsil.use(t_h)
            g_psu[f % 2].use(t_h)
            g_wg[f % 2].use(("p", P.cnt["p"]))
            g_wu[f % 2].use(("p", P.cnt["p"]))
            if f + 2 < NF:
                load_wgu(f + 2)

        if dbg_h is not None:
            stDH = P.stream("gpsimd", 1, "dbgH")
            stDH.load([(dbg_h[f], h_sb[:, f * TOWN:(f + 1) * TOWN], {})
                       for f in range(HRES)] +
                      [(dbg_h[f], h_scr[f - HRES], {}) for f in range(HRES, NF)],
                      waits=[t for t in tok_h if t is not None])
        fence = P.fence()
        stF1.close()

    if "F2" in cfg.phases:
        stF2 = ExitStack()
        wd_bf = [SB(stF2, f"wd_bf{b}", [128, 4 * 256], BF16) for b in range(2)]
        hst = [SB(stF2, f"hst{b}", [128, TOWN], BF16) for b in range(4)]
        r2sq = [SB(stF2, f"r2sq{b}", [128, TOWN], BF16) for b in range(2)]
        bc2_rstd = SB(stF2, "bc2_rstd", [128, TOWN], F32)
        bc2_mr = SB(stF2, "bc2_mr", [128, TOWN], F32)
        tmp2 = SB(stF2, "tmp2", [128, TOWN], F32)
        outf = [SB(stF2, f"outf{b}", [128, TOWN], F32) for b in range(2)]

        rstat2 = SB(stF2, "rstat2", [32, 4 * TOWN], F32)
        rvec2 = mk_rvec(rstat2)
        P.ve(lambda e: e.memset(rstat2[:], 0.0), waits=fence)
        ps_y = [PS(stF2, f"ps_y{b}", [128, TOWN]) for b in range(2)]
        ps2_sum = PS(stF2, "ps2_sum", [32, max(TOWN, 512)])
        ps2_sq = PS(stF2, "ps2_sq", [32, max(TOWN, 512)])

        stY = P.stream("sync", 2, "sY")
        g_wd = [Guard(), Guard()]
        g_hst = [Guard() for _ in range(4)]
        g_psy = [Guard(), Guard()]
        g_sq2 = [Guard(), Guard()]
        g_outf = [Guard(), Guard()]

        stWD = P.stream("gpsimd", 2, "dWD")
        stHL = P.stream("sync", 4, "dHL")
        tok_wd = {}
        tok_hload = {}

        def load_wd(dg, fb, first=False):
            f0, f1 = 4 * fb, min(NF, 4 * fb + 4)
            tok_wd[(dg, fb)] = stWD.load(
                [(wd_bf[fb % 2].ap().rearrange("p (f m) -> p f m", f=4)[:, 0:f1 - f0, :],
                  wdT[dg, f0:f1].rearrange("f p m -> p f m"), {})],
                waits=g_wd[fb % 2].take() + (list(fence) if first else []), idx=fb)

        hfirst = [True] * 4
        def load_h(dg, f):
            if f < HRES:
                return
            bs = f % 4
            w = g_hst[bs].take() + [tok_h[f]]
            if hfirst[bs]:
                hfirst[bs] = False
                w = w + list(fence)
            tok_hload[(dg, f)] = stHL.load([(hst[bs][:], h_scr[f - HRES], {})],
                                           waits=w, idx=f)

        PRE = 4
        NFB = (NF + 3) // 4
        tok_r2 = [None] * KD
        tok_stats2 = None
        for dg in range(cfg.NDG):
            d0 = 2 * dg
            nd = min(2, KD - d0)
            load_wd(dg, 0, first=(dg == 0))
            if NFB > 1:
                load_wd(dg, 1)
            for f in range(min(PRE, NF)):
                load_h(dg, f)
            tok_dmm = None
            for f in range(NF):
                bw = (f // 4) % 2
                if f < HRES:
                    hsrc = h_sb.ap()[:, f * TOWN:(f + 1) * TOWN]
                    tok_hs = tok_h[f]
                else:
                    hsrc = hst[f % 4][:]
                    tok_hs = tok_hload[(dg, f)]
                for d in range(nd):
                    for (h0, h1) in HALVES:
                        w = []
                        if d == 0 and h0 == 0:
                            w = [tok_wd[(dg, f // 4)], tok_hs]
                            if f == 0:
                                for dd in range(nd):
                                    w += g_psy[dd].take()
                        tok_dmm = P.pe(
                            lambda e, o=ps_y[d][:, h0:h1],
                                   l=wd_bf[bw][:, (f % 4) * 256 + d * 128:
                                               (f % 4) * 256 + (d + 1) * 128],
                                   r=hsrc[:, h0:h1],
                                   s=(f == 0), t_=(f == NF - 1):
                                e.matmul(o, l, r, start=s, stop=t_),
                            waits=w, inc=(d == nd - 1 and h1 == TOWN))
                if f >= HRES:
                    g_hst[f % 4].use(tok_dmm)
                if f % 4 == 3 or f == NF - 1:
                    g_wd[(f // 4) % 2].use(tok_dmm)
                    if f // 4 + 2 < NFB:
                        load_wd(dg, f // 4 + 2)
                if f + PRE < NF:
                    load_h(dg, f + PRE)
            for d in range(nd):
                oi = d0 + d
                t_r2 = P.ve(lambda e, o=rx[:, oi * TOWN:(oi + 1) * TOWN], i=ps_y[d][:, 0:TOWN]:
                               e.scalar_tensor_tensor(out=o, in0=i, scalar=1.0, in1=o,
                                                      op0=ALU.mult, op1=ALU.add),
                            waits=[tok_dmm, tok_x1[oi] if tok_x1[oi] else None])
                tok_r2[oi] = t_r2
                g_psy[d].use(t_r2)
                t_sq = P.sc(lambda e, o=r2sq[d][:], i=rx[:, oi * TOWN:(oi + 1) * TOWN]:
                               e.activation(out=o, in_=i, func=AF.Square),
                            waits=[t_r2] + g_sq2[d].take())
                for (h0, h1) in HALVES:
                    P.pe(lambda e, o=ps2_sum[0:1, h0:h1],
                                r=rx[:, oi * TOWN + h0: oi * TOWN + h1],
                                s=(oi == 0), t_=(oi == KD - 1):
                            e.matmul(o, ones_bf[:, 0:1], r, start=s, stop=t_),
                         waits=[t_r2, tok_ones] if (oi == 0 and h0 == 0) else [t_r2])
                    tok_stats2 = P.pe(
                        lambda e, o=ps2_sq[0:1, h0:h1], r=r2sq[d][:, h0:h1],
                               s=(oi == 0), t_=(oi == KD - 1):
                            e.matmul(o, ones_bf[:, 0:1], r, start=s, stop=t_),
                        waits=[t_sq], inc=(h1 == TOWN))
                g_sq2[d].use(tok_stats2)

        t_mean = P.ve(lambda e: e.tensor_scalar(out=rvec2(0), in0=ps2_sum[0:1, 0:TOWN],
                                                scalar1=1.0 / D, scalar2=None, op0=ALU.mult),
                      waits=[tok_stats2])
        t_msq = P.ve(lambda e: e.tensor_scalar(out=rvec2(1), in0=ps2_sq[0:1, 0:TOWN],
                                               scalar1=1.0 / D, scalar2=None, op0=ALU.mult))
        t_m2 = P.ve(lambda e: e.tensor_tensor(out=rvec2(2), in0=rvec2(0), in1=rvec2(0), op=ALU.mult))
        t_var = P.ve(lambda e: e.tensor_tensor(out=rvec2(3), in0=rvec2(1), in1=rvec2(2),
                                               op=ALU.subtract))
        t_ln = P.sc(lambda e: e.activation(out=rvec2(1), in_=rvec2(3), func=AF.Ln, bias=const_sb[0:1, 416:417]),
                    waits=[t_var])
        t_rstd = P.sc(lambda e: e.activation(out=rvec2(2), in_=rvec2(1), func=AF.Exp, scale=-0.5))
        t_mr = P.ve(lambda e: e.tensor_tensor(out=rvec2(1), in0=rvec2(0), in1=rvec2(2), op=ALU.mult),
                    waits=[t_rstd])
        t_bc = None
        for (h0, h1) in HALVES:
            P.pe(lambda e, o=ps_y[0][:, h0:h1], r=rvec2(2, h0, h1):
                    e.matmul(o, onesr[0:1, :], r, start=True, stop=True),
                 waits=([t_mr, tok_onesr] + g_psy[0].take() + g_psy[1].take()) if h0 == 0 else ())
            t_bc = P.pe(lambda e, o=ps_y[1][:, h0:h1], r=rvec2(1, h0, h1):
                           e.matmul(o, onesr[0:1, :], r, start=True, stop=True),
                        waits=(), inc=(h1 == TOWN))
        t_bc1 = P.ve(lambda e: e.tensor_copy(bc2_rstd[:], ps_y[0][:, 0:TOWN]), waits=[t_bc])
        t_bc2 = P.ve(lambda e: e.tensor_copy(bc2_mr[:], ps_y[1][:, 0:TOWN]))
        for oi in range(KD):
            bo = oi % 2
            t_t = P.ve(lambda e, i=rx[:, oi * TOWN:(oi + 1) * TOWN]:
                          e.tensor_tensor(out=tmp2[:], in0=i, in1=bc2_rstd[:], op=ALU.mult),
                       waits=[tok_r2[oi]] if tok_r2[oi] else ())
            t_o = P.ve(lambda e, o=outf[bo][:]:
                          e.tensor_tensor(out=o, in0=tmp2[:], in1=bc2_mr[:], op=ALU.subtract),
                       waits=g_outf[bo].take())
            g_outf[bo].use(stY.load([(y_own[oi], outf[bo][:], {})], waits=[t_o], idx=oi))
        stF2.close()

    st_h.close()
    st_rx.close()

    if dbg_hend is not None and "F1" in cfg.phases:
        P.stream("gpsimd", 1, "dbgHE").load(
            [(dbg_hend[f], h_sb[:, f * TOWN:(f + 1) * TOWN], {}) for f in range(cfg.HRES)] +
            [(dbg_hend[f], h_scr[f - cfg.HRES], {}) for f in range(cfg.HRES, NF)],
            waits=P.fence())

    P.waits("sync", [(k, P.cnt[k]) for k in P.dma_sems if P.cnt[k]])
    P.waits("gpsimd", [(k, P.cnt[k]) for k in P.dma_sems if P.cnt[k]])

    with ExitStack() as es:
        sems = {}
        for s in list(P.cnt):
            if P.cnt[s]:
                sems[s] = es.enter_context(nc.semaphore("sem_" + s))
        blk = es.enter_context(nc.Block())
        for eng_name in ENGINES:
            opl = P.ops[eng_name]
            if not opl:
                continue
            dec = getattr(blk, eng_name)
            def body(eng, opl=opl):
                for fn in opl:
                    fn(eng, sems)
            dec(body)
    st_top.close()
    return nc



def rope_tables(S):
    inv = 1.0 / 10000.0 ** (np.arange(0, HD, 2, dtype=np.float64) / HD)
    pos = np.arange(S, dtype=np.float64)
    fr = np.outer(pos, inv)
    emb = np.concatenate([fr, fr], axis=-1)
    return np.cos(emb), np.sin(emb)


def prep_maps(cfg, inputs):
    S, D, F, KD, NF = cfg.S, cfg.D, cfg.F, cfg.KD, cfg.NF
    TOWN, NJ = cfg.TOWN, cfg.NJ
    x = np.ascontiguousarray(np.asarray(inputs["L0_attn_hidden_states"], np.float32).reshape(S, D))
    Wq = np.asarray(inputs["L0_attn_Wq"], np.float32)
    Wkr = np.asarray(inputs["L0_attn_Wkr"], np.float32)
    Wdk = np.asarray(inputs["L0_attn_Wdk"], np.float32)
    Wupk = np.asarray(inputs["L0_attn_Wupk"], np.float32)
    Wupv = np.asarray(inputs["L0_attn_Wupv"], np.float32)
    Wo = np.asarray(inputs["L0_attn_Wo"], np.float32)
    Wg = np.asarray(inputs["L0_ffn_W_gate"], np.float32)
    Wu = np.asarray(inputs["L0_ffn_W_up"], np.float32)
    Wd = np.asarray(inputs["L0_ffn_W_down"], np.float32)

    xT_full = np.ascontiguousarray(x.T)
    xT = xT_full.reshape(KD, 128, S)

    qperm = np.concatenate([128 * h + PERM for h in range(NH)])
    WqT = np.ascontiguousarray(Wq[qperm].reshape(NH, 128, KD, 128).transpose(0, 2, 3, 1))
    WoT = np.ascontiguousarray(Wo.reshape(KD, 128, NOC, 128).transpose(0, 2, 3, 1))
    wkv = np.concatenate([Wdk, Wkr], axis=0)
    wkvT = np.ascontiguousarray(wkv.T).reshape(KD, 128, CD + RD)
    upkv = np.concatenate([Wupk, Wupv], axis=0)
    upkvT = np.ascontiguousarray(upkv.T).reshape(4, 128, 1536)
    WgT = np.ascontiguousarray(Wg.reshape(NF, 128, KD, 128).transpose(0, 2, 3, 1))
    WuT = np.ascontiguousarray(Wu.reshape(NF, 128, KD, 128).transpose(0, 2, 3, 1))
    WdT = np.ascontiguousarray(Wd.reshape(cfg.NDG, 256, NF, 128).transpose(0, 2, 3, 1))

    cos, sin = rope_tables(S)
    dscale = 1.0 / np.sqrt(HD)
    sgn = np.where(PERM < 64, -1.0, 1.0)
    cosq_p = (cos[:, PERM].T * dscale).astype(np.float32)
    sinmq_p = ((sin[:, PERM] * sgn[None, :]).T * dscale).astype(np.float32)
    rdim = np.array([r if r < 32 else r + 32 for r in range(64)])
    krow = np.concatenate([rdim, rdim])
    ksgn = np.where(krow < 64, -1.0, 1.0)
    cosk_p = np.ascontiguousarray(cos[:, krow].T.astype(np.float32))
    sinmk_p = np.ascontiguousarray((sin[:, krow] * ksgn[None, :]).T.astype(np.float32))

    consts = np.zeros((128, 417), np.float32)
    consts[:, 0:128] = np.eye(128)
    R32 = np.zeros((128, 128), np.float32)
    for i in range(128):
        R32[i ^ 32, i] = 1.0
    consts[:, 128:256] = R32
    consts[0, 256:384] = 1.0
    consts[:, 384] = 1.0
    consts[:, 416] = EPS

    shared = dict(xT=xT, wkvT=wkvT, upkvT=upkvT, wqT=WqT, woT=WoT,
                  wgT=WgT, wuT=WuT, wdT=WdT, consts=consts,
                  cosk=cosk_p, sinmk=sinmk_p)

    in_maps = []
    own_cols_all = []
    for c in range(NCORES):
        cols = np.concatenate([np.arange(128) + 128 * tile_of(c, j) for j in range(NJ)])
        own_cols_all.append(cols)
        x_ownT = np.ascontiguousarray(xT_full[:, cols]).reshape(KD, 128, TOWN)
        cq = np.ascontiguousarray(cosq_p[:, cols])
        sq = np.ascontiguousarray(sinmq_p[:, cols])
        mask = np.zeros((128, 2048), np.float32)
        for par, p in ((0, c), (1, 7 - c)):
            r = np.arange(128)[:, None]
            kk = np.arange(1024)[None, :]
            mask[:, par * 1024:(par + 1) * 1024] = np.where(kk <= 128 * p + r, 0.0, NEG)
        m = dict(shared)
        m.update(x_ownT=x_ownT, cosq=cq, sinmq=sq, maskin=mask)
        in_maps.append(m)
    return in_maps, own_cols_all


def assemble(cfg, results, own_cols_all):
    S, D = cfg.S, cfg.D
    y = np.empty((S, D), np.float32)
    for c in range(NCORES):
        yc = results[c]["y_own"].reshape(D, cfg.TOWN)
        y[own_cols_all[c], :] = yc.T
    return y.reshape(1, S, D)


def kernel(**inputs):
    cfg = Cfg()
    nc = build(cfg)
    in_maps, own_cols = prep_maps(cfg, inputs)
    res = run_bass_kernel_spmd(nc, in_maps, core_ids=list(range(NCORES)))
    return assemble(cfg, res.results, own_cols)


# version 13
# speedup vs baseline: 1.1210x; 1.0961x over previous
import numpy as np
from contextlib import ExitStack
import concourse.bass as bass
import concourse.mybir as mybir
from concourse.bass_utils import run_bass_kernel_spmd

F32 = mybir.dt.float32
BF16 = mybir.dt.bfloat16
AF = mybir.ActivationFunctionType
ALU = mybir.AluOpType
AX = mybir.AxisListType

NCORES = 8
HD = 128
NH = 32
KVH = 8
GR = NH // KVH
CD = 512
RD = 512
NOC = 32
NEG = -60000.0
EPS = 1e-5


class Cfg:
    def __init__(self, S=8192, D=4096, F=11008, hres=36,
                 phases=("A1", "A2", "B", "C", "F1", "F2"), dbg=()):
        self.S, self.D, self.F = S, D, F
        self.KD = D // 128
        self.NTT = S // 512
        self.TOWN = S // NCORES
        self.NJ = self.TOWN // 128
        self.NF = F // 128
        self.HRES = min(hres, self.NF)
        self.NDG = (self.KD + 1) // 2
        self.phases = set(phases)
        self.dbg = set(dbg)
        self.halves = [(h0, min(self.TOWN, h0 + 512)) for h0 in range(0, self.TOWN, 512)]
        assert S % 1024 == 0 and D % 256 == 0 and F % 128 == 0
        assert self.NJ == S // 1024


PERM = np.array(list(range(0, 32)) + list(range(64, 96)) +
                list(range(32, 64)) + list(range(96, 128)))


def tile_of(c, j):
    p = c if j % 2 == 0 else 7 - c
    return 8 * j + p



ENGINES = ("sync", "gpsimd", "scalar", "vector", "tensor")
QSEM = {"sync": "s", "gpsimd": "g", "scalar": "c"}


class Stream:
    def __init__(self, P, queue, nbufs, tag):
        self.P, self.q, self.n = P, queue, nbufs
        self.sems = []
        for i in range(nbufs):
            name = f"{tag}{i}"
            assert name not in P.cnt
            P.cnt[name] = 0
            P.dma_sems.add(name)
            self.sems.append(name)
        self.idx = 0

    def load(self, transfers, waits=(), idx=None):
        P = self.P
        if idx is None:
            idx = self.idx
        self.idx = idx + 1
        sem = self.sems[idx % self.n]
        P.waits(self.q, waits)
        for (out_ap, in_ap, kw) in transfers:
            P.cnt[sem] += 16
            def f(e, SS, out_ap=out_ap, in_ap=in_ap, sem=sem, kw=kw):
                e.dma_start(out=out_ap, in_=in_ap, **kw).then_inc(SS[sem], 16)
            P.raw(self.q, f)
        return (sem, P.cnt[sem])


class Prog:
    def __init__(self):
        self.ops = {e: [] for e in ENGINES}
        self.cnt = {k: 0 for k in ("v", "a", "p")}
        self.dma_sems = set()

    def stream(self, queue, nbufs, tag):
        return Stream(self, queue, nbufs, tag)

    def raw(self, eng, fn):
        self.ops[eng].append(fn)

    def waits(self, eng, toks):
        best = {}
        for t in toks:
            if t is None:
                continue
            s, v = t
            assert v <= self.cnt[s], (s, v, self.cnt[s])
            best[s] = max(best.get(s, 0), v)
        for s, v in sorted(best.items()):
            if v <= 0:
                continue
            def f(e, SS, s=s, v=v):
                e.wait_ge(SS[s], v)
            self.raw(eng, f)

    def fence(self):
        return [(s, c) for s, c in self.cnt.items() if c > 0]

    def _chain(self, eng, sem, fn, waits):
        self.waits(eng, waits)
        prev = self.cnt[sem]
        self.cnt[sem] += 1
        val = self.cnt[sem]
        def f(e, SS, fn=fn, sem=sem, prev=prev):
            if prev:
                e.wait_ge(SS[sem], prev)
            fn(e).then_inc(SS[sem], 1)
        self.raw(eng, f)
        return (sem, val)

    def ve(self, fn, waits=()):
        return self._chain("vector", "v", fn, waits)

    def sc(self, fn, waits=()):
        return self._chain("scalar", "a", fn, waits)

    def pe(self, fn, waits=(), inc=False):
        self.waits("tensor", waits)
        if inc:
            self.cnt["p"] += 1
            val = self.cnt["p"]
            def f(e, SS, fn=fn):
                fn(e).then_inc(SS["p"], 1)
            self.raw("tensor", f)
            return ("p", val)
        def f(e, SS, fn=fn):
            fn(e)
        self.raw("tensor", f)
        return None


class Guard:
    def __init__(self):
        self.toks = []

    def use(self, tok):
        if tok is not None:
            self.toks.append(tok)

    def take(self):
        t, self.toks = self.toks, []
        return t



def build(cfg: Cfg):
    nc = bass.Bass("TRN2", target_bir_lowering=False, debug=False)
    S, D, F, KD, NF = cfg.S, cfg.D, cfg.F, cfg.KD, cfg.NF
    NTT, NJ, TOWN = cfg.NTT, cfg.NJ, cfg.TOWN
    HALVES = cfg.halves
    P = Prog()

    dt = nc.dram_tensor
    xT = dt("xT", [KD, 128, S], F32, kind="ExternalInput").ap()
    x_ownT = dt("x_ownT", [KD, 128, TOWN], F32, kind="ExternalInput").ap()
    wkvT = dt("wkvT", [KD, 128, CD + RD], F32, kind="ExternalInput").ap()
    upkvT = dt("upkvT", [4, 128, 1536], F32, kind="ExternalInput").ap()
    wqT = dt("wqT", [NH, KD, 128, 128], F32, kind="ExternalInput").ap()
    woT = dt("woT", [KD, NOC, 128, 128], F32, kind="ExternalInput").ap()
    wgT = dt("wgT", [NF, KD, 128, 128], F32, kind="ExternalInput").ap()
    wuT = dt("wuT", [NF, KD, 128, 128], F32, kind="ExternalInput").ap()
    wdT = dt("wdT", [cfg.NDG, NF, 128, 256], F32, kind="ExternalInput").ap()
    consts = dt("consts", [128, 417], F32, kind="ExternalInput").ap()
    cosk = dt("cosk", [128, S], F32, kind="ExternalInput").ap()
    sinmk = dt("sinmk", [128, S], F32, kind="ExternalInput").ap()
    cosq = dt("cosq", [128, TOWN], F32, kind="ExternalInput").ap()
    sinmq = dt("sinmq", [128, TOWN], F32, kind="ExternalInput").ap()
    maskin = dt("maskin", [128, 2048], F32, kind="ExternalInput").ap()

    y_own = dt("y_own", [KD, 128, TOWN], F32, kind="ExternalOutput").ap()

    K_scr = dt("K_scr", [KVH, 128, S], BF16, kind="Internal").ap()
    V_scr = dt("V_scr", [S // 128, 128, KVH, 128], BF16, kind="Internal").ap()
    O_scr = dt("O_scr", [NOC, NJ, 128, 128], BF16, kind="Internal").ap()
    NSPILL = NF - cfg.HRES
    h_scr = dt("h_scr", [max(NSPILL, 1), 128, TOWN], BF16, kind="Internal").ap()

    def dbg(name, shape, dtype=F32):
        if name in cfg.dbg:
            return dt("dbg_" + name, shape, dtype, kind="ExternalOutput").ap()
        return None

    dbg_q = dbg("q", [NH, 128, TOWN])
    dbg_k = dbg("k", [KVH, 128, S])
    dbg_v = dbg("v", [S // 128, 128, KVH, 128])
    dbg_o = dbg("o", [NOC, NJ, 128, 128])
    dbg_r1 = dbg("r1", [KD, 128, TOWN])
    dbg_ln1 = dbg("ln1", [1, 4 * TOWN])
    dbg_ps = dbg("ps", [1, 2 * TOWN])
    dbg_bc = dbg("bc", [2, 128, TOWN])
    dbg_x1 = dbg("x1", [KD, 128, TOWN])
    dbg_h = dbg("h", [NF, 128, TOWN])
    dbg_hend = dbg("hend", [NF, 128, TOWN])

    def SB(stk, name, shape, dtype):
        return stk.enter_context(nc.sbuf_tensor(name, shape, dtype))

    def PS(stk, name, shape, dtype=F32):
        esz = 4 if dtype == F32 else 2
        per = int(np.prod(shape[1:])) * esz
        pad = (-per) % 2048
        shape = list(shape)
        shape[-1] = shape[-1] + pad // (esz * int(np.prod(shape[1:-1])))
        return stk.enter_context(nc.psum_tensor(name, shape, dtype))

    st_top = ExitStack()

    const_sb = SB(st_top, "const_sb", [128, 417], F32)
    ident_bf = SB(st_top, "ident_bf", [128, 128], BF16)
    r32_bf = SB(st_top, "r32_bf", [128, 128], BF16)
    ones_bf = SB(st_top, "ones_bf", [128, 32], BF16)
    onesr = SB(st_top, "onesr", [32, 128], F32)
    maskT_sb = SB(st_top, "maskT_sb", [128, 2048], BF16)
    cosq_sb = SB(st_top, "cosq_sb", [128, TOWN], F32)
    sinmq_sb = SB(st_top, "sinmq_sb", [128, TOWN], F32)

    stCS = P.stream("sync", 1, "dCS")
    stCG = P.stream("gpsimd", 1, "dCG")
    tok_cs = stCS.load([(const_sb[:], consts, {}), (onesr[:], consts[0:32, 256:384], {}),
                        (cosq_sb[:], cosq, {}), (sinmq_sb[:], sinmq, {})])
    tok_cg = stCG.load([(ident_bf[:], consts[:, 0:128], {}),
                        (r32_bf[:], consts[:, 128:256], {}),
                        (ones_bf[:], consts[:, 384:416], {}),
                        (maskT_sb[:], maskin, {})])
    tok_const = tok_onesr = tok_cq = tok_sq = tok_cs
    tok_ident = tok_r32 = tok_ones = tok_mask = tok_cg

    def load_slab(stream, idx, dst, src3d, kd, width, extra_waits=()):
        ngrp = max(1, (kd * 128) // 1024)
        ngrp = min(ngrp, kd)
        per = (kd + ngrp - 1) // ngrp
        tr = []
        for gi in range(ngrp):
            k0, k1 = gi * per, min(kd, (gi + 1) * per)
            if k0 >= k1:
                break
            tr.append((dst.ap().rearrange("p (k m) -> p k m", k=kd)[:, k0:k1, :],
                       src3d[k0:k1].rearrange("k p m -> p k m"), {}))
        return stream.load(tr, waits=extra_waits, idx=idx)

    tok_q_head = [None] * NH
    tok_A1_stores = []
    fence = []

    if "A1" in cfg.phases:
        stA = ExitStack()
        CW = CD + RD
        wkv_sb = SB(stA, "wkv_sb", [128, KD * CW], BF16)
        upkv_sb = SB(stA, "upkv_sb", [128, 4 * 1536], BF16)
        x_bf = [SB(stA, f"x_bf{b}", [128, KD * 512], BF16) for b in range(2)]
        ckv_bf = [SB(stA, f"ckv_bf{b}", [128, 4 * 512], BF16) for b in range(2)]
        ckt = [SB(stA, f"ckt{b}", [128, 1024], F32) for b in range(2)]
        kr_bf = [SB(stA, f"kr_bf{b}", [128, 512], BF16) for b in range(2)]
        kt1 = SB(stA, "kt1", [128, 512], F32)
        kt2 = SB(stA, "kt2", [128, 512], F32)
        kA = SB(stA, "kA", [128, 512], BF16)
        kc_bf = SB(stA, "kc_bf", [128, 512], BF16)
        vt_bf = SB(stA, "vt_bf", [128, 512], BF16)

        ps_ckv = [PS(stA, f"ps_ckv{b}", [128, 512]) for b in range(2)]
        ps_kr = [PS(stA, f"ps_kr{b}", [128, 512]) for b in range(2)]
        ps_rot = PS(stA, "ps_rot", [128, 512])
        ps_kc = PS(stA, "ps_kc", [128, 512])
        ps_v = [PS(stA, f"ps_v{b}", [128, 512]) for b in range(2)]

        g_xbf = [Guard(), Guard()]
        g_ckt = [Guard(), Guard()]
        g_psckv = [Guard(), Guard()]
        g_pskr = [Guard(), Guard()]
        g_psrot = Guard()
        g_pskc = Guard()
        g_psv = [Guard(), Guard()]
        g_ckvbf = [Guard(), Guard()]
        g_krbf = [Guard(), Guard()]
        g_kA = Guard()
        g_kcbf = Guard()
        g_vtbf = Guard()

        stWV = P.stream("gpsimd", 1, "dWV")
        stX = P.stream("gpsimd", 2, "dX")
        stCK = P.stream("sync", 2, "dCK")
        stKA = P.stream("sync", 1, "sKA")
        stKC = P.stream("sync", 1, "sKC")
        stVst = P.stream("sync", 1, "sVs")

        load_slab(stWV, 0, wkv_sb, wkvT, KD, CW)
        tok_wkv = tok_upkv = load_slab(stWV, 0, upkv_sb, upkvT, 4, 1536)

        tok_x = [None] * NTT
        tok_ck = [None] * NTT
        tok_ckv_done = [None] * NTT

        def load_tile(tt):
            b = tt % 2
            ngrp = max(1, KD // 8)
            per = (KD + ngrp - 1) // ngrp
            tr = []
            for gi in range(ngrp):
                k0, k1 = gi * per, min(KD, (gi + 1) * per)
                tr.append((x_bf[b].ap().rearrange("p (k t) -> p k t", k=KD)[:, k0:k1, :],
                           xT[k0:k1, :, tt * 512:(tt + 1) * 512].rearrange("k p t -> p k t"),
                           {}))
            tok_x[tt] = stX.load(tr, waits=g_xbf[b].take(), idx=tt)
            tok_ck[tt] = stCK.load(
                [(ckt[b][:, 0:512], cosk[:, tt * 512:(tt + 1) * 512], {}),
                 (ckt[b][:, 512:1024], sinmk[:, tt * 512:(tt + 1) * 512], {})],
                waits=g_ckt[b].take(), idx=tt)

        load_tile(0)
        if NTT > 1:
            load_tile(1)

        rope_state = {}

        def rope_tail(tt, ri):
            b = tt % 2
            bk = ri % 2
            tok_mm = rope_state.pop((tt, ri))
            t_copy = P.ve(lambda e, o=kr_bf[bk][:], i2=ps_kr[bk][:]: e.tensor_copy(o, i2),
                          waits=[tok_mm] + g_krbf[bk].take())
            g_pskr[bk].use(t_copy)
            t_rot = P.pe(lambda e, r=kr_bf[bk][:]: e.matmul(ps_rot[:], r32_bf[:], r,
                                                            start=True, stop=True),
                         waits=[t_copy, tok_r32] + g_psrot.take(), inc=True)
            t_1 = P.ve(lambda e, c1=ckt[b][:, 0:512], r=kr_bf[bk][:]: e.tensor_tensor(
                           out=kt1[:], in0=r, in1=c1, op=ALU.mult),
                       waits=[tok_ck[tt]])
            t_2 = P.ve(lambda e, c2=ckt[b][:, 512:1024]: e.tensor_tensor(
                           out=kt2[:], in0=ps_rot[:], in1=c2, op=ALU.mult),
                       waits=[t_rot])
            g_psrot.use(t_2)
            g_krbf[bk].use(t_2)
            t_A = P.ve(lambda e: e.tensor_tensor(out=kA[:], in0=kt1[:], in1=kt2[:],
                                                 op=ALU.add),
                       waits=g_kA.take())
            g_ckt[b].use(t_1)
            g_ckt[b].use(t_2)
            g_kA.use(stKA.load(
                [(K_scr[2 * ri + hh, 0:64, slice(tt * 512, (tt + 1) * 512)],
                  kA[hh * 64:(hh + 1) * 64, :], {}) for hh in range(2)], waits=[t_A]))

        def emit_proj(tt):
            b = tt % 2
            xa = x_bf[b].ap().rearrange("p (k t) -> p k t", k=KD)
            t_ev = None
            for mi in range(4):
                pp = ps_ckv[mi % 2]
                tok_mm = None
                for ki in range(KD):
                    w = ([tok_x[tt], tok_wkv] + g_psckv[mi % 2].take()) if ki == 0 else []
                    tok_mm = P.pe(
                        lambda e, o=pp[:], l=wkv_sb[:, ki * CW + mi * 128: ki * CW + mi * 128 + 128],
                               r=xa[:, ki, :], s=(ki == 0), t_=(ki == KD - 1):
                            e.matmul(o, l, r, start=s, stop=t_),
                        waits=w, inc=(ki == KD - 1))
                t_ev = P.ve(lambda e, o=ckv_bf[b][:, mi * 512:(mi + 1) * 512], i2=pp[:]:
                               e.tensor_copy(o, i2),
                            waits=[tok_mm] + (g_ckvbf[b].take() if mi == 0 else []))
                g_psckv[mi % 2].use(t_ev)
            tok_ckv_done[tt] = t_ev
            for ri in range(RD // 128):
                bk = ri % 2
                tok_mm = None
                for ki in range(KD):
                    w = ([tok_x[tt]] + g_pskr[bk].take()) if ki == 0 else []
                    tok_mm = P.pe(
                        lambda e, o=ps_kr[bk][:],
                               l=wkv_sb[:, ki * CW + CD + ri * 128: ki * CW + CD + ri * 128 + 128],
                               r=xa[:, ki, :], s=(ki == 0), t_=(ki == KD - 1):
                            e.matmul(o, l, r, start=s, stop=t_),
                        waits=w, inc=(ki == KD - 1))
                rope_state[(tt, ri)] = tok_mm
                if ri > 0:
                    rope_tail(tt, ri - 1)
            g_xbf[b].use(("p", P.cnt["p"]))
            rope_tail(tt, RD // 128 - 1)

        def emit_upkv(tt):
            b = tt % 2
            cb = ckv_bf[b]
            tcol = slice(tt * 512, (tt + 1) * 512)

            def kc_group(ri):
                tok_mm = None
                for ki in range(4):
                    w = ([tok_ckv_done[tt], tok_upkv] + g_pskc.take()) if ki == 0 else []
                    tok_mm = P.pe(
                        lambda e, o=ps_kc[:],
                               l=upkv_sb[:, ki * 1536 + ri * 128: ki * 1536 + ri * 128 + 128],
                               r=cb[:, ki * 512:(ki + 1) * 512], s=(ki == 0), t_=(ki == 3):
                            e.matmul(o, l, r, start=s, stop=t_),
                        waits=w, inc=(ki == 3))
                t_kc = P.ve(lambda e: e.tensor_copy(kc_bf[:], ps_kc[:]),
                            waits=[tok_mm] + g_kcbf.take())
                g_pskc.use(t_kc)
                g_kcbf.use(stKC.load(
                    [(K_scr[2 * ri + hh, 64:128, tcol], kc_bf[hh * 64:(hh + 1) * 64, :], {})
                     for hh in range(2)], waits=[t_kc]))

            def v_group(tc, vh):
                gl = tt * 4 + tc
                pv = ps_v[vh]
                tok_mm = None
                for ki in range(4):
                    w = ([tok_ckv_done[tt]] + g_psv[vh].take()) if ki == 0 else []
                    tok_mm = P.pe(
                        lambda e, o=pv[:],
                               l=cb[:, ki * 512 + tc * 128: ki * 512 + tc * 128 + 128],
                               r=upkv_sb[:, ki * 1536 + 512 + vh * 512: ki * 1536 + 512 + (vh + 1) * 512],
                               s=(ki == 0), t_=(ki == 3):
                            e.matmul(o, l, r, start=s, stop=t_),
                        waits=w, inc=(ki == 3))
                t_v = P.ve(lambda e, i2=pv[:]: e.tensor_copy(vt_bf[:], i2),
                           waits=[tok_mm] + g_vtbf.take())
                g_psv[vh].use(t_v)
                g_vtbf.use(stVst.load(
                    [(V_scr[gl, :, 4 * vh:4 * vh + 4, :],
                      vt_bf.ap().rearrange("p (h d) -> p h d", h=4), {})],
                    waits=[t_v]))

            for tc in range(4):
                kc_group(tc)
                v_group(tc, 0)
                v_group(tc, 1)
            g_ckvbf[b].use(("p", P.cnt["p"]))

        emit_proj(0)
        for tt in range(NTT):
            if tt + 1 < NTT:
                emit_proj(tt + 1)
            if tt + 2 < NTT:
                load_tile(tt + 2)
            emit_upkv(tt)

        tok_A1_stores = [(nm, P.cnt[nm]) for nm in ("sKA0", "sKC0", "sVs0")]
        if dbg_k is not None:
            P.stream("gpsimd", 1, "dbgK").load([(dbg_k, K_scr, {})], waits=tok_A1_stores)
        if dbg_v is not None:
            P.stream("gpsimd", 1, "dbgV").load([(dbg_v, V_scr, {})], waits=tok_A1_stores)
        fence = P.fence()
        stA.close()

    st_q = ExitStack()
    q_all = SB(st_q, "q_all", [128, NH * TOWN], BF16) \
        if ("A2" in cfg.phases or "B" in cfg.phases) else None
    qmax_bc = SB(st_q, "qmax_bc", [128, NH * NJ], F32) \
        if ("A2" in cfg.phases or "B" in cfg.phases) else None
    tok_qmax = None

    if "A2" in cfg.phases:
        stA2 = ExitStack()
        x_own_bf = SB(stA2, "x_own_bf", [128, KD * TOWN], BF16)
        wq_bf = [SB(stA2, f"wq_bf{b}", [128, KD * 128], BF16) for b in range(2)]
        qraw = [SB(stA2, f"qraw{b}", [128, TOWN], BF16) for b in range(2)]
        qt1 = SB(stA2, "qt1", [128, TOWN], F32)
        qt2 = SB(stA2, "qt2", [128, TOWN], F32)
        qsq2 = [SB(stA2, f"qsq2{b}", [128, TOWN], BF16) for b in range(2)]
        qsqmax = SB(stA2, "qsqmax", [1, NH * NJ], F32)
        ps_q = [PS(stA2, f"ps_q{b}", [128, TOWN]) for b in range(2)]
        ps_qrot = PS(stA2, "ps_qrot", [128, TOWN])
        ps_qn = PS(stA2, "ps_qn", [32, TOWN])

        g_wq = [Guard(), Guard()]
        g_psq = [Guard(), Guard()]
        g_psqrot = Guard()
        g_qraw = [Guard(), Guard()]
        g_qsq2 = [Guard(), Guard()]
        g_psqn = Guard()

        stXO = P.stream("gpsimd", 1, "dXO")
        stWQ = P.stream("gpsimd", 2, "dWQ")
        tok_xo = stXO.load([(x_own_bf[:, ki * TOWN:(ki + 1) * TOWN], x_ownT[ki], {})
                            for ki in range(KD)], waits=fence)

        tok_wq = [None] * NH

        def load_wq(hi):
            tok_wq[hi] = load_slab(stWQ, hi, wq_bf[hi % 2], wqT[hi], KD, 128,
                                   extra_waits=g_wq[hi % 2].take())

        load_wq(0)
        if NH > 1:
            load_wq(1)

        q_state = {}

        def q_mms(hi):
            bw = hi % 2
            pq = ps_q[hi % 2]
            tok_mm = None
            for ki in range(KD):
                for (h0, h1) in HALVES:
                    w = ([tok_xo, tok_wq[hi]] + g_psq[hi % 2].take()) \
                        if (ki == 0 and h0 == 0) else []
                    tok_mm = P.pe(
                        lambda e, o=pq[:, h0:h1],
                               l=wq_bf[bw][:, ki * 128:(ki + 1) * 128],
                               r=x_own_bf[:, ki * TOWN + h0: ki * TOWN + h1],
                               s=(ki == 0), t_=(ki == KD - 1):
                            e.matmul(o, l, r, start=s, stop=t_),
                        waits=w, inc=(ki == KD - 1 and h1 == TOWN))
            q_state[hi] = tok_mm
            g_wq[hi % 2].use(("p", P.cnt["p"]))
            if hi + 2 < NH:
                load_wq(hi + 2)

        def q_rope(hi):
            bq = hi % 2
            pq = ps_q[bq]
            tok_mm = q_state.pop(hi)
            t_copy = P.ve(lambda e, o=qraw[bq][:], i2=pq[:, 0:TOWN]: e.tensor_copy(o, i2),
                          waits=[tok_mm] + g_qraw[bq].take())
            g_psq[bq].use(t_copy)
            t_rot = None
            for (h0, h1) in HALVES:
                t_rot = P.pe(lambda e, o=ps_qrot[:, h0:h1], r=qraw[bq][:, h0:h1]:
                                e.matmul(o, r32_bf[:], r, start=True, stop=True),
                             waits=([t_copy, tok_r32] + g_psqrot.take()) if h0 == 0 else (),
                             inc=(h1 == TOWN))
            t_1 = P.ve(lambda e, r=qraw[bq][:]: e.tensor_tensor(out=qt1[:], in0=r,
                                                                in1=cosq_sb[:], op=ALU.mult),
                       waits=[tok_cq])
            t_2 = P.ve(lambda e: e.tensor_tensor(out=qt2[:], in0=ps_qrot[:, 0:TOWN],
                                                 in1=sinmq_sb[:], op=ALU.mult),
                       waits=[t_rot, tok_sq])
            g_psqrot.use(t_2)
            g_qraw[bq].use(t_2)
            t_q = P.ve(lambda e, o=q_all[:, hi * TOWN:(hi + 1) * TOWN]:
                          e.tensor_tensor(out=o, in0=qt1[:], in1=qt2[:], op=ALU.add))
            tok_q_head[hi] = t_q

        tok_qmx = None

        def qnorm_head(hi):
            nonlocal tok_qmx
            bs = hi % 2
            t_qs = P.sc(lambda e, o=qsq2[bs][:], i=q_all[:, hi * TOWN:(hi + 1) * TOWN]:
                           e.activation(out=o, in_=i, func=AF.Square),
                        waits=[tok_q_head[hi]] + g_qsq2[bs].take())
            t_mm = None
            for (h0, h1) in HALVES:
                w = [t_qs, tok_ones] + (g_psqn.take() if h0 == 0 else [])
                t_mm = P.pe(lambda e, o=ps_qn[0:1, h0:h1], r=qsq2[bs][:, h0:h1]:
                               e.matmul(o, ones_bf[:, 0:1], r, start=True, stop=True),
                            waits=w, inc=(h1 == TOWN))
            g_qsq2[bs].use(t_mm)
            for j in range(NJ):
                tok_qmx = P.ve(
                    lambda e, o=qsqmax[0:1, hi * NJ + j: hi * NJ + j + 1],
                           i=ps_qn[0:1, j * 128:(j + 1) * 128]:
                        e.reduce_max(o, i, axis=AX.X),
                    waits=[t_mm] if j == 0 else ())
            g_psqn.use(tok_qmx)

        q_mms(0)
        for hi in range(NH):
            if hi + 1 < NH:
                q_mms(hi + 1)
            q_rope(hi)
            qnorm_head(hi)

        t_qsr = P.sc(lambda e: e.activation(out=qsqmax[:], in_=qsqmax[:], func=AF.Sqrt),
                     waits=[tok_qmx])
        t_qbc = P.pe(lambda e: e.matmul(ps_qrot[:, 0:NH * NJ], onesr[0:1, :], qsqmax[:],
                                        start=True, stop=True),
                     waits=[t_qsr, tok_onesr] + g_psqrot.take(), inc=True)
        tok_qmax = P.ve(lambda e: e.tensor_copy(qmax_bc[:], ps_qrot[:, 0:NH * NJ]),
                        waits=[t_qbc])

        if dbg_q is not None:
            P.stream("gpsimd", 1, "dbgQ").load(
                [(dbg_q[hi], q_all[:, hi * TOWN:(hi + 1) * TOWN], {}) for hi in range(NH)],
                waits=[tok_q_head[NH - 1]])
        fence = P.fence()
        stA2.close()

    tok_O_stores = None
    if "B" in cfg.phases:
        stB = ExitStack()
        VW = 132
        NTC = S // 128
        k_sb = [SB(stB, f"k_sb{b}", [128, S], BF16) for b in range(2)]
        v_sb = [SB(stB, f"v_sb{b}", [128, NTC * VW], BF16) for b in range(2)]
        prT = [SB(stB, f"prT{b}", [128, 1024], BF16) for b in range(2)]
        ksq = SB(stB, "ksq", [128, S], BF16)
        kslot = SB(stB, "kslot", [1, 16], F32)
        kn1 = SB(stB, "kn1", [1, 1], F32)
        kn_bc = [SB(stB, f"kn_bc{b}", [128, 1], F32) for b in range(2)]
        knrow = SB(stB, "knrow", [1, 128], F32)
        negb = [SB(stB, f"negb{b}", [128, 1], F32) for b in range(2)]
        rinv = SB(stB, "rinv", [128, 1], F32)
        onb = SB(stB, "onb", [128, 128], BF16)
        otr = SB(stB, "otr", [128, 128], BF16)

        ps_s = [PS(stB, f"ps_s{b}", [128, 1024]) for b in range(2)]
        ps_o = [PS(stB, f"ps_o{b}", [128, 132]) for b in range(2)]
        ps_tr = PS(stB, "ps_tr", [128, 128], BF16)
        ps_kn = PS(stB, "ps_kn", [128, 512])

        g_kv = [Guard(), Guard()]
        g_pss = [Guard(), Guard()]
        g_pso = [Guard(), Guard()]
        g_prT = [Guard(), Guard()]
        g_negb = [Guard(), Guard()]
        g_rinv = Guard()
        g_otr = Guard()
        g_onb = Guard()
        g_pstr = Guard()
        g_ksq = Guard()
        g_knbc = [Guard(), Guard()]
        g_kn = Guard()
        g_pskn = Guard()

        stKV = P.stream("sync", 2, "dKV")
        stOst = P.stream("sync", 1, "sO")
        tok_kld = [None] * KVH
        tok_kn = [None] * KVH

        tok_vones = None
        for b in range(2):
            tok_vones = P.ve(
                lambda e, o=v_sb[b].ap().rearrange("p (tc w) -> p tc w", w=VW)[:, :, 128:129]:
                    e.memset(o, 1.0),
                waits=(list(tok_A1_stores) if b == 0 else []))

        def load_kv(g, first=False):
            w = g_kv[g % 2].take() + list(tok_A1_stores)
            if first:
                w = w + fence + ([tok_qmax] if tok_qmax else [])
            tok_kld[g] = stKV.load(
                [(k_sb[g % 2][:], K_scr[g], {}),
                 (v_sb[g % 2].ap().rearrange("p (tc w) -> p tc w", w=VW)[:, :, 0:128],
                  V_scr[:, :, g, :].rearrange("tc p d -> p tc d"), {})],
                waits=w, idx=g)

        def emit_knorm(g):
            bg = g % 2
            t_sq = P.sc(lambda e, i2=k_sb[bg][:]: e.activation(out=ksq[:], in_=i2,
                                                               func=AF.Square),
                        waits=[tok_kld[g]] + g_ksq.take())
            nblk = S // 512
            t_mx = None
            for b in range(nblk):
                w = [t_sq] + ([t_mx] if t_mx else [])
                if b == 0:
                    w += g_pskn.take()
                t_mm = P.pe(lambda e, o=ps_kn[0:32, 0:512],
                                   r=ksq[:, b * 512:(b + 1) * 512]:
                               e.matmul(o, ones_bf[:, 0:32], r, start=True, stop=True),
                            waits=w, inc=True)
                t_mx = P.ve(lambda e, a=kslot[0:1, b:b + 1], i2=ps_kn[0:1, 0:512]:
                               e.tensor_scalar(out=ksq[0:1, 0:512], in0=i2, scalar1=1.0,
                                               scalar2=None, op0=ALU.mult, op1=ALU.max,
                                               accum_out=a),
                            waits=[t_mm])
            t_red = P.ve(lambda e, n=nblk: e.reduce_max(kn1[:], kslot[0:1, 0:n], axis=AX.X),
                         waits=[t_mx])
            t_srt = P.sc(lambda e: e.activation(out=kn1[:], in_=kn1[:], func=AF.Sqrt,
                                                scale=1.0),
                         waits=[t_red])
            t_rf = P.ve(lambda e: e.tensor_scalar(out=knrow[:], in0=onesr[0:1, :],
                                                  scalar1=kn1[0:1, 0:1], scalar2=None,
                                                  op0=ALU.mult),
                        waits=[t_srt, tok_onesr] + g_kn.take())
            t_bc = P.pe(lambda e: e.transpose(ps_kn[0:128, 0:1], knrow[:],
                                              const_sb[0:1, 384:385]),
                        waits=[t_rf], inc=True)
            g_kn.use(t_bc)
            t_cp = P.ve(lambda e, o=kn_bc[bg][:]: e.tensor_scalar(
                            out=o, in0=ps_kn[:, 0:1], scalar1=-1.02, scalar2=None,
                            op0=ALU.mult),
                        waits=[t_bc] + g_knbc[bg].take())
            g_pskn.use(t_cp)
            g_ksq.use(t_bc)
            tok_kn[g] = t_cp

        load_kv(0, first=True)
        if KVH > 1:
            load_kv(1)
        emit_knorm(0)

        iters = [(g, g * GR + hh, j) for g in range(KVH) for hh in range(GR)
                 for j in range(NJ)]
        state = {}
        grp_ctr = [0]

        def emit_negb(i):
            g, h, j = iters[i]
            bi = i % 2
            t_nb = P.ve(lambda e, o=negb[bi][:], q=qmax_bc[:, h * NJ + j: h * NJ + j + 1],
                               kn=kn_bc[g % 2][:]:
                           e.tensor_tensor(out=o, in0=q, in1=kn, op=ALU.mult),
                        waits=[tok_qmax, tok_kn[g]] + g_negb[bi].take())
            g_knbc[g % 2].use(t_nb)
            state[("b", i)] = t_nb

        def emit_iter(i, hooks=(), pend=None):
            g, h, j = iters[i]
            bg = g % 2
            bi = i % 2
            t_nb = state.pop(("b", i))
            qtile = q_all[:, h * TOWN + j * 128: h * TOWN + j * 128 + 128]
            ngrp = j + 1
            nkc = 8 * ngrp
            po = ps_o[bi]
            vv = v_sb[bg].ap().rearrange("p (tc w) -> p tc w", w=VW)
            tok_exp = [None] * ngrp
            avtok = [None]

            def mm_grp(gr):
                bank = grp_ctr[0] % 2
                grp_ctr[0] += 1
                last = (gr == ngrp - 1)
                tok_mm = None
                for i8 in range(8):
                    kc = gr * 8 + i8
                    w = g_pss[bank].take() if i8 == 0 else []
                    if gr == 0 and i8 == 0:
                        w += [tok_kld[g]]
                        if tok_q_head[h]:
                            w.append(tok_q_head[h])
                    tok_mm = P.pe(
                        lambda e, o=ps_s[bank][:, i8 * 128:(i8 + 1) * 128],
                               l=k_sb[bg][:, kc * 128:(kc + 1) * 128], r=qtile:
                            e.matmul(o, l, r, start=True, stop=True),
                        waits=w, inc=(i8 == 7))
                tok_rdy = tok_mm
                if last:
                    moff = (j % 2) * 1024
                    tok_rdy = P.ve(
                        lambda e, o=ps_s[bank][:], m2=maskT_sb[:, moff:moff + 1024]:
                            e.tensor_tensor(out=o, in0=o, in1=m2, op=ALU.add),
                        waits=[tok_mm, tok_mask])
                t_exp = P.sc(
                    lambda e, o=prT[bank][:], i2=ps_s[bank][:], bb=negb[bi].ap():
                        e.activation(out=o, in_=i2, func=AF.Exp, bias=bb, scale=1.0),
                    waits=[tok_rdy, t_nb] + g_prT[bank].take())
                g_pss[bank].use(t_exp)
                tok_exp[gr] = t_exp
                return bank

            def mk_av(gr, bank):
                def av():
                    for i8 in range(8):
                        kc = gr * 8 + i8
                        w = [tok_exp[gr]] if i8 == 0 else []
                        if kc == 0:
                            w = list(w) + [tok_vones] + g_pso[bi].take()
                        avtok[0] = P.pe(
                            lambda e, o=po[:, 0:129],
                                   l=prT[bank][:, i8 * 128:(i8 + 1) * 128],
                                   r=vv[:, kc, 0:129],
                                   s=(kc == 0), t_=(kc == nkc - 1):
                                e.matmul(o, l, r, start=s, stop=t_),
                            waits=w, inc=(i8 == 7))
                    g_prT[bank].use(avtok[0])
                return av

            bank0 = mm_grp(0)
            if pend is not None:
                pend()
            for hk in hooks:
                hk()
            prev = (0, bank0)
            for gr in range(1, ngrp):
                bank = mm_grp(gr)
                mk_av(*prev)()
                prev = (gr, bank)
            g_negb[bi].use(tok_exp[ngrp - 1])
            pend_av = mk_av(*prev)

            def epilogue():
                t_ri = P.ve(lambda e: e.reciprocal(rinv[:], po[:, 128:129]),
                            waits=[avtok[0]] + g_rinv.take())
                t_on = P.ve(lambda e, i2=po[:, 0:128]: e.tensor_scalar(
                                out=onb[:], in0=i2, scalar1=rinv.ap(), scalar2=None,
                                op0=ALU.mult),
                            waits=[t_ri] + g_onb.take())
                g_rinv.use(t_on)
                g_pso[bi].use(t_on)
                t_otr = P.pe(lambda e: e.transpose(ps_tr[:, 0:128], onb[:], ident_bf[:]),
                             waits=[t_on, tok_ident] + g_pstr.take(), inc=True)
                g_onb.use(t_otr)
                t_oc = P.ve(lambda e: e.tensor_copy(otr[:], ps_tr[:, 0:128]),
                            waits=[t_otr] + g_otr.take())
                g_pstr.use(t_oc)
                g_otr.use(stOst.load([(O_scr[h, j], otr[:], {})], waits=[t_oc]))
            return pend_av, epilogue

        emit_negb(0)
        pend_av = None
        prev_epi = None
        for i in range(len(iters)):
            hooks = []
            if prev_epi:
                hooks.append(prev_epi)
            g, h, j = iters[i]
            if i > 0 and iters[i - 1][0] != g:
                gp = iters[i - 1][0]
                def kv_adv(gp=gp):
                    g_kv[gp % 2].use(("p", P.cnt["p"]))
                    if gp + 2 < KVH:
                        load_kv(gp + 2)
                hooks.append(kv_adv)
            if i + 1 < len(iters):
                g2 = iters[i + 1][0]
                if g2 != g and tok_kn[g2] is None:
                    hooks.append(lambda g2=g2: emit_knorm(g2))
                hooks.append(lambda i=i: emit_negb(i + 1))
            pend_av, prev_epi = emit_iter(i, hooks, pend_av)
        pend_av()
        prev_epi()

        tok_O_stores = ("sO0", P.cnt["sO0"])
        if dbg_o is not None:
            P.stream("gpsimd", 1, "dbgO").load([(dbg_o, O_scr, {})], waits=[tok_O_stores])
        fence = P.fence()
        stB.close()
    st_q.close()

    st_rx = ExitStack()
    need_rx = ("C" in cfg.phases) or ("F1" in cfg.phases) or ("F2" in cfg.phases)
    rx = SB(st_rx, "rx", [128, KD * TOWN], BF16) if need_rx else None

    def mk_rvec(rstat):
        def rvec(i, h0=0, h1=None):
            h1 = TOWN if h1 is None else h1
            return rstat[0:1, i * TOWN + h0: i * TOWN + h1]
        return rvec

    tok_x1 = [None] * KD

    if "C" in cfg.phases:
        stC = ExitStack()
        o_sb = SB(stC, "o_sb", [128, NOC * TOWN], BF16)
        wo_bf = [SB(stC, f"wo_bf{b}", [128, NOC * 128], BF16) for b in range(2)]
        xf32 = [SB(stC, f"xf32{b}", [128, TOWN], F32) for b in range(2)]
        r1sq = [SB(stC, f"r1sq{b}", [128, TOWN], BF16) for b in range(2)]
        bc_rstd = SB(stC, "bc_rstd", [128, TOWN], F32)
        bc_mr = SB(stC, "bc_mr", [128, TOWN], F32)
        tmpb = SB(stC, "tmpb", [128, TOWN], F32) if TOWN <= 512 else xf32[0]

        rstat1 = SB(stC, "rstat1", [32, 4 * TOWN], F32)
        rvec = mk_rvec(rstat1)
        P.ve(lambda e: e.memset(rstat1[:], 0.0), waits=fence)
        ps_a = [PS(stC, f"ps_a{b}", [128, TOWN]) for b in range(2)]
        ps_sum = PS(stC, "ps_sum", [32, max(TOWN, 512)])
        ps_sq = PS(stC, "ps_sq", [32, max(TOWN, 512)])

        g_wo = [Guard(), Guard()]
        g_psa = [Guard(), Guard()]
        g_xf = [Guard(), Guard()]
        g_sq = [Guard(), Guard()]

        stOL = P.stream("sync", 1, "dOL")
        stWO = P.stream("gpsimd", 2, "dWO")
        stXF = P.stream("sync", 2, "dXF")
        w0 = list(fence)
        if tok_O_stores:
            w0.append(tok_O_stores)
        tok_oall = stOL.load(
            [(o_sb.ap().rearrange("p (c j q) -> p c j q", c=NOC, j=NJ)[:, hi],
              O_scr[hi].rearrange("j d q -> d j q"), {}) for hi in range(NOC)],
            waits=w0)

        tok_wo = [None] * KD

        def load_wo(oi, first=False):
            tok_wo[oi] = load_slab(stWO, oi, wo_bf[oi % 2], woT[oi], NOC, 128,
                                   extra_waits=g_wo[oi % 2].take() + (list(fence) if first else []))

        load_wo(0, first=True)
        if KD > 1:
            load_wo(1)

        tok_r1 = [None] * KD
        tok_stats_done = None
        for oi in range(KD):
            bw = oi % 2
            pa = ps_a[oi % 2]
            t_xf = stXF.load([(xf32[bw][:], x_ownT[oi], {})], waits=g_xf[bw].take(), idx=oi)
            tok_mm = None
            for ki in range(NOC):
                for (h0, h1) in HALVES:
                    w = ([tok_oall, tok_wo[oi]] + g_psa[oi % 2].take()) \
                        if (ki == 0 and h0 == 0) else []
                    tok_mm = P.pe(
                        lambda e, o=pa[:, h0:h1],
                               l=wo_bf[bw][:, ki * 128:(ki + 1) * 128],
                               r=o_sb[:, ki * TOWN + h0: ki * TOWN + h1],
                               s=(ki == 0), t_=(ki == NOC - 1):
                            e.matmul(o, l, r, start=s, stop=t_),
                        waits=w, inc=(ki == NOC - 1 and h1 == TOWN))
            t_r1 = P.ve(lambda e, o=rx[:, oi * TOWN:(oi + 1) * TOWN], i=pa[:, 0:TOWN],
                               x=xf32[bw][:]:
                           e.scalar_tensor_tensor(out=o, in0=i, scalar=1.0, in1=x,
                                                  op0=ALU.mult, op1=ALU.add),
                        waits=[tok_mm, t_xf])
            tok_r1[oi] = t_r1
            g_psa[oi % 2].use(t_r1)
            g_xf[bw].use(t_r1)
            t_sqa = P.sc(lambda e, o=r1sq[oi % 2][:], i=rx[:, oi * TOWN:(oi + 1) * TOWN]:
                            e.activation(out=o, in_=i, func=AF.Square),
                         waits=[t_r1] + g_sq[oi % 2].take())
            for (h0, h1) in HALVES:
                P.pe(lambda e, o=ps_sum[0:1, h0:h1],
                            r=rx[:, oi * TOWN + h0: oi * TOWN + h1],
                            s=(oi == 0), t_=(oi == KD - 1):
                        e.matmul(o, ones_bf[:, 0:1], r, start=s, stop=t_),
                     waits=[t_r1, tok_ones] if (oi == 0 and h0 == 0) else [t_r1])
                tok_stats_done = P.pe(
                    lambda e, o=ps_sq[0:1, h0:h1], r=r1sq[oi % 2][:, h0:h1],
                           s=(oi == 0), t_=(oi == KD - 1):
                        e.matmul(o, ones_bf[:, 0:1], r, start=s, stop=t_),
                    waits=[t_sqa], inc=(h1 == TOWN))
            g_sq[oi % 2].use(tok_stats_done)
            g_wo[oi % 2].use(("p", P.cnt["p"]))
            if oi + 2 < KD:
                load_wo(oi + 2)

        tok_dbgr1 = None
        if dbg_r1 is not None:
            tok_dbgr1 = P.stream("gpsimd", 1, "dbgR1").load(
                [(dbg_r1[oi], rx[:, oi * TOWN:(oi + 1) * TOWN], {}) for oi in range(KD)],
                waits=[tok_r1[KD - 1]])

        if dbg_ln1 is not None:
            pssum_dump = SB(stC, "pssum_dump", [1, 2 * TOWN], F32)
            t_pd = P.ve(lambda e: e.tensor_copy(pssum_dump[0:1, 0:TOWN], ps_sum[0:1, 0:TOWN]),
                        waits=[tok_stats_done])
            t_pd2 = P.ve(lambda e: e.tensor_copy(pssum_dump[0:1, TOWN:2 * TOWN], ps_sq[0:1, 0:TOWN]))
            P.stream("sync", 1, "dbgPS").load([(dbg_ps, pssum_dump[:], {})], waits=[t_pd2])
        t_mean = P.ve(lambda e: e.tensor_scalar(out=rvec(0), in0=ps_sum[0:1, 0:TOWN],
                                                scalar1=1.0 / D, scalar2=None, op0=ALU.mult),
                      waits=[tok_stats_done])
        t_msq = P.ve(lambda e: e.tensor_scalar(out=rvec(1), in0=ps_sq[0:1, 0:TOWN],
                                               scalar1=1.0 / D, scalar2=None, op0=ALU.mult))
        t_m2 = P.ve(lambda e: e.tensor_tensor(out=rvec(2), in0=rvec(0), in1=rvec(0), op=ALU.mult))
        t_var = P.ve(lambda e: e.tensor_tensor(out=rvec(3), in0=rvec(1), in1=rvec(2),
                                               op=ALU.subtract))
        t_ln = P.sc(lambda e: e.activation(out=rvec(1), in_=rvec(3), func=AF.Ln, bias=const_sb[0:1, 416:417]),
                    waits=[t_var])
        t_rstd = P.sc(lambda e: e.activation(out=rvec(2), in_=rvec(1), func=AF.Exp, scale=-0.5))
        t_mr = P.ve(lambda e: e.tensor_tensor(out=rvec(1), in0=rvec(0), in1=rvec(2), op=ALU.mult),
                    waits=[t_rstd])
        t_bc = None
        for (h0, h1) in HALVES:
            P.pe(lambda e, o=ps_a[0][:, h0:h1], r=rvec(2, h0, h1):
                    e.matmul(o, onesr[0:1, :], r, start=True, stop=True),
                 waits=[t_mr, tok_onesr] if h0 == 0 else ())
            t_bc = P.pe(lambda e, o=ps_a[1][:, h0:h1], r=rvec(1, h0, h1):
                           e.matmul(o, onesr[0:1, :], r, start=True, stop=True),
                        waits=(), inc=(h1 == TOWN))
        t_bc1 = P.ve(lambda e: e.tensor_copy(bc_rstd[:], ps_a[0][:, 0:TOWN]), waits=[t_bc])
        t_bc2 = P.ve(lambda e: e.tensor_copy(bc_mr[:], ps_a[1][:, 0:TOWN]))
        if dbg_ln1 is not None:
            P.stream("sync", 1, "dbgL").load(
                [(dbg_ln1, rstat1[0:1, :], {}), (dbg_bc[0], bc_rstd[:], {}),
                 (dbg_bc[1], bc_mr[:], {})], waits=[t_bc2])
        for oi in range(KD):
            t_t = P.ve(lambda e, i=rx[:, oi * TOWN:(oi + 1) * TOWN]:
                          e.tensor_tensor(out=tmpb[:], in0=i, in1=bc_rstd[:], op=ALU.mult),
                       waits=[tok_dbgr1] if (oi == 0 and tok_dbgr1) else ())
            tok_x1[oi] = P.ve(lambda e, o=rx[:, oi * TOWN:(oi + 1) * TOWN]:
                                 e.tensor_tensor(out=o, in0=tmpb[:], in1=bc_mr[:],
                                                 op=ALU.subtract))
        if dbg_x1 is not None:
            tok_dbgx1 = P.stream("gpsimd", 1, "dbgX1").load(
                [(dbg_x1[oi], rx[:, oi * TOWN:(oi + 1) * TOWN], {}) for oi in range(KD)],
                waits=[tok_x1[KD - 1]])
        fence = P.fence()
        stC.close()

    st_h = ExitStack()
    HRES = cfg.HRES
    h_sb = SB(st_h, "h_sb", [128, max(HRES, 1) * TOWN], BF16) \
        if ("F1" in cfg.phases or "F2" in cfg.phases) else None
    tok_h = [None] * NF

    if "F1" in cfg.phases:
        stF1 = ExitStack()
        wg_bf = [SB(stF1, f"wg_bf{b}", [128, KD * 128], BF16) for b in range(2)]
        wu_bf = [SB(stF1, f"wu_bf{b}", [128, KD * 128], BF16) for b in range(2)]
        gsil = SB(stF1, "gsil", [128, TOWN], BF16)
        hsp = [SB(stF1, f"hsp{b}", [128, TOWN], BF16) for b in range(2)]
        ps_g = [PS(stF1, f"ps_g{b}", [128, TOWN]) for b in range(2)]
        ps_u = [PS(stF1, f"ps_u{b}", [128, TOWN]) for b in range(2)]

        g_wg = [Guard(), Guard()]
        g_wu = [Guard(), Guard()]
        g_psg = [Guard(), Guard()]
        g_psu = [Guard(), Guard()]
        g_hsp = [Guard(), Guard()]
        g_gsil = Guard()

        stWGU = P.stream("gpsimd", 2, "dWGU")
        stHsp = P.stream("sync", 2, "sH")
        tok_wgu = [None] * NF

        def load_wgu(f, first=False):
            ext = g_wg[f % 2].take() + g_wu[f % 2].take() + (list(fence) if first else [])
            load_slab(stWGU, f, wg_bf[f % 2], wgT[f], KD, 128, extra_waits=ext)
            tok_wgu[f] = load_slab(stWGU, f, wu_bf[f % 2], wuT[f], KD, 128)

        load_wgu(0, first=True)
        if NF > 1:
            load_wgu(1)

        for f in range(NF):
            bw = f % 2
            pg, pu = ps_g[f % 2], ps_u[f % 2]
            tok_gmm = None
            tok_umm = None
            for ki in range(KD):
                for (h0, h1) in HALVES:
                    w = []
                    if ki == 0 and h0 == 0:
                        w = [tok_wgu[f]] + g_psg[f % 2].take()
                        if tok_x1[KD - 1]:
                            w.append(tok_x1[KD - 1])
                    tok_gmm = P.pe(
                        lambda e, o=pg[:, h0:h1],
                               l=wg_bf[bw][:, ki * 128:(ki + 1) * 128],
                               r=rx[:, ki * TOWN + h0: ki * TOWN + h1],
                               s=(ki == 0), t_=(ki == KD - 1):
                            e.matmul(o, l, r, start=s, stop=t_),
                        waits=w, inc=(ki == KD - 1 and h1 == TOWN))
            for ki in range(KD):
                for (h0, h1) in HALVES:
                    w = g_psu[f % 2].take() if (ki == 0 and h0 == 0) else []
                    tok_umm = P.pe(
                        lambda e, o=pu[:, h0:h1],
                               l=wu_bf[bw][:, ki * 128:(ki + 1) * 128],
                               r=rx[:, ki * TOWN + h0: ki * TOWN + h1],
                               s=(ki == 0), t_=(ki == KD - 1):
                            e.matmul(o, l, r, start=s, stop=t_),
                        waits=w, inc=(ki == KD - 1 and h1 == TOWN))
            t_sil = P.sc(lambda e, i=pg[:, 0:TOWN]: e.activation(out=gsil[:], in_=i, func=AF.Silu),
                         waits=[tok_gmm] + g_gsil.take())
            g_psg[f % 2].use(t_sil)
            if f < HRES:
                t_h = P.ve(lambda e, o=h_sb[:, f * TOWN:(f + 1) * TOWN], i=pu[:, 0:TOWN]:
                              e.tensor_tensor(out=o, in0=gsil[:], in1=i, op=ALU.mult),
                           waits=[tok_umm, t_sil])
                tok_h[f] = t_h
            else:
                bs = f % 2
                t_h = P.ve(lambda e, o=hsp[bs][:], i=pu[:, 0:TOWN]:
                              e.tensor_tensor(out=o, in0=gsil[:], in1=i, op=ALU.mult),
                           waits=[tok_umm, t_sil] + g_hsp[bs].take())
                tok_h[f] = stHsp.load([(h_scr[f - HRES], hsp[bs][:], {})],
                                      waits=[t_h], idx=f)
                g_hsp[bs].use(tok_h[f])
            g_gsil.use(t_h)
            g_psu[f % 2].use(t_h)
            g_wg[f % 2].use(("p", P.cnt["p"]))
            g_wu[f % 2].use(("p", P.cnt["p"]))
            if f + 2 < NF:
                load_wgu(f + 2)

        if dbg_h is not None:
            stDH = P.stream("gpsimd", 1, "dbgH")
            stDH.load([(dbg_h[f], h_sb[:, f * TOWN:(f + 1) * TOWN], {})
                       for f in range(HRES)] +
                      [(dbg_h[f], h_scr[f - HRES], {}) for f in range(HRES, NF)],
                      waits=[t for t in tok_h if t is not None])
        fence = P.fence()
        stF1.close()

    if "F2" in cfg.phases:
        stF2 = ExitStack()
        wd_bf = [SB(stF2, f"wd_bf{b}", [128, 4 * 256], BF16) for b in range(2)]
        hst = [SB(stF2, f"hst{b}", [128, TOWN], BF16) for b in range(4)]
        r2sq = [SB(stF2, f"r2sq{b}", [128, TOWN], BF16) for b in range(2)]
        bc2_rstd = SB(stF2, "bc2_rstd", [128, TOWN], F32)
        bc2_mr = SB(stF2, "bc2_mr", [128, TOWN], F32)
        tmp2 = SB(stF2, "tmp2", [128, TOWN], F32)
        outf = [SB(stF2, f"outf{b}", [128, TOWN], F32) for b in range(2)]

        rstat2 = SB(stF2, "rstat2", [32, 4 * TOWN], F32)
        rvec2 = mk_rvec(rstat2)
        P.ve(lambda e: e.memset(rstat2[:], 0.0), waits=fence)
        ps_y = [PS(stF2, f"ps_y{b}", [128, TOWN]) for b in range(2)]
        ps2_sum = PS(stF2, "ps2_sum", [32, max(TOWN, 512)])
        ps2_sq = PS(stF2, "ps2_sq", [32, max(TOWN, 512)])

        stY = P.stream("sync", 2, "sY")
        g_wd = [Guard(), Guard()]
        g_hst = [Guard() for _ in range(4)]
        g_psy = [Guard(), Guard()]
        g_sq2 = [Guard(), Guard()]
        g_outf = [Guard(), Guard()]

        stWD = P.stream("gpsimd", 2, "dWD")
        stHL = P.stream("sync", 4, "dHL")
        tok_wd = {}
        tok_hload = {}

        def load_wd(dg, fb, first=False):
            f0, f1 = 4 * fb, min(NF, 4 * fb + 4)
            tok_wd[(dg, fb)] = stWD.load(
                [(wd_bf[fb % 2].ap().rearrange("p (f m) -> p f m", f=4)[:, 0:f1 - f0, :],
                  wdT[dg, f0:f1].rearrange("f p m -> p f m"), {})],
                waits=g_wd[fb % 2].take() + (list(fence) if first else []), idx=fb)

        hfirst = [True] * 4
        def load_h(dg, f):
            if f < HRES:
                return
            bs = f % 4
            w = g_hst[bs].take() + [tok_h[f]]
            if hfirst[bs]:
                hfirst[bs] = False
                w = w + list(fence)
            tok_hload[(dg, f)] = stHL.load([(hst[bs][:], h_scr[f - HRES], {})],
                                           waits=w, idx=f)

        PRE = 4
        NFB = (NF + 3) // 4
        tok_r2 = [None] * KD
        tok_stats2 = None
        for dg in range(cfg.NDG):
            d0 = 2 * dg
            nd = min(2, KD - d0)
            load_wd(dg, 0, first=(dg == 0))
            if NFB > 1:
                load_wd(dg, 1)
            for f in range(min(PRE, NF)):
                load_h(dg, f)
            tok_dmm = None
            for f in range(NF):
                bw = (f // 4) % 2
                if f < HRES:
                    hsrc = h_sb.ap()[:, f * TOWN:(f + 1) * TOWN]
                    tok_hs = tok_h[f]
                else:
                    hsrc = hst[f % 4][:]
                    tok_hs = tok_hload[(dg, f)]
                for d in range(nd):
                    for (h0, h1) in HALVES:
                        w = []
                        if d == 0 and h0 == 0:
                            w = [tok_wd[(dg, f // 4)], tok_hs]
                            if f == 0:
                                for dd in range(nd):
                                    w += g_psy[dd].take()
                        tok_dmm = P.pe(
                            lambda e, o=ps_y[d][:, h0:h1],
                                   l=wd_bf[bw][:, (f % 4) * 256 + d * 128:
                                               (f % 4) * 256 + (d + 1) * 128],
                                   r=hsrc[:, h0:h1],
                                   s=(f == 0), t_=(f == NF - 1):
                                e.matmul(o, l, r, start=s, stop=t_),
                            waits=w, inc=(d == nd - 1 and h1 == TOWN))
                if f >= HRES:
                    g_hst[f % 4].use(tok_dmm)
                if f % 4 == 3 or f == NF - 1:
                    g_wd[(f // 4) % 2].use(tok_dmm)
                    if f // 4 + 2 < NFB:
                        load_wd(dg, f // 4 + 2)
                if f + PRE < NF:
                    load_h(dg, f + PRE)
            for d in range(nd):
                oi = d0 + d
                t_r2 = P.ve(lambda e, o=rx[:, oi * TOWN:(oi + 1) * TOWN], i=ps_y[d][:, 0:TOWN]:
                               e.scalar_tensor_tensor(out=o, in0=i, scalar=1.0, in1=o,
                                                      op0=ALU.mult, op1=ALU.add),
                            waits=[tok_dmm, tok_x1[oi] if tok_x1[oi] else None])
                tok_r2[oi] = t_r2
                g_psy[d].use(t_r2)
                t_sq = P.sc(lambda e, o=r2sq[d][:], i=rx[:, oi * TOWN:(oi + 1) * TOWN]:
                               e.activation(out=o, in_=i, func=AF.Square),
                            waits=[t_r2] + g_sq2[d].take())
                for (h0, h1) in HALVES:
                    P.pe(lambda e, o=ps2_sum[0:1, h0:h1],
                                r=rx[:, oi * TOWN + h0: oi * TOWN + h1],
                                s=(oi == 0), t_=(oi == KD - 1):
                            e.matmul(o, ones_bf[:, 0:1], r, start=s, stop=t_),
                         waits=[t_r2, tok_ones] if (oi == 0 and h0 == 0) else [t_r2])
                    tok_stats2 = P.pe(
                        lambda e, o=ps2_sq[0:1, h0:h1], r=r2sq[d][:, h0:h1],
                               s=(oi == 0), t_=(oi == KD - 1):
                            e.matmul(o, ones_bf[:, 0:1], r, start=s, stop=t_),
                        waits=[t_sq], inc=(h1 == TOWN))
                g_sq2[d].use(tok_stats2)

        t_mean = P.ve(lambda e: e.tensor_scalar(out=rvec2(0), in0=ps2_sum[0:1, 0:TOWN],
                                                scalar1=1.0 / D, scalar2=None, op0=ALU.mult),
                      waits=[tok_stats2])
        t_msq = P.ve(lambda e: e.tensor_scalar(out=rvec2(1), in0=ps2_sq[0:1, 0:TOWN],
                                               scalar1=1.0 / D, scalar2=None, op0=ALU.mult))
        t_m2 = P.ve(lambda e: e.tensor_tensor(out=rvec2(2), in0=rvec2(0), in1=rvec2(0), op=ALU.mult))
        t_var = P.ve(lambda e: e.tensor_tensor(out=rvec2(3), in0=rvec2(1), in1=rvec2(2),
                                               op=ALU.subtract))
        t_ln = P.sc(lambda e: e.activation(out=rvec2(1), in_=rvec2(3), func=AF.Ln, bias=const_sb[0:1, 416:417]),
                    waits=[t_var])
        t_rstd = P.sc(lambda e: e.activation(out=rvec2(2), in_=rvec2(1), func=AF.Exp, scale=-0.5))
        t_mr = P.ve(lambda e: e.tensor_tensor(out=rvec2(1), in0=rvec2(0), in1=rvec2(2), op=ALU.mult),
                    waits=[t_rstd])
        t_bc = None
        for (h0, h1) in HALVES:
            P.pe(lambda e, o=ps_y[0][:, h0:h1], r=rvec2(2, h0, h1):
                    e.matmul(o, onesr[0:1, :], r, start=True, stop=True),
                 waits=([t_mr, tok_onesr] + g_psy[0].take() + g_psy[1].take()) if h0 == 0 else ())
            t_bc = P.pe(lambda e, o=ps_y[1][:, h0:h1], r=rvec2(1, h0, h1):
                           e.matmul(o, onesr[0:1, :], r, start=True, stop=True),
                        waits=(), inc=(h1 == TOWN))
        t_bc1 = P.ve(lambda e: e.tensor_copy(bc2_rstd[:], ps_y[0][:, 0:TOWN]), waits=[t_bc])
        t_bc2 = P.ve(lambda e: e.tensor_copy(bc2_mr[:], ps_y[1][:, 0:TOWN]))
        for oi in range(KD):
            bo = oi % 2
            t_t = P.ve(lambda e, i=rx[:, oi * TOWN:(oi + 1) * TOWN]:
                          e.tensor_tensor(out=tmp2[:], in0=i, in1=bc2_rstd[:], op=ALU.mult),
                       waits=[tok_r2[oi]] if tok_r2[oi] else ())
            t_o = P.ve(lambda e, o=outf[bo][:]:
                          e.tensor_tensor(out=o, in0=tmp2[:], in1=bc2_mr[:], op=ALU.subtract),
                       waits=g_outf[bo].take())
            g_outf[bo].use(stY.load([(y_own[oi], outf[bo][:], {})], waits=[t_o], idx=oi))
        stF2.close()

    st_h.close()
    st_rx.close()

    if dbg_hend is not None and "F1" in cfg.phases:
        P.stream("gpsimd", 1, "dbgHE").load(
            [(dbg_hend[f], h_sb[:, f * TOWN:(f + 1) * TOWN], {}) for f in range(cfg.HRES)] +
            [(dbg_hend[f], h_scr[f - cfg.HRES], {}) for f in range(cfg.HRES, NF)],
            waits=P.fence())

    P.waits("sync", [(k, P.cnt[k]) for k in P.dma_sems if P.cnt[k]])
    P.waits("gpsimd", [(k, P.cnt[k]) for k in P.dma_sems if P.cnt[k]])

    with ExitStack() as es:
        sems = {}
        for s in list(P.cnt):
            if P.cnt[s]:
                sems[s] = es.enter_context(nc.semaphore("sem_" + s))
        blk = es.enter_context(nc.Block())
        for eng_name in ENGINES:
            opl = P.ops[eng_name]
            if not opl:
                continue
            dec = getattr(blk, eng_name)
            def body(eng, opl=opl):
                for fn in opl:
                    fn(eng, sems)
            dec(body)
    st_top.close()
    return nc



def rope_tables(S):
    inv = 1.0 / 10000.0 ** (np.arange(0, HD, 2, dtype=np.float64) / HD)
    pos = np.arange(S, dtype=np.float64)
    fr = np.outer(pos, inv)
    emb = np.concatenate([fr, fr], axis=-1)
    return np.cos(emb), np.sin(emb)


def prep_maps(cfg, inputs):
    S, D, F, KD, NF = cfg.S, cfg.D, cfg.F, cfg.KD, cfg.NF
    TOWN, NJ = cfg.TOWN, cfg.NJ
    x = np.ascontiguousarray(np.asarray(inputs["L0_attn_hidden_states"], np.float32).reshape(S, D))
    Wq = np.asarray(inputs["L0_attn_Wq"], np.float32)
    Wkr = np.asarray(inputs["L0_attn_Wkr"], np.float32)
    Wdk = np.asarray(inputs["L0_attn_Wdk"], np.float32)
    Wupk = np.asarray(inputs["L0_attn_Wupk"], np.float32)
    Wupv = np.asarray(inputs["L0_attn_Wupv"], np.float32)
    Wo = np.asarray(inputs["L0_attn_Wo"], np.float32)
    Wg = np.asarray(inputs["L0_ffn_W_gate"], np.float32)
    Wu = np.asarray(inputs["L0_ffn_W_up"], np.float32)
    Wd = np.asarray(inputs["L0_ffn_W_down"], np.float32)

    xT_full = np.ascontiguousarray(x.T)
    xT = xT_full.reshape(KD, 128, S)

    qperm = np.concatenate([128 * h + PERM for h in range(NH)])
    WqT = np.ascontiguousarray(Wq[qperm].reshape(NH, 128, KD, 128).transpose(0, 2, 3, 1))
    WoT = np.ascontiguousarray(Wo.reshape(KD, 128, NOC, 128).transpose(0, 2, 3, 1))
    wkv = np.concatenate([Wdk, Wkr], axis=0)
    wkvT = np.ascontiguousarray(wkv.T).reshape(KD, 128, CD + RD)
    upkv = np.concatenate([Wupk, Wupv], axis=0)
    upkvT = np.ascontiguousarray(upkv.T).reshape(4, 128, 1536)
    WgT = np.ascontiguousarray(Wg.reshape(NF, 128, KD, 128).transpose(0, 2, 3, 1))
    WuT = np.ascontiguousarray(Wu.reshape(NF, 128, KD, 128).transpose(0, 2, 3, 1))
    WdT = np.ascontiguousarray(Wd.reshape(cfg.NDG, 256, NF, 128).transpose(0, 2, 3, 1))

    cos, sin = rope_tables(S)
    dscale = 1.0 / np.sqrt(HD)
    sgn = np.where(PERM < 64, -1.0, 1.0)
    cosq_p = (cos[:, PERM].T * dscale).astype(np.float32)
    sinmq_p = ((sin[:, PERM] * sgn[None, :]).T * dscale).astype(np.float32)
    rdim = np.array([r if r < 32 else r + 32 for r in range(64)])
    krow = np.concatenate([rdim, rdim])
    ksgn = np.where(krow < 64, -1.0, 1.0)
    cosk_p = np.ascontiguousarray(cos[:, krow].T.astype(np.float32))
    sinmk_p = np.ascontiguousarray((sin[:, krow] * ksgn[None, :]).T.astype(np.float32))

    consts = np.zeros((128, 417), np.float32)
    consts[:, 0:128] = np.eye(128)
    R32 = np.zeros((128, 128), np.float32)
    for i in range(128):
        R32[i ^ 32, i] = 1.0
    consts[:, 128:256] = R32
    consts[0, 256:384] = 1.0
    consts[:, 384] = 1.0
    consts[:, 416] = EPS

    shared = dict(xT=xT, wkvT=wkvT, upkvT=upkvT, wqT=WqT, woT=WoT,
                  wgT=WgT, wuT=WuT, wdT=WdT, consts=consts,
                  cosk=cosk_p, sinmk=sinmk_p)

    in_maps = []
    own_cols_all = []
    for c in range(NCORES):
        cols = np.concatenate([np.arange(128) + 128 * tile_of(c, j) for j in range(NJ)])
        own_cols_all.append(cols)
        x_ownT = np.ascontiguousarray(xT_full[:, cols]).reshape(KD, 128, TOWN)
        cq = np.ascontiguousarray(cosq_p[:, cols])
        sq = np.ascontiguousarray(sinmq_p[:, cols])
        mask = np.zeros((128, 2048), np.float32)
        kk = np.arange(128)[:, None, None]
        i8 = np.arange(8)[None, :, None]
        qq = np.arange(128)[None, None, :]
        for par, p in ((0, c), (1, 7 - c)):
            m = np.where(128 * i8 + kk <= 128 * p + qq, 0.0, NEG)
            mask[:, par * 1024:(par + 1) * 1024] = m.reshape(128, 1024)
        m = dict(shared)
        m.update(x_ownT=x_ownT, cosq=cq, sinmq=sq, maskin=mask)
        in_maps.append(m)
    return in_maps, own_cols_all


def assemble(cfg, results, own_cols_all):
    S, D = cfg.S, cfg.D
    y = np.empty((S, D), np.float32)
    for c in range(NCORES):
        yc = results[c]["y_own"].reshape(D, cfg.TOWN)
        y[own_cols_all[c], :] = yc.T
    return y.reshape(1, S, D)


def kernel(**inputs):
    cfg = Cfg()
    nc = build(cfg)
    in_maps, own_cols = prep_maps(cfg, inputs)
    res = run_bass_kernel_spmd(nc, in_maps, core_ids=list(range(NCORES)))
    return assemble(cfg, res.results, own_cols)


# version 14 (final)
# speedup vs baseline: 1.2387x; 1.1050x over previous
import numpy as np
from contextlib import ExitStack
import concourse.bass as bass
import concourse.mybir as mybir
from concourse.bass_utils import run_bass_kernel_spmd

F32 = mybir.dt.float32
BF16 = mybir.dt.bfloat16
AF = mybir.ActivationFunctionType
ALU = mybir.AluOpType
AX = mybir.AxisListType

NCORES = 8
HD = 128
NH = 32
KVH = 8
GR = NH // KVH
CD = 512
RD = 512
NOC = 32
NEG = -60000.0
EPS = 1e-5


class Cfg:
    def __init__(self, S=8192, D=4096, F=11008, hres=36,
                 phases=("A1", "A2", "B", "C", "F1", "F2"), dbg=()):
        self.S, self.D, self.F = S, D, F
        self.KD = D // 128
        self.NTT = S // 512
        self.TOWN = S // NCORES
        self.NJ = self.TOWN // 128
        self.NF = F // 128
        self.HRES = min(hres, self.NF)
        self.NDG = (self.KD + 1) // 2
        self.phases = set(phases)
        self.dbg = set(dbg)
        self.halves = [(h0, min(self.TOWN, h0 + 512)) for h0 in range(0, self.TOWN, 512)]
        assert S % 1024 == 0 and D % 256 == 0 and F % 128 == 0
        assert self.NJ == S // 1024


PERM = np.array(list(range(0, 32)) + list(range(64, 96)) +
                list(range(32, 64)) + list(range(96, 128)))


def tile_of(c, j):
    p = c if j % 2 == 0 else 7 - c
    return 8 * j + p



ENGINES = ("sync", "gpsimd", "scalar", "vector", "tensor")
QSEM = {"sync": "s", "gpsimd": "g", "scalar": "c"}


class Stream:
    def __init__(self, P, queue, nbufs, tag):
        self.P, self.q, self.n = P, queue, nbufs
        self.sems = []
        for i in range(nbufs):
            name = f"{tag}{i}"
            assert name not in P.cnt
            P.cnt[name] = 0
            P.dma_sems.add(name)
            self.sems.append(name)
        self.idx = 0

    def load(self, transfers, waits=(), idx=None):
        P = self.P
        if idx is None:
            idx = self.idx
        self.idx = idx + 1
        sem = self.sems[idx % self.n]
        P.waits(self.q, waits)
        for (out_ap, in_ap, kw) in transfers:
            P.cnt[sem] += 16
            def f(e, SS, out_ap=out_ap, in_ap=in_ap, sem=sem, kw=kw):
                e.dma_start(out=out_ap, in_=in_ap, **kw).then_inc(SS[sem], 16)
            P.raw(self.q, f)
        return (sem, P.cnt[sem])


class Prog:
    def __init__(self):
        self.ops = {e: [] for e in ENGINES}
        self.cnt = {k: 0 for k in ("v", "a", "p")}
        self.dma_sems = set()

    def stream(self, queue, nbufs, tag):
        return Stream(self, queue, nbufs, tag)

    def raw(self, eng, fn):
        self.ops[eng].append(fn)

    def waits(self, eng, toks):
        best = {}
        for t in toks:
            if t is None:
                continue
            s, v = t
            assert v <= self.cnt[s], (s, v, self.cnt[s])
            best[s] = max(best.get(s, 0), v)
        for s, v in sorted(best.items()):
            if v <= 0:
                continue
            def f(e, SS, s=s, v=v):
                e.wait_ge(SS[s], v)
            self.raw(eng, f)

    def fence(self):
        return [(s, c) for s, c in self.cnt.items() if c > 0]

    def _chain(self, eng, sem, fn, waits):
        self.waits(eng, waits)
        prev = self.cnt[sem]
        self.cnt[sem] += 1
        val = self.cnt[sem]
        def f(e, SS, fn=fn, sem=sem, prev=prev):
            if prev:
                e.wait_ge(SS[sem], prev)
            fn(e).then_inc(SS[sem], 1)
        self.raw(eng, f)
        return (sem, val)

    def ve(self, fn, waits=()):
        return self._chain("vector", "v", fn, waits)

    def sc(self, fn, waits=()):
        return self._chain("scalar", "a", fn, waits)

    def pe(self, fn, waits=(), inc=False):
        self.waits("tensor", waits)
        if inc:
            self.cnt["p"] += 1
            val = self.cnt["p"]
            def f(e, SS, fn=fn):
                fn(e).then_inc(SS["p"], 1)
            self.raw("tensor", f)
            return ("p", val)
        def f(e, SS, fn=fn):
            fn(e)
        self.raw("tensor", f)
        return None


class Guard:
    def __init__(self):
        self.toks = []

    def use(self, tok):
        if tok is not None:
            self.toks.append(tok)

    def take(self):
        t, self.toks = self.toks, []
        return t



def build(cfg: Cfg):
    nc = bass.Bass("TRN2", target_bir_lowering=False, debug=False)
    S, D, F, KD, NF = cfg.S, cfg.D, cfg.F, cfg.KD, cfg.NF
    NTT, NJ, TOWN = cfg.NTT, cfg.NJ, cfg.TOWN
    HALVES = cfg.halves
    P = Prog()

    dt = nc.dram_tensor
    xT = dt("xT", [KD, 128, S], F32, kind="ExternalInput").ap()
    x_ownT = dt("x_ownT", [KD, 128, TOWN], F32, kind="ExternalInput").ap()
    wkvT = dt("wkvT", [KD, 128, CD + RD], F32, kind="ExternalInput").ap()
    upkvT = dt("upkvT", [4, 128, 1536], F32, kind="ExternalInput").ap()
    wqT = dt("wqT", [NH, KD, 128, 128], F32, kind="ExternalInput").ap()
    woT = dt("woT", [KD, NOC, 128, 128], F32, kind="ExternalInput").ap()
    wgT = dt("wgT", [NF, KD, 128, 128], F32, kind="ExternalInput").ap()
    wuT = dt("wuT", [NF, KD, 128, 128], F32, kind="ExternalInput").ap()
    wdT = dt("wdT", [cfg.NDG, NF, 128, 256], F32, kind="ExternalInput").ap()
    consts = dt("consts", [128, 417], F32, kind="ExternalInput").ap()
    cosk = dt("cosk", [128, S], F32, kind="ExternalInput").ap()
    sinmk = dt("sinmk", [128, S], F32, kind="ExternalInput").ap()
    cosq = dt("cosq", [128, TOWN], F32, kind="ExternalInput").ap()
    sinmq = dt("sinmq", [128, TOWN], F32, kind="ExternalInput").ap()
    maskin = dt("maskin", [128, 2048], F32, kind="ExternalInput").ap()

    y_own = dt("y_own", [KD, 128, TOWN], F32, kind="ExternalOutput").ap()

    K_scr = dt("K_scr", [KVH, 128, S], BF16, kind="Internal").ap()
    V_scr = dt("V_scr", [S // 128, 128, KVH, 128], BF16, kind="Internal").ap()
    O_scr = dt("O_scr", [NOC, NJ, 128, 128], BF16, kind="Internal").ap()
    NSPILL = NF - cfg.HRES
    h_scr = dt("h_scr", [max(NSPILL, 1), 128, TOWN], BF16, kind="Internal").ap()

    def dbg(name, shape, dtype=F32):
        if name in cfg.dbg:
            return dt("dbg_" + name, shape, dtype, kind="ExternalOutput").ap()
        return None

    dbg_q = dbg("q", [NH, 128, TOWN])
    dbg_k = dbg("k", [KVH, 128, S])
    dbg_v = dbg("v", [S // 128, 128, KVH, 128])
    dbg_o = dbg("o", [NOC, NJ, 128, 128])
    dbg_r1 = dbg("r1", [KD, 128, TOWN])
    dbg_ln1 = dbg("ln1", [1, 4 * TOWN])
    dbg_ps = dbg("ps", [1, 2 * TOWN])
    dbg_bc = dbg("bc", [2, 128, TOWN])
    dbg_x1 = dbg("x1", [KD, 128, TOWN])
    dbg_h = dbg("h", [NF, 128, TOWN])
    dbg_hend = dbg("hend", [NF, 128, TOWN])

    def SB(stk, name, shape, dtype):
        return stk.enter_context(nc.sbuf_tensor(name, shape, dtype))

    def PS(stk, name, shape, dtype=F32):
        esz = 4 if dtype == F32 else 2
        per = int(np.prod(shape[1:])) * esz
        pad = (-per) % 2048
        shape = list(shape)
        shape[-1] = shape[-1] + pad // (esz * int(np.prod(shape[1:-1])))
        return stk.enter_context(nc.psum_tensor(name, shape, dtype))

    st_top = ExitStack()

    const_sb = SB(st_top, "const_sb", [128, 417], F32)
    ident_bf = SB(st_top, "ident_bf", [128, 128], BF16)
    r32_bf = SB(st_top, "r32_bf", [128, 128], BF16)
    ones_bf = SB(st_top, "ones_bf", [128, 32], BF16)
    onesr = SB(st_top, "onesr", [32, 128], F32)
    maskT_sb = SB(st_top, "maskT_sb", [128, 2048], BF16)
    cosq_sb = SB(st_top, "cosq_sb", [128, TOWN], F32)
    sinmq_sb = SB(st_top, "sinmq_sb", [128, TOWN], F32)

    stCS = P.stream("sync", 1, "dCS")
    stCG = P.stream("gpsimd", 1, "dCG")
    tok_cs = stCS.load([(const_sb[:], consts, {}), (onesr[:], consts[0:32, 256:384], {}),
                        (cosq_sb[:], cosq, {}), (sinmq_sb[:], sinmq, {})])
    tok_cg = stCG.load([(ident_bf[:], consts[:, 0:128], {}),
                        (r32_bf[:], consts[:, 128:256], {}),
                        (ones_bf[:], consts[:, 384:416], {}),
                        (maskT_sb[:], maskin, {})])
    tok_const = tok_onesr = tok_cq = tok_sq = tok_cs
    tok_ident = tok_r32 = tok_ones = tok_mask = tok_cg

    def load_slab(stream, idx, dst, src3d, kd, width, extra_waits=()):
        ngrp = max(1, (kd * 128) // 1024)
        ngrp = min(ngrp, kd)
        per = (kd + ngrp - 1) // ngrp
        tr = []
        for gi in range(ngrp):
            k0, k1 = gi * per, min(kd, (gi + 1) * per)
            if k0 >= k1:
                break
            tr.append((dst.ap().rearrange("p (k m) -> p k m", k=kd)[:, k0:k1, :],
                       src3d[k0:k1].rearrange("k p m -> p k m"), {}))
        return stream.load(tr, waits=extra_waits, idx=idx)

    tok_q_head = [None] * NH
    tok_A1_stores = []
    fence = []

    if "A1" in cfg.phases:
        stA = ExitStack()
        CW = CD + RD
        wkv_sb = SB(stA, "wkv_sb", [128, KD * CW], BF16)
        upkv_sb = SB(stA, "upkv_sb", [128, 4 * 1536], BF16)
        x_bf = [SB(stA, f"x_bf{b}", [128, KD * 512], BF16) for b in range(2)]
        ckv_bf = [SB(stA, f"ckv_bf{b}", [128, 4 * 512], BF16) for b in range(2)]
        ckt = [SB(stA, f"ckt{b}", [128, 1024], F32) for b in range(2)]
        kr_bf = [SB(stA, f"kr_bf{b}", [128, 512], BF16) for b in range(2)]
        kt1 = SB(stA, "kt1", [128, 512], F32)
        kt2 = SB(stA, "kt2", [128, 512], F32)
        kA = SB(stA, "kA", [128, 512], BF16)
        kc_bf = SB(stA, "kc_bf", [128, 512], BF16)
        vt_bf = SB(stA, "vt_bf", [128, 512], BF16)

        ps_ckv = [PS(stA, f"ps_ckv{b}", [128, 512]) for b in range(2)]
        ps_kr = [PS(stA, f"ps_kr{b}", [128, 512]) for b in range(2)]
        ps_rot = PS(stA, "ps_rot", [128, 512])
        ps_kc = PS(stA, "ps_kc", [128, 512])
        ps_v = [PS(stA, f"ps_v{b}", [128, 512]) for b in range(2)]

        g_xbf = [Guard(), Guard()]
        g_ckt = [Guard(), Guard()]
        g_psckv = [Guard(), Guard()]
        g_pskr = [Guard(), Guard()]
        g_psrot = Guard()
        g_pskc = Guard()
        g_psv = [Guard(), Guard()]
        g_ckvbf = [Guard(), Guard()]
        g_krbf = [Guard(), Guard()]
        g_kA = Guard()
        g_kcbf = Guard()
        g_vtbf = Guard()

        stWV = P.stream("gpsimd", 1, "dWV")
        stX = P.stream("gpsimd", 2, "dX")
        stCK = P.stream("sync", 2, "dCK")
        stKA = P.stream("sync", 1, "sKA")
        stKC = P.stream("sync", 1, "sKC")
        stVst = P.stream("sync", 1, "sVs")

        load_slab(stWV, 0, wkv_sb, wkvT, KD, CW)
        tok_wkv = tok_upkv = load_slab(stWV, 0, upkv_sb, upkvT, 4, 1536)

        tok_x = [None] * NTT
        tok_ck = [None] * NTT
        tok_ckv_done = [None] * NTT

        def load_tile(tt):
            b = tt % 2
            ngrp = max(1, KD // 8)
            per = (KD + ngrp - 1) // ngrp
            tr = []
            for gi in range(ngrp):
                k0, k1 = gi * per, min(KD, (gi + 1) * per)
                tr.append((x_bf[b].ap().rearrange("p (k t) -> p k t", k=KD)[:, k0:k1, :],
                           xT[k0:k1, :, tt * 512:(tt + 1) * 512].rearrange("k p t -> p k t"),
                           {}))
            tok_x[tt] = stX.load(tr, waits=g_xbf[b].take(), idx=tt)
            tok_ck[tt] = stCK.load(
                [(ckt[b][:, 0:512], cosk[:, tt * 512:(tt + 1) * 512], {}),
                 (ckt[b][:, 512:1024], sinmk[:, tt * 512:(tt + 1) * 512], {})],
                waits=g_ckt[b].take(), idx=tt)

        load_tile(0)
        if NTT > 1:
            load_tile(1)

        rope_state = {}

        def rope_tail(tt, ri):
            b = tt % 2
            bk = ri % 2
            tok_mm = rope_state.pop((tt, ri))
            t_copy = P.ve(lambda e, o=kr_bf[bk][:], i2=ps_kr[bk][:]: e.tensor_copy(o, i2),
                          waits=[tok_mm] + g_krbf[bk].take())
            g_pskr[bk].use(t_copy)
            t_rot = P.pe(lambda e, r=kr_bf[bk][:]: e.matmul(ps_rot[:], r32_bf[:], r,
                                                            start=True, stop=True),
                         waits=[t_copy, tok_r32] + g_psrot.take(), inc=True)
            t_1 = P.ve(lambda e, c1=ckt[b][:, 0:512], r=kr_bf[bk][:]: e.tensor_tensor(
                           out=kt1[:], in0=r, in1=c1, op=ALU.mult),
                       waits=[tok_ck[tt]])
            t_2 = P.ve(lambda e, c2=ckt[b][:, 512:1024]: e.tensor_tensor(
                           out=kt2[:], in0=ps_rot[:], in1=c2, op=ALU.mult),
                       waits=[t_rot])
            g_psrot.use(t_2)
            g_krbf[bk].use(t_2)
            t_A = P.ve(lambda e: e.tensor_tensor(out=kA[:], in0=kt1[:], in1=kt2[:],
                                                 op=ALU.add),
                       waits=g_kA.take())
            g_ckt[b].use(t_1)
            g_ckt[b].use(t_2)
            g_kA.use(stKA.load(
                [(K_scr[2 * ri + hh, 0:64, slice(tt * 512, (tt + 1) * 512)],
                  kA[hh * 64:(hh + 1) * 64, :], {}) for hh in range(2)], waits=[t_A]))

        def emit_proj(tt):
            b = tt % 2
            xa = x_bf[b].ap().rearrange("p (k t) -> p k t", k=KD)
            t_ev = None
            for mi in range(4):
                pp = ps_ckv[mi % 2]
                tok_mm = None
                for ki in range(KD):
                    w = ([tok_x[tt], tok_wkv] + g_psckv[mi % 2].take()) if ki == 0 else []
                    tok_mm = P.pe(
                        lambda e, o=pp[:], l=wkv_sb[:, ki * CW + mi * 128: ki * CW + mi * 128 + 128],
                               r=xa[:, ki, :], s=(ki == 0), t_=(ki == KD - 1):
                            e.matmul(o, l, r, start=s, stop=t_),
                        waits=w, inc=(ki == KD - 1))
                t_ev = P.ve(lambda e, o=ckv_bf[b][:, mi * 512:(mi + 1) * 512], i2=pp[:]:
                               e.tensor_copy(o, i2),
                            waits=[tok_mm] + (g_ckvbf[b].take() if mi == 0 else []))
                g_psckv[mi % 2].use(t_ev)
            tok_ckv_done[tt] = t_ev
            for ri in range(RD // 128):
                bk = ri % 2
                tok_mm = None
                for ki in range(KD):
                    w = ([tok_x[tt]] + g_pskr[bk].take()) if ki == 0 else []
                    tok_mm = P.pe(
                        lambda e, o=ps_kr[bk][:],
                               l=wkv_sb[:, ki * CW + CD + ri * 128: ki * CW + CD + ri * 128 + 128],
                               r=xa[:, ki, :], s=(ki == 0), t_=(ki == KD - 1):
                            e.matmul(o, l, r, start=s, stop=t_),
                        waits=w, inc=(ki == KD - 1))
                rope_state[(tt, ri)] = tok_mm
                if ri > 0:
                    rope_tail(tt, ri - 1)
            g_xbf[b].use(("p", P.cnt["p"]))
            rope_tail(tt, RD // 128 - 1)

        def emit_upkv(tt):
            b = tt % 2
            cb = ckv_bf[b]
            tcol = slice(tt * 512, (tt + 1) * 512)

            def kc_group(ri):
                tok_mm = None
                for ki in range(4):
                    w = ([tok_ckv_done[tt], tok_upkv] + g_pskc.take()) if ki == 0 else []
                    tok_mm = P.pe(
                        lambda e, o=ps_kc[:],
                               l=upkv_sb[:, ki * 1536 + ri * 128: ki * 1536 + ri * 128 + 128],
                               r=cb[:, ki * 512:(ki + 1) * 512], s=(ki == 0), t_=(ki == 3):
                            e.matmul(o, l, r, start=s, stop=t_),
                        waits=w, inc=(ki == 3))
                t_kc = P.ve(lambda e: e.tensor_copy(kc_bf[:], ps_kc[:]),
                            waits=[tok_mm] + g_kcbf.take())
                g_pskc.use(t_kc)
                g_kcbf.use(stKC.load(
                    [(K_scr[2 * ri + hh, 64:128, tcol], kc_bf[hh * 64:(hh + 1) * 64, :], {})
                     for hh in range(2)], waits=[t_kc]))

            def v_group(tc, vh):
                gl = tt * 4 + tc
                pv = ps_v[vh]
                tok_mm = None
                for ki in range(4):
                    w = ([tok_ckv_done[tt]] + g_psv[vh].take()) if ki == 0 else []
                    tok_mm = P.pe(
                        lambda e, o=pv[:],
                               l=cb[:, ki * 512 + tc * 128: ki * 512 + tc * 128 + 128],
                               r=upkv_sb[:, ki * 1536 + 512 + vh * 512: ki * 1536 + 512 + (vh + 1) * 512],
                               s=(ki == 0), t_=(ki == 3):
                            e.matmul(o, l, r, start=s, stop=t_),
                        waits=w, inc=(ki == 3))
                t_v = P.ve(lambda e, i2=pv[:]: e.tensor_copy(vt_bf[:], i2),
                           waits=[tok_mm] + g_vtbf.take())
                g_psv[vh].use(t_v)
                g_vtbf.use(stVst.load(
                    [(V_scr[gl, :, 4 * vh:4 * vh + 4, :],
                      vt_bf.ap().rearrange("p (h d) -> p h d", h=4), {})],
                    waits=[t_v]))

            for tc in range(4):
                kc_group(tc)
                v_group(tc, 0)
                v_group(tc, 1)
            g_ckvbf[b].use(("p", P.cnt["p"]))

        emit_proj(0)
        for tt in range(NTT):
            if tt + 1 < NTT:
                emit_proj(tt + 1)
            if tt + 2 < NTT:
                load_tile(tt + 2)
            emit_upkv(tt)

        tok_A1_stores = [(nm, P.cnt[nm]) for nm in ("sKA0", "sKC0", "sVs0")]
        if dbg_k is not None:
            P.stream("gpsimd", 1, "dbgK").load([(dbg_k, K_scr, {})], waits=tok_A1_stores)
        if dbg_v is not None:
            P.stream("gpsimd", 1, "dbgV").load([(dbg_v, V_scr, {})], waits=tok_A1_stores)
        fence = P.fence()
        stA.close()

    st_q = ExitStack()
    q_all = SB(st_q, "q_all", [128, NH * TOWN], BF16) \
        if ("A2" in cfg.phases or "B" in cfg.phases) else None
    qmax_bc = SB(st_q, "qmax_bc", [128, NH * NJ], F32) \
        if ("A2" in cfg.phases or "B" in cfg.phases) else None
    tok_qmax = None

    if "A2" in cfg.phases:
        stA2 = ExitStack()
        x_own_bf = SB(stA2, "x_own_bf", [128, KD * TOWN], BF16)
        wq_bf = [SB(stA2, f"wq_bf{b}", [128, KD * 128], BF16) for b in range(2)]
        qraw = [SB(stA2, f"qraw{b}", [128, TOWN], BF16) for b in range(2)]
        qt1 = SB(stA2, "qt1", [128, TOWN], F32)
        qt2 = SB(stA2, "qt2", [128, TOWN], F32)
        qsq2 = [SB(stA2, f"qsq2{b}", [128, TOWN], BF16) for b in range(2)]
        qsqmax = SB(stA2, "qsqmax", [1, NH * NJ], F32)
        ps_q = [PS(stA2, f"ps_q{b}", [128, TOWN]) for b in range(2)]
        ps_qrot = PS(stA2, "ps_qrot", [128, TOWN])
        ps_qn = PS(stA2, "ps_qn", [32, TOWN])

        g_wq = [Guard(), Guard()]
        g_psq = [Guard(), Guard()]
        g_psqrot = Guard()
        g_qraw = [Guard(), Guard()]
        g_qsq2 = [Guard(), Guard()]
        g_psqn = Guard()

        stXO = P.stream("gpsimd", 1, "dXO")
        stWQ = P.stream("gpsimd", 2, "dWQ")
        tok_xo = stXO.load([(x_own_bf[:, ki * TOWN:(ki + 1) * TOWN], x_ownT[ki], {})
                            for ki in range(KD)], waits=fence)

        tok_wq = [None] * NH

        def load_wq(hi):
            tok_wq[hi] = load_slab(stWQ, hi, wq_bf[hi % 2], wqT[hi], KD, 128,
                                   extra_waits=g_wq[hi % 2].take())

        load_wq(0)
        if NH > 1:
            load_wq(1)

        q_state = {}

        def q_mms(hi):
            bw = hi % 2
            pq = ps_q[hi % 2]
            tok_mm = None
            for ki in range(KD):
                for (h0, h1) in HALVES:
                    w = ([tok_xo, tok_wq[hi]] + g_psq[hi % 2].take()) \
                        if (ki == 0 and h0 == 0) else []
                    tok_mm = P.pe(
                        lambda e, o=pq[:, h0:h1],
                               l=wq_bf[bw][:, ki * 128:(ki + 1) * 128],
                               r=x_own_bf[:, ki * TOWN + h0: ki * TOWN + h1],
                               s=(ki == 0), t_=(ki == KD - 1):
                            e.matmul(o, l, r, start=s, stop=t_),
                        waits=w, inc=(ki == KD - 1 and h1 == TOWN))
            q_state[hi] = tok_mm
            g_wq[hi % 2].use(("p", P.cnt["p"]))
            if hi + 2 < NH:
                load_wq(hi + 2)

        def q_rope(hi):
            bq = hi % 2
            pq = ps_q[bq]
            tok_mm = q_state.pop(hi)
            t_copy = P.ve(lambda e, o=qraw[bq][:], i2=pq[:, 0:TOWN]: e.tensor_copy(o, i2),
                          waits=[tok_mm] + g_qraw[bq].take())
            g_psq[bq].use(t_copy)
            t_rot = None
            for (h0, h1) in HALVES:
                t_rot = P.pe(lambda e, o=ps_qrot[:, h0:h1], r=qraw[bq][:, h0:h1]:
                                e.matmul(o, r32_bf[:], r, start=True, stop=True),
                             waits=([t_copy, tok_r32] + g_psqrot.take()) if h0 == 0 else (),
                             inc=(h1 == TOWN))
            t_1 = P.ve(lambda e, r=qraw[bq][:]: e.tensor_tensor(out=qt1[:], in0=r,
                                                                in1=cosq_sb[:], op=ALU.mult),
                       waits=[tok_cq])
            t_2 = P.ve(lambda e: e.tensor_tensor(out=qt2[:], in0=ps_qrot[:, 0:TOWN],
                                                 in1=sinmq_sb[:], op=ALU.mult),
                       waits=[t_rot, tok_sq])
            g_psqrot.use(t_2)
            g_qraw[bq].use(t_2)
            t_q = P.ve(lambda e, o=q_all[:, hi * TOWN:(hi + 1) * TOWN]:
                          e.tensor_tensor(out=o, in0=qt1[:], in1=qt2[:], op=ALU.add))
            tok_q_head[hi] = t_q

        tok_qmx = None
        tok_qsq = [None] * NH

        def qnorm_sq(hi):
            tok_qsq[hi] = P.sc(
                lambda e, o=qsq2[hi % 2][:], i=q_all[:, hi * TOWN:(hi + 1) * TOWN]:
                    e.activation(out=o, in_=i, func=AF.Square),
                waits=[tok_q_head[hi]] + g_qsq2[hi % 2].take())

        def qnorm_red(hi):
            nonlocal tok_qmx
            bs = hi % 2
            t_mm = None
            for (h0, h1) in HALVES:
                w = [tok_qsq[hi], tok_ones] + (g_psqn.take() if h0 == 0 else [])
                t_mm = P.pe(lambda e, o=ps_qn[0:1, h0:h1], r=qsq2[bs][:, h0:h1]:
                               e.matmul(o, ones_bf[:, 0:1], r, start=True, stop=True),
                            waits=w, inc=(h1 == TOWN))
            g_qsq2[bs].use(t_mm)
            for j in range(NJ):
                tok_qmx = P.ve(
                    lambda e, o=qsqmax[0:1, hi * NJ + j: hi * NJ + j + 1],
                           i=ps_qn[0:1, j * 128:(j + 1) * 128]:
                        e.reduce_max(o, i, axis=AX.X),
                    waits=[t_mm] if j == 0 else ())
            g_psqn.use(tok_qmx)

        q_mms(0)
        for hi in range(NH):
            if hi + 1 < NH:
                q_mms(hi + 1)
            q_rope(hi)
            qnorm_sq(hi)
            if hi > 0:
                qnorm_red(hi - 1)
        qnorm_red(NH - 1)

        t_qsr = P.sc(lambda e: e.activation(out=qsqmax[:], in_=qsqmax[:], func=AF.Sqrt),
                     waits=[tok_qmx])
        t_qbc = P.pe(lambda e: e.matmul(ps_qrot[:, 0:NH * NJ], onesr[0:1, :], qsqmax[:],
                                        start=True, stop=True),
                     waits=[t_qsr, tok_onesr] + g_psqrot.take(), inc=True)
        tok_qmax = P.ve(lambda e: e.tensor_copy(qmax_bc[:], ps_qrot[:, 0:NH * NJ]),
                        waits=[t_qbc])

        if dbg_q is not None:
            P.stream("gpsimd", 1, "dbgQ").load(
                [(dbg_q[hi], q_all[:, hi * TOWN:(hi + 1) * TOWN], {}) for hi in range(NH)],
                waits=[tok_q_head[NH - 1]])
        fence = P.fence()
        stA2.close()

    tok_O_stores = None
    if "B" in cfg.phases:
        stB = ExitStack()
        VW = 132
        NTC = S // 128
        k_sb = [SB(stB, f"k_sb{b}", [128, S], BF16) for b in range(2)]
        v_sb = [SB(stB, f"v_sb{b}", [128, NTC * VW], BF16) for b in range(2)]
        prT = [SB(stB, f"prT{b}", [128, 1024], BF16) for b in range(2)]
        ksq = SB(stB, "ksq", [128, S], BF16)
        kslot = SB(stB, "kslot", [1, 16], F32)
        kn1 = SB(stB, "kn1", [1, 1], F32)
        kn_bc = [SB(stB, f"kn_bc{b}", [128, 1], F32) for b in range(2)]
        knrow = SB(stB, "knrow", [1, 128], F32)
        negb = [SB(stB, f"negb{b}", [128, 1], F32) for b in range(2)]
        rinv = SB(stB, "rinv", [128, 1], F32)
        onb = SB(stB, "onb", [128, 128], BF16)
        otr = SB(stB, "otr", [128, 128], BF16)

        ps_s = [PS(stB, f"ps_s{b}", [128, 1024]) for b in range(2)]
        ps_o = [PS(stB, f"ps_o{b}", [128, 132]) for b in range(2)]
        ps_tr = PS(stB, "ps_tr", [128, 128], BF16)
        ps_kn = PS(stB, "ps_kn", [128, 512])

        g_kv = [Guard(), Guard()]
        g_pss = [Guard(), Guard()]
        g_pso = [Guard(), Guard()]
        g_prT = [Guard(), Guard()]
        g_negb = [Guard(), Guard()]
        g_rinv = Guard()
        g_otr = Guard()
        g_onb = Guard()
        g_pstr = Guard()
        g_ksq = Guard()
        g_knbc = [Guard(), Guard()]
        g_kn = Guard()
        g_pskn = Guard()

        stKV = P.stream("sync", 2, "dKV")
        stOst = P.stream("sync", 1, "sO")
        tok_kld = [None] * KVH
        tok_kn = [None] * KVH

        tok_vones = None
        for b in range(2):
            tok_vones = P.ve(
                lambda e, o=v_sb[b].ap().rearrange("p (tc w) -> p tc w", w=VW)[:, :, 128:129]:
                    e.memset(o, 1.0),
                waits=(list(tok_A1_stores) if b == 0 else []))

        def load_kv(g, first=False):
            w = g_kv[g % 2].take() + list(tok_A1_stores)
            if first:
                w = w + fence + ([tok_qmax] if tok_qmax else [])
            tok_kld[g] = stKV.load(
                [(k_sb[g % 2][:], K_scr[g], {}),
                 (v_sb[g % 2].ap().rearrange("p (tc w) -> p tc w", w=VW)[:, :, 0:128],
                  V_scr[:, :, g, :].rearrange("tc p d -> p tc d"), {})],
                waits=w, idx=g)

        def emit_knorm(g):
            bg = g % 2
            t_sq = P.sc(lambda e, i2=k_sb[bg][:]: e.activation(out=ksq[:], in_=i2,
                                                               func=AF.Square),
                        waits=[tok_kld[g]] + g_ksq.take())
            nblk = S // 512
            t_mx = None
            for b in range(nblk):
                w = [t_sq] + ([t_mx] if t_mx else [])
                if b == 0:
                    w += g_pskn.take()
                t_mm = P.pe(lambda e, o=ps_kn[0:32, 0:512],
                                   r=ksq[:, b * 512:(b + 1) * 512]:
                               e.matmul(o, ones_bf[:, 0:32], r, start=True, stop=True),
                            waits=w, inc=True)
                t_mx = P.ve(lambda e, a=kslot[0:1, b:b + 1], i2=ps_kn[0:1, 0:512]:
                               e.tensor_scalar(out=ksq[0:1, 0:512], in0=i2, scalar1=1.0,
                                               scalar2=None, op0=ALU.mult, op1=ALU.max,
                                               accum_out=a),
                            waits=[t_mm])
            t_red = P.ve(lambda e, n=nblk: e.reduce_max(kn1[:], kslot[0:1, 0:n], axis=AX.X),
                         waits=[t_mx])
            t_srt = P.sc(lambda e: e.activation(out=kn1[:], in_=kn1[:], func=AF.Sqrt,
                                                scale=1.0),
                         waits=[t_red])
            t_rf = P.ve(lambda e: e.tensor_scalar(out=knrow[:], in0=onesr[0:1, :],
                                                  scalar1=kn1[0:1, 0:1], scalar2=None,
                                                  op0=ALU.mult),
                        waits=[t_srt, tok_onesr] + g_kn.take())
            t_bc = P.pe(lambda e: e.transpose(ps_kn[0:128, 0:1], knrow[:],
                                              const_sb[0:1, 384:385]),
                        waits=[t_rf], inc=True)
            g_kn.use(t_bc)
            t_cp = P.ve(lambda e, o=kn_bc[bg][:]: e.tensor_scalar(
                            out=o, in0=ps_kn[:, 0:1], scalar1=-1.02, scalar2=None,
                            op0=ALU.mult),
                        waits=[t_bc] + g_knbc[bg].take())
            g_pskn.use(t_cp)
            g_ksq.use(t_bc)
            tok_kn[g] = t_cp

        load_kv(0, first=True)
        if KVH > 1:
            load_kv(1)
        emit_knorm(0)

        iters = [(g, g * GR + hh, j) for g in range(KVH) for hh in range(GR)
                 for j in range(NJ)]
        state = {}
        grp_ctr = [0]

        def emit_negb(i):
            g, h, j = iters[i]
            bi = i % 2
            t_nb = P.ve(lambda e, o=negb[bi][:], q=qmax_bc[:, h * NJ + j: h * NJ + j + 1],
                               kn=kn_bc[g % 2][:]:
                           e.tensor_tensor(out=o, in0=q, in1=kn, op=ALU.mult),
                        waits=[tok_qmax, tok_kn[g]] + g_negb[bi].take())
            g_knbc[g % 2].use(t_nb)
            state[("b", i)] = t_nb

        pendq = []

        def drain(keep):
            while len(pendq) > keep:
                av, after = pendq.pop(0)
                av()
                if after is not None:
                    after()

        def emit_iter(i, hooks=(), after=None):
            g, h, j = iters[i]
            bg = g % 2
            bi = i % 2
            t_nb = state.pop(("b", i))
            qtile = q_all[:, h * TOWN + j * 128: h * TOWN + j * 128 + 128]
            ngrp = j + 1
            nkc = 8 * ngrp
            po = ps_o[bi]
            vv = v_sb[bg].ap().rearrange("p (tc w) -> p tc w", w=VW)
            tok_exp = [None] * ngrp
            avtok = [None]

            def mm_grp(gr):
                bank = grp_ctr[0] % 2
                grp_ctr[0] += 1
                last = (gr == ngrp - 1)
                tok_mm = None
                for i8 in range(8):
                    kc = gr * 8 + i8
                    w = g_pss[bank].take() if i8 == 0 else []
                    if gr == 0 and i8 == 0:
                        w += [tok_kld[g]]
                        if tok_q_head[h]:
                            w.append(tok_q_head[h])
                    tok_mm = P.pe(
                        lambda e, o=ps_s[bank][:, i8 * 128:(i8 + 1) * 128],
                               l=k_sb[bg][:, kc * 128:(kc + 1) * 128], r=qtile:
                            e.matmul(o, l, r, start=True, stop=True),
                        waits=w, inc=(i8 == 7))
                tok_rdy = tok_mm
                if last:
                    moff = (j % 2) * 1024
                    tok_rdy = P.ve(
                        lambda e, o=ps_s[bank][:], m2=maskT_sb[:, moff:moff + 1024]:
                            e.tensor_tensor(out=o, in0=o, in1=m2, op=ALU.add),
                        waits=[tok_mm, tok_mask])
                return bank, tok_rdy

            def exp_grp(gr, bank, tok_rdy):
                t_exp = P.sc(
                    lambda e, o=prT[bank][:], i2=ps_s[bank][:], bb=negb[bi].ap():
                        e.activation(out=o, in_=i2, func=AF.Exp, bias=bb, scale=1.0),
                    waits=[tok_rdy, t_nb] + g_prT[bank].take())
                g_pss[bank].use(t_exp)
                tok_exp[gr] = t_exp

            def mk_av(gr, bank):
                def av():
                    for i8 in range(8):
                        kc = gr * 8 + i8
                        w = [tok_exp[gr]] if i8 == 0 else []
                        if kc == 0:
                            w = list(w) + [tok_vones] + g_pso[bi].take()
                        avtok[0] = P.pe(
                            lambda e, o=po[:, 0:129],
                                   l=prT[bank][:, i8 * 128:(i8 + 1) * 128],
                                   r=vv[:, kc, 0:129],
                                   s=(kc == 0), t_=(kc == nkc - 1):
                                e.matmul(o, l, r, start=s, stop=t_),
                            waits=w, inc=(i8 == 7))
                    g_prT[bank].use(avtok[0])
                return av

            def epilogue():
                t_ri = P.ve(lambda e: e.reciprocal(rinv[:], po[:, 128:129]),
                            waits=[avtok[0]] + g_rinv.take())
                t_on = P.ve(lambda e, i2=po[:, 0:128]: e.tensor_scalar(
                                out=onb[:], in0=i2, scalar1=rinv.ap(), scalar2=None,
                                op0=ALU.mult),
                            waits=[t_ri] + g_onb.take())
                g_rinv.use(t_on)
                g_pso[bi].use(t_on)
                t_otr = P.pe(lambda e: e.transpose(ps_tr[:, 0:128], onb[:], ident_bf[:]),
                             waits=[t_on, tok_ident] + g_pstr.take(), inc=True)
                g_onb.use(t_otr)
                t_oc = P.ve(lambda e: e.tensor_copy(otr[:], ps_tr[:, 0:128]),
                            waits=[t_otr] + g_otr.take())
                g_pstr.use(t_oc)
                g_otr.use(stOst.load([(O_scr[h, j], otr[:], {})], waits=[t_oc]))
                if after is not None:
                    after()

            for gr in range(ngrp):
                bank, tok_rdy = mm_grp(gr)
                pendq.append([mk_av(gr, bank), None])
                drain(2)
                exp_grp(gr, bank, tok_rdy)
                if gr == 0:
                    for hk in hooks:
                        hk()
            g_negb[bi].use(tok_exp[ngrp - 1])
            pendq[-1][1] = epilogue

        emit_negb(0)
        for i in range(len(iters)):
            hooks = []
            g, h, j = iters[i]
            if i + 1 < len(iters):
                g2 = iters[i + 1][0]
                if g2 != g and tok_kn[g2] is None:
                    hooks.append(lambda g2=g2: emit_knorm(g2))
                hooks.append(lambda i=i: emit_negb(i + 1))
            after = None
            if i + 1 >= len(iters) or iters[i + 1][0] != g:
                def after(g=g):
                    g_kv[g % 2].use(("p", P.cnt["p"]))
                    if g + 2 < KVH:
                        load_kv(g + 2)
            emit_iter(i, hooks, after)
        drain(0)

        tok_O_stores = ("sO0", P.cnt["sO0"])
        if dbg_o is not None:
            P.stream("gpsimd", 1, "dbgO").load([(dbg_o, O_scr, {})], waits=[tok_O_stores])
        fence = P.fence()
        stB.close()
    st_q.close()

    st_rx = ExitStack()
    need_rx = ("C" in cfg.phases) or ("F1" in cfg.phases) or ("F2" in cfg.phases)
    rx = SB(st_rx, "rx", [128, KD * TOWN], BF16) if need_rx else None

    def mk_rvec(rstat):
        def rvec(i, h0=0, h1=None):
            h1 = TOWN if h1 is None else h1
            return rstat[0:1, i * TOWN + h0: i * TOWN + h1]
        return rvec

    tok_x1 = [None] * KD

    if "C" in cfg.phases:
        stC = ExitStack()
        o_sb = SB(stC, "o_sb", [128, NOC * TOWN], BF16)
        wo_bf = [SB(stC, f"wo_bf{b}", [128, NOC * 128], BF16) for b in range(2)]
        xf32 = [SB(stC, f"xf32{b}", [128, TOWN], F32) for b in range(2)]
        r1sq = [SB(stC, f"r1sq{b}", [128, TOWN], BF16) for b in range(2)]
        bc_rstd = SB(stC, "bc_rstd", [128, TOWN], F32)
        bc_mr = SB(stC, "bc_mr", [128, TOWN], F32)
        tmpb = SB(stC, "tmpb", [128, TOWN], F32) if TOWN <= 512 else xf32[0]

        rstat1 = SB(stC, "rstat1", [32, 4 * TOWN], F32)
        rvec = mk_rvec(rstat1)
        P.ve(lambda e: e.memset(rstat1[:], 0.0), waits=fence)
        ps_a = [PS(stC, f"ps_a{b}", [128, TOWN]) for b in range(2)]
        ps_sum = PS(stC, "ps_sum", [32, max(TOWN, 512)])
        ps_sq = PS(stC, "ps_sq", [32, max(TOWN, 512)])

        g_wo = [Guard(), Guard()]
        g_psa = [Guard(), Guard()]
        g_xf = [Guard(), Guard()]
        g_sq = [Guard(), Guard()]

        stOL = P.stream("sync", 1, "dOL")
        stWO = P.stream("gpsimd", 2, "dWO")
        stXF = P.stream("sync", 2, "dXF")
        w0 = list(fence)
        if tok_O_stores:
            w0.append(tok_O_stores)
        tok_oall = stOL.load(
            [(o_sb.ap().rearrange("p (c j q) -> p c j q", c=NOC, j=NJ)[:, hi],
              O_scr[hi].rearrange("j d q -> d j q"), {}) for hi in range(NOC)],
            waits=w0)

        tok_wo = [None] * KD

        def load_wo(oi, first=False):
            tok_wo[oi] = load_slab(stWO, oi, wo_bf[oi % 2], woT[oi], NOC, 128,
                                   extra_waits=g_wo[oi % 2].take() + (list(fence) if first else []))

        load_wo(0, first=True)
        if KD > 1:
            load_wo(1)

        tok_r1 = [None] * KD
        tok_stats_done = None
        for oi in range(KD):
            bw = oi % 2
            pa = ps_a[oi % 2]
            t_xf = stXF.load([(xf32[bw][:], x_ownT[oi], {})], waits=g_xf[bw].take(), idx=oi)
            tok_mm = None
            for ki in range(NOC):
                for (h0, h1) in HALVES:
                    w = ([tok_oall, tok_wo[oi]] + g_psa[oi % 2].take()) \
                        if (ki == 0 and h0 == 0) else []
                    tok_mm = P.pe(
                        lambda e, o=pa[:, h0:h1],
                               l=wo_bf[bw][:, ki * 128:(ki + 1) * 128],
                               r=o_sb[:, ki * TOWN + h0: ki * TOWN + h1],
                               s=(ki == 0), t_=(ki == NOC - 1):
                            e.matmul(o, l, r, start=s, stop=t_),
                        waits=w, inc=(ki == NOC - 1 and h1 == TOWN))
            t_r1 = P.ve(lambda e, o=rx[:, oi * TOWN:(oi + 1) * TOWN], i=pa[:, 0:TOWN],
                               x=xf32[bw][:]:
                           e.scalar_tensor_tensor(out=o, in0=i, scalar=1.0, in1=x,
                                                  op0=ALU.mult, op1=ALU.add),
                        waits=[tok_mm, t_xf])
            tok_r1[oi] = t_r1
            g_psa[oi % 2].use(t_r1)
            g_xf[bw].use(t_r1)
            t_sqa = P.sc(lambda e, o=r1sq[oi % 2][:], i=rx[:, oi * TOWN:(oi + 1) * TOWN]:
                            e.activation(out=o, in_=i, func=AF.Square),
                         waits=[t_r1] + g_sq[oi % 2].take())
            for (h0, h1) in HALVES:
                P.pe(lambda e, o=ps_sum[0:1, h0:h1],
                            r=rx[:, oi * TOWN + h0: oi * TOWN + h1],
                            s=(oi == 0), t_=(oi == KD - 1):
                        e.matmul(o, ones_bf[:, 0:1], r, start=s, stop=t_),
                     waits=[t_r1, tok_ones] if (oi == 0 and h0 == 0) else [t_r1])
                tok_stats_done = P.pe(
                    lambda e, o=ps_sq[0:1, h0:h1], r=r1sq[oi % 2][:, h0:h1],
                           s=(oi == 0), t_=(oi == KD - 1):
                        e.matmul(o, ones_bf[:, 0:1], r, start=s, stop=t_),
                    waits=[t_sqa], inc=(h1 == TOWN))
            g_sq[oi % 2].use(tok_stats_done)
            g_wo[oi % 2].use(("p", P.cnt["p"]))
            if oi + 2 < KD:
                load_wo(oi + 2)

        tok_dbgr1 = None
        if dbg_r1 is not None:
            tok_dbgr1 = P.stream("gpsimd", 1, "dbgR1").load(
                [(dbg_r1[oi], rx[:, oi * TOWN:(oi + 1) * TOWN], {}) for oi in range(KD)],
                waits=[tok_r1[KD - 1]])

        if dbg_ln1 is not None:
            pssum_dump = SB(stC, "pssum_dump", [1, 2 * TOWN], F32)
            t_pd = P.ve(lambda e: e.tensor_copy(pssum_dump[0:1, 0:TOWN], ps_sum[0:1, 0:TOWN]),
                        waits=[tok_stats_done])
            t_pd2 = P.ve(lambda e: e.tensor_copy(pssum_dump[0:1, TOWN:2 * TOWN], ps_sq[0:1, 0:TOWN]))
            P.stream("sync", 1, "dbgPS").load([(dbg_ps, pssum_dump[:], {})], waits=[t_pd2])
        t_mean = P.ve(lambda e: e.tensor_scalar(out=rvec(0), in0=ps_sum[0:1, 0:TOWN],
                                                scalar1=1.0 / D, scalar2=None, op0=ALU.mult),
                      waits=[tok_stats_done])
        t_msq = P.ve(lambda e: e.tensor_scalar(out=rvec(1), in0=ps_sq[0:1, 0:TOWN],
                                               scalar1=1.0 / D, scalar2=None, op0=ALU.mult))
        t_m2 = P.ve(lambda e: e.tensor_tensor(out=rvec(2), in0=rvec(0), in1=rvec(0), op=ALU.mult))
        t_var = P.ve(lambda e: e.tensor_tensor(out=rvec(3), in0=rvec(1), in1=rvec(2),
                                               op=ALU.subtract))
        t_ln = P.sc(lambda e: e.activation(out=rvec(1), in_=rvec(3), func=AF.Ln, bias=const_sb[0:1, 416:417]),
                    waits=[t_var])
        t_rstd = P.sc(lambda e: e.activation(out=rvec(2), in_=rvec(1), func=AF.Exp, scale=-0.5))
        t_mr = P.ve(lambda e: e.tensor_tensor(out=rvec(1), in0=rvec(0), in1=rvec(2), op=ALU.mult),
                    waits=[t_rstd])
        t_bc = None
        for (h0, h1) in HALVES:
            P.pe(lambda e, o=ps_a[0][:, h0:h1], r=rvec(2, h0, h1):
                    e.matmul(o, onesr[0:1, :], r, start=True, stop=True),
                 waits=[t_mr, tok_onesr] if h0 == 0 else ())
            t_bc = P.pe(lambda e, o=ps_a[1][:, h0:h1], r=rvec(1, h0, h1):
                           e.matmul(o, onesr[0:1, :], r, start=True, stop=True),
                        waits=(), inc=(h1 == TOWN))
        t_bc1 = P.ve(lambda e: e.tensor_copy(bc_rstd[:], ps_a[0][:, 0:TOWN]), waits=[t_bc])
        t_bc2 = P.ve(lambda e: e.tensor_copy(bc_mr[:], ps_a[1][:, 0:TOWN]))
        if dbg_ln1 is not None:
            P.stream("sync", 1, "dbgL").load(
                [(dbg_ln1, rstat1[0:1, :], {}), (dbg_bc[0], bc_rstd[:], {}),
                 (dbg_bc[1], bc_mr[:], {})], waits=[t_bc2])
        for oi in range(KD):
            t_t = P.ve(lambda e, i=rx[:, oi * TOWN:(oi + 1) * TOWN]:
                          e.tensor_tensor(out=tmpb[:], in0=i, in1=bc_rstd[:], op=ALU.mult),
                       waits=[tok_dbgr1] if (oi == 0 and tok_dbgr1) else ())
            tok_x1[oi] = P.ve(lambda e, o=rx[:, oi * TOWN:(oi + 1) * TOWN]:
                                 e.tensor_tensor(out=o, in0=tmpb[:], in1=bc_mr[:],
                                                 op=ALU.subtract))
        if dbg_x1 is not None:
            tok_dbgx1 = P.stream("gpsimd", 1, "dbgX1").load(
                [(dbg_x1[oi], rx[:, oi * TOWN:(oi + 1) * TOWN], {}) for oi in range(KD)],
                waits=[tok_x1[KD - 1]])
        fence = P.fence()
        stC.close()

    st_h = ExitStack()
    HRES = cfg.HRES
    h_sb = SB(st_h, "h_sb", [128, max(HRES, 1) * TOWN], BF16) \
        if ("F1" in cfg.phases or "F2" in cfg.phases) else None
    tok_h = [None] * NF

    if "F1" in cfg.phases:
        stF1 = ExitStack()
        wg_bf = [SB(stF1, f"wg_bf{b}", [128, KD * 128], BF16) for b in range(2)]
        wu_bf = [SB(stF1, f"wu_bf{b}", [128, KD * 128], BF16) for b in range(2)]
        gsil = SB(stF1, "gsil", [128, TOWN], BF16)
        hsp = [SB(stF1, f"hsp{b}", [128, TOWN], BF16) for b in range(2)]
        ps_g = [PS(stF1, f"ps_g{b}", [128, TOWN]) for b in range(2)]
        ps_u = [PS(stF1, f"ps_u{b}", [128, TOWN]) for b in range(2)]

        g_wg = [Guard(), Guard()]
        g_wu = [Guard(), Guard()]
        g_psg = [Guard(), Guard()]
        g_psu = [Guard(), Guard()]
        g_hsp = [Guard(), Guard()]
        g_gsil = Guard()

        stWGU = P.stream("gpsimd", 2, "dWGU")
        stHsp = P.stream("sync", 2, "sH")
        tok_wgu = [None] * NF

        def load_wgu(f, first=False):
            ext = g_wg[f % 2].take() + g_wu[f % 2].take() + (list(fence) if first else [])
            load_slab(stWGU, f, wg_bf[f % 2], wgT[f], KD, 128, extra_waits=ext)
            tok_wgu[f] = load_slab(stWGU, f, wu_bf[f % 2], wuT[f], KD, 128)

        load_wgu(0, first=True)
        if NF > 1:
            load_wgu(1)

        for f in range(NF):
            bw = f % 2
            pg, pu = ps_g[f % 2], ps_u[f % 2]
            tok_gmm = None
            tok_umm = None
            for ki in range(KD):
                for (h0, h1) in HALVES:
                    w = []
                    if ki == 0 and h0 == 0:
                        w = [tok_wgu[f]] + g_psg[f % 2].take()
                        if tok_x1[KD - 1]:
                            w.append(tok_x1[KD - 1])
                    tok_gmm = P.pe(
                        lambda e, o=pg[:, h0:h1],
                               l=wg_bf[bw][:, ki * 128:(ki + 1) * 128],
                               r=rx[:, ki * TOWN + h0: ki * TOWN + h1],
                               s=(ki == 0), t_=(ki == KD - 1):
                            e.matmul(o, l, r, start=s, stop=t_),
                        waits=w, inc=(ki == KD - 1 and h1 == TOWN))
            for ki in range(KD):
                for (h0, h1) in HALVES:
                    w = g_psu[f % 2].take() if (ki == 0 and h0 == 0) else []
                    tok_umm = P.pe(
                        lambda e, o=pu[:, h0:h1],
                               l=wu_bf[bw][:, ki * 128:(ki + 1) * 128],
                               r=rx[:, ki * TOWN + h0: ki * TOWN + h1],
                               s=(ki == 0), t_=(ki == KD - 1):
                            e.matmul(o, l, r, start=s, stop=t_),
                        waits=w, inc=(ki == KD - 1 and h1 == TOWN))
            t_sil = P.sc(lambda e, i=pg[:, 0:TOWN]: e.activation(out=gsil[:], in_=i, func=AF.Silu),
                         waits=[tok_gmm] + g_gsil.take())
            g_psg[f % 2].use(t_sil)
            if f < HRES:
                t_h = P.ve(lambda e, o=h_sb[:, f * TOWN:(f + 1) * TOWN], i=pu[:, 0:TOWN]:
                              e.tensor_tensor(out=o, in0=gsil[:], in1=i, op=ALU.mult),
                           waits=[tok_umm, t_sil])
                tok_h[f] = t_h
            else:
                bs = f % 2
                t_h = P.ve(lambda e, o=hsp[bs][:], i=pu[:, 0:TOWN]:
                              e.tensor_tensor(out=o, in0=gsil[:], in1=i, op=ALU.mult),
                           waits=[tok_umm, t_sil] + g_hsp[bs].take())
                tok_h[f] = stHsp.load([(h_scr[f - HRES], hsp[bs][:], {})],
                                      waits=[t_h], idx=f)
                g_hsp[bs].use(tok_h[f])
            g_gsil.use(t_h)
            g_psu[f % 2].use(t_h)
            g_wg[f % 2].use(("p", P.cnt["p"]))
            g_wu[f % 2].use(("p", P.cnt["p"]))
            if f + 2 < NF:
                load_wgu(f + 2)

        if dbg_h is not None:
            stDH = P.stream("gpsimd", 1, "dbgH")
            stDH.load([(dbg_h[f], h_sb[:, f * TOWN:(f + 1) * TOWN], {})
                       for f in range(HRES)] +
                      [(dbg_h[f], h_scr[f - HRES], {}) for f in range(HRES, NF)],
                      waits=[t for t in tok_h if t is not None])
        fence = P.fence()
        stF1.close()

    if "F2" in cfg.phases:
        stF2 = ExitStack()
        wd_bf = [SB(stF2, f"wd_bf{b}", [128, 4 * 256], BF16) for b in range(2)]
        hst = [SB(stF2, f"hst{b}", [128, TOWN], BF16) for b in range(4)]
        r2sq = [SB(stF2, f"r2sq{b}", [128, TOWN], BF16) for b in range(2)]
        bc2_rstd = SB(stF2, "bc2_rstd", [128, TOWN], F32)
        bc2_mr = SB(stF2, "bc2_mr", [128, TOWN], F32)
        tmp2 = SB(stF2, "tmp2", [128, TOWN], F32)
        outf = [SB(stF2, f"outf{b}", [128, TOWN], F32) for b in range(2)]

        rstat2 = SB(stF2, "rstat2", [32, 4 * TOWN], F32)
        rvec2 = mk_rvec(rstat2)
        P.ve(lambda e: e.memset(rstat2[:], 0.0), waits=fence)
        ps_y = [PS(stF2, f"ps_y{b}", [128, TOWN]) for b in range(2)]
        ps2_sum = PS(stF2, "ps2_sum", [32, max(TOWN, 512)])
        ps2_sq = PS(stF2, "ps2_sq", [32, max(TOWN, 512)])

        stY = P.stream("sync", 2, "sY")
        g_wd = [Guard(), Guard()]
        g_hst = [Guard() for _ in range(4)]
        g_psy = [Guard(), Guard()]
        g_sq2 = [Guard(), Guard()]
        g_outf = [Guard(), Guard()]

        stWD = P.stream("gpsimd", 2, "dWD")
        stHL = P.stream("sync", 4, "dHL")
        tok_wd = {}
        tok_hload = {}

        def load_wd(dg, fb, first=False):
            f0, f1 = 4 * fb, min(NF, 4 * fb + 4)
            tok_wd[(dg, fb)] = stWD.load(
                [(wd_bf[fb % 2].ap().rearrange("p (f m) -> p f m", f=4)[:, 0:f1 - f0, :],
                  wdT[dg, f0:f1].rearrange("f p m -> p f m"), {})],
                waits=g_wd[fb % 2].take() + (list(fence) if first else []), idx=fb)

        hfirst = [True] * 4
        def load_h(dg, f):
            if f < HRES:
                return
            bs = f % 4
            w = g_hst[bs].take() + [tok_h[f]]
            if hfirst[bs]:
                hfirst[bs] = False
                w = w + list(fence)
            tok_hload[(dg, f)] = stHL.load([(hst[bs][:], h_scr[f - HRES], {})],
                                           waits=w, idx=f)

        PRE = 4
        NFB = (NF + 3) // 4
        tok_r2 = [None] * KD
        tok_stats2 = None
        for dg in range(cfg.NDG):
            d0 = 2 * dg
            nd = min(2, KD - d0)
            load_wd(dg, 0, first=(dg == 0))
            if NFB > 1:
                load_wd(dg, 1)
            for f in range(min(PRE, NF)):
                load_h(dg, f)
            tok_dmm = None
            for f in range(NF):
                bw = (f // 4) % 2
                if f < HRES:
                    hsrc = h_sb.ap()[:, f * TOWN:(f + 1) * TOWN]
                    tok_hs = tok_h[f]
                else:
                    hsrc = hst[f % 4][:]
                    tok_hs = tok_hload[(dg, f)]
                for d in range(nd):
                    for (h0, h1) in HALVES:
                        w = []
                        if d == 0 and h0 == 0:
                            w = [tok_wd[(dg, f // 4)], tok_hs]
                            if f == 0:
                                for dd in range(nd):
                                    w += g_psy[dd].take()
                        tok_dmm = P.pe(
                            lambda e, o=ps_y[d][:, h0:h1],
                                   l=wd_bf[bw][:, (f % 4) * 256 + d * 128:
                                               (f % 4) * 256 + (d + 1) * 128],
                                   r=hsrc[:, h0:h1],
                                   s=(f == 0), t_=(f == NF - 1):
                                e.matmul(o, l, r, start=s, stop=t_),
                            waits=w, inc=(d == nd - 1 and h1 == TOWN))
                if f >= HRES:
                    g_hst[f % 4].use(tok_dmm)
                if f % 4 == 3 or f == NF - 1:
                    g_wd[(f // 4) % 2].use(tok_dmm)
                    if f // 4 + 2 < NFB:
                        load_wd(dg, f // 4 + 2)
                if f + PRE < NF:
                    load_h(dg, f + PRE)
            for d in range(nd):
                oi = d0 + d
                t_r2 = P.ve(lambda e, o=rx[:, oi * TOWN:(oi + 1) * TOWN], i=ps_y[d][:, 0:TOWN]:
                               e.scalar_tensor_tensor(out=o, in0=i, scalar=1.0, in1=o,
                                                      op0=ALU.mult, op1=ALU.add),
                            waits=[tok_dmm, tok_x1[oi] if tok_x1[oi] else None])
                tok_r2[oi] = t_r2
                g_psy[d].use(t_r2)
                t_sq = P.sc(lambda e, o=r2sq[d][:], i=rx[:, oi * TOWN:(oi + 1) * TOWN]:
                               e.activation(out=o, in_=i, func=AF.Square),
                            waits=[t_r2] + g_sq2[d].take())
                for (h0, h1) in HALVES:
                    P.pe(lambda e, o=ps2_sum[0:1, h0:h1],
                                r=rx[:, oi * TOWN + h0: oi * TOWN + h1],
                                s=(oi == 0), t_=(oi == KD - 1):
                            e.matmul(o, ones_bf[:, 0:1], r, start=s, stop=t_),
                         waits=[t_r2, tok_ones] if (oi == 0 and h0 == 0) else [t_r2])
                    tok_stats2 = P.pe(
                        lambda e, o=ps2_sq[0:1, h0:h1], r=r2sq[d][:, h0:h1],
                               s=(oi == 0), t_=(oi == KD - 1):
                            e.matmul(o, ones_bf[:, 0:1], r, start=s, stop=t_),
                        waits=[t_sq], inc=(h1 == TOWN))
                g_sq2[d].use(tok_stats2)

        t_mean = P.ve(lambda e: e.tensor_scalar(out=rvec2(0), in0=ps2_sum[0:1, 0:TOWN],
                                                scalar1=1.0 / D, scalar2=None, op0=ALU.mult),
                      waits=[tok_stats2])
        t_msq = P.ve(lambda e: e.tensor_scalar(out=rvec2(1), in0=ps2_sq[0:1, 0:TOWN],
                                               scalar1=1.0 / D, scalar2=None, op0=ALU.mult))
        t_m2 = P.ve(lambda e: e.tensor_tensor(out=rvec2(2), in0=rvec2(0), in1=rvec2(0), op=ALU.mult))
        t_var = P.ve(lambda e: e.tensor_tensor(out=rvec2(3), in0=rvec2(1), in1=rvec2(2),
                                               op=ALU.subtract))
        t_ln = P.sc(lambda e: e.activation(out=rvec2(1), in_=rvec2(3), func=AF.Ln, bias=const_sb[0:1, 416:417]),
                    waits=[t_var])
        t_rstd = P.sc(lambda e: e.activation(out=rvec2(2), in_=rvec2(1), func=AF.Exp, scale=-0.5))
        t_mr = P.ve(lambda e: e.tensor_tensor(out=rvec2(1), in0=rvec2(0), in1=rvec2(2), op=ALU.mult),
                    waits=[t_rstd])
        t_bc = None
        for (h0, h1) in HALVES:
            P.pe(lambda e, o=ps_y[0][:, h0:h1], r=rvec2(2, h0, h1):
                    e.matmul(o, onesr[0:1, :], r, start=True, stop=True),
                 waits=([t_mr, tok_onesr] + g_psy[0].take() + g_psy[1].take()) if h0 == 0 else ())
            t_bc = P.pe(lambda e, o=ps_y[1][:, h0:h1], r=rvec2(1, h0, h1):
                           e.matmul(o, onesr[0:1, :], r, start=True, stop=True),
                        waits=(), inc=(h1 == TOWN))
        t_bc1 = P.ve(lambda e: e.tensor_copy(bc2_rstd[:], ps_y[0][:, 0:TOWN]), waits=[t_bc])
        t_bc2 = P.ve(lambda e: e.tensor_copy(bc2_mr[:], ps_y[1][:, 0:TOWN]))
        for oi in range(KD):
            bo = oi % 2
            t_t = P.ve(lambda e, i=rx[:, oi * TOWN:(oi + 1) * TOWN]:
                          e.tensor_tensor(out=tmp2[:], in0=i, in1=bc2_rstd[:], op=ALU.mult),
                       waits=[tok_r2[oi]] if tok_r2[oi] else ())
            t_o = P.ve(lambda e, o=outf[bo][:]:
                          e.tensor_tensor(out=o, in0=tmp2[:], in1=bc2_mr[:], op=ALU.subtract),
                       waits=g_outf[bo].take())
            g_outf[bo].use(stY.load([(y_own[oi], outf[bo][:], {})], waits=[t_o], idx=oi))
        stF2.close()

    st_h.close()
    st_rx.close()

    if dbg_hend is not None and "F1" in cfg.phases:
        P.stream("gpsimd", 1, "dbgHE").load(
            [(dbg_hend[f], h_sb[:, f * TOWN:(f + 1) * TOWN], {}) for f in range(cfg.HRES)] +
            [(dbg_hend[f], h_scr[f - cfg.HRES], {}) for f in range(cfg.HRES, NF)],
            waits=P.fence())

    P.waits("sync", [(k, P.cnt[k]) for k in P.dma_sems if P.cnt[k]])
    P.waits("gpsimd", [(k, P.cnt[k]) for k in P.dma_sems if P.cnt[k]])

    with ExitStack() as es:
        sems = {}
        for s in list(P.cnt):
            if P.cnt[s]:
                sems[s] = es.enter_context(nc.semaphore("sem_" + s))
        blk = es.enter_context(nc.Block())
        for eng_name in ENGINES:
            opl = P.ops[eng_name]
            if not opl:
                continue
            dec = getattr(blk, eng_name)
            def body(eng, opl=opl):
                for fn in opl:
                    fn(eng, sems)
            dec(body)
    st_top.close()
    return nc



def rope_tables(S):
    inv = 1.0 / 10000.0 ** (np.arange(0, HD, 2, dtype=np.float64) / HD)
    pos = np.arange(S, dtype=np.float64)
    fr = np.outer(pos, inv)
    emb = np.concatenate([fr, fr], axis=-1)
    return np.cos(emb), np.sin(emb)


def prep_maps(cfg, inputs):
    S, D, F, KD, NF = cfg.S, cfg.D, cfg.F, cfg.KD, cfg.NF
    TOWN, NJ = cfg.TOWN, cfg.NJ
    x = np.ascontiguousarray(np.asarray(inputs["L0_attn_hidden_states"], np.float32).reshape(S, D))
    Wq = np.asarray(inputs["L0_attn_Wq"], np.float32)
    Wkr = np.asarray(inputs["L0_attn_Wkr"], np.float32)
    Wdk = np.asarray(inputs["L0_attn_Wdk"], np.float32)
    Wupk = np.asarray(inputs["L0_attn_Wupk"], np.float32)
    Wupv = np.asarray(inputs["L0_attn_Wupv"], np.float32)
    Wo = np.asarray(inputs["L0_attn_Wo"], np.float32)
    Wg = np.asarray(inputs["L0_ffn_W_gate"], np.float32)
    Wu = np.asarray(inputs["L0_ffn_W_up"], np.float32)
    Wd = np.asarray(inputs["L0_ffn_W_down"], np.float32)

    xT_full = np.ascontiguousarray(x.T)
    xT = xT_full.reshape(KD, 128, S)

    qperm = np.concatenate([128 * h + PERM for h in range(NH)])
    WqT = np.ascontiguousarray(Wq[qperm].reshape(NH, 128, KD, 128).transpose(0, 2, 3, 1))
    WoT = np.ascontiguousarray(Wo.reshape(KD, 128, NOC, 128).transpose(0, 2, 3, 1))
    wkv = np.concatenate([Wdk, Wkr], axis=0)
    wkvT = np.ascontiguousarray(wkv.T).reshape(KD, 128, CD + RD)
    upkv = np.concatenate([Wupk, Wupv], axis=0)
    upkvT = np.ascontiguousarray(upkv.T).reshape(4, 128, 1536)
    WgT = np.ascontiguousarray(Wg.reshape(NF, 128, KD, 128).transpose(0, 2, 3, 1))
    WuT = np.ascontiguousarray(Wu.reshape(NF, 128, KD, 128).transpose(0, 2, 3, 1))
    WdT = np.ascontiguousarray(Wd.reshape(cfg.NDG, 256, NF, 128).transpose(0, 2, 3, 1))

    cos, sin = rope_tables(S)
    dscale = 1.0 / np.sqrt(HD)
    sgn = np.where(PERM < 64, -1.0, 1.0)
    cosq_p = (cos[:, PERM].T * dscale).astype(np.float32)
    sinmq_p = ((sin[:, PERM] * sgn[None, :]).T * dscale).astype(np.float32)
    rdim = np.array([r if r < 32 else r + 32 for r in range(64)])
    krow = np.concatenate([rdim, rdim])
    ksgn = np.where(krow < 64, -1.0, 1.0)
    cosk_p = np.ascontiguousarray(cos[:, krow].T.astype(np.float32))
    sinmk_p = np.ascontiguousarray((sin[:, krow] * ksgn[None, :]).T.astype(np.float32))

    consts = np.zeros((128, 417), np.float32)
    consts[:, 0:128] = np.eye(128)
    R32 = np.zeros((128, 128), np.float32)
    for i in range(128):
        R32[i ^ 32, i] = 1.0
    consts[:, 128:256] = R32
    consts[0, 256:384] = 1.0
    consts[:, 384] = 1.0
    consts[:, 416] = EPS

    shared = dict(xT=xT, wkvT=wkvT, upkvT=upkvT, wqT=WqT, woT=WoT,
                  wgT=WgT, wuT=WuT, wdT=WdT, consts=consts,
                  cosk=cosk_p, sinmk=sinmk_p)

    in_maps = []
    own_cols_all = []
    for c in range(NCORES):
        cols = np.concatenate([np.arange(128) + 128 * tile_of(c, j) for j in range(NJ)])
        own_cols_all.append(cols)
        x_ownT = np.ascontiguousarray(xT_full[:, cols]).reshape(KD, 128, TOWN)
        cq = np.ascontiguousarray(cosq_p[:, cols])
        sq = np.ascontiguousarray(sinmq_p[:, cols])
        mask = np.zeros((128, 2048), np.float32)
        kk = np.arange(128)[:, None, None]
        i8 = np.arange(8)[None, :, None]
        qq = np.arange(128)[None, None, :]
        for par, p in ((0, c), (1, 7 - c)):
            m = np.where(128 * i8 + kk <= 128 * p + qq, 0.0, NEG)
            mask[:, par * 1024:(par + 1) * 1024] = m.reshape(128, 1024)
        m = dict(shared)
        m.update(x_ownT=x_ownT, cosq=cq, sinmq=sq, maskin=mask)
        in_maps.append(m)
    return in_maps, own_cols_all


def assemble(cfg, results, own_cols_all):
    S, D = cfg.S, cfg.D
    y = np.empty((S, D), np.float32)
    for c in range(NCORES):
        yc = results[c]["y_own"].reshape(D, cfg.TOWN)
        y[own_cols_all[c], :] = yc.T
    return y.reshape(1, S, D)


def kernel(**inputs):
    cfg = Cfg()
    nc = build(cfg)
    in_maps, own_cols = prep_maps(cfg, inputs)
    res = run_bass_kernel_spmd(nc, in_maps, core_ids=list(range(NCORES)))
    return assemble(cfg, res.results, own_cols)
